# version 05
# baseline (speedup 1.0000x reference)
import numpy as np

import concourse.bass as bass
import concourse.mybir as mybir
from concourse.bass_utils import run_bass_kernel_spmd

F32 = mybir.dt.float32
BF16 = mybir.dt.bfloat16
U8 = mybir.dt.uint8
AF = mybir.ActivationFunctionType
ALU = mybir.AluOpType
AX = mybir.AxisListType

NCORES = 8
Bb, L, Dm = 2, 8192, 2048
H, HD, FD = 32, 64, 64
Ii = 8192
T = 2048
HF = 1024
CH = 128
NCH = T // CH
DT = Dm // 128
OT = Dm // 128
TCF = 512
NTC = T // TCF
EPS_ATTN = 1e-12
EPS_LN = 1e-5

STAGE = "full"
DEBUG = False
SERIAL = False


class Sem:
    def __init__(self, h):
        self.h = h
        self.n = 0


class Prog:
    ENGINES = ("sync", "tensor", "vector", "scalar", "gpsimd")

    def __init__(self):
        self.ops = {e: [] for e in self.ENGINES}

    def op(self, eng, builder, waits=(), incs=()):
        waits = [(s, v) for (s, v) in waits if v > 0]
        incs = [(s, n) for (s, n) in incs]
        for s, n in incs:
            s.n += n
        self.ops[eng].append((builder, waits, incs))

    def replay(self, name, eng):
        for builder, waits, incs in self.ops[name]:
            for s, v in waits:
                eng.wait_ge(s.h, v)
            ins = builder(eng)
            if incs:
                assert ins is not None
                for s, n in incs:
                    ins = ins.then_inc(s.h, n)


class Arena:

    def __init__(self, tile, base=0):
        self.tile = tile
        self.off = base

    def alloc(self, parts, elems, dtype, align=512):
        esz = {F32: 4, BF16: 2, U8: 1}[dtype]
        self.off = (self.off + align - 1) // align * align
        nbytes = elems * esz
        ap = self.tile[0:parts, self.off:self.off + nbytes]
        if dtype != U8:
            ap = ap.bitcast(dtype)
        self.off += nbytes
        return ap


ARENA_B = 204544


def build_nc(stage=STAGE, debug=DEBUG, serial=SERIAL):
    nc = bass.Bass("TRN2", target_bir_lowering=False)

    def din(name, shape, dt):
        return nc.dram_tensor(name, shape, dt, kind="ExternalInput").ap()

    xT_d = din("xT", [Dm, T], F32)
    xn_d = din("xn", [T, Dm], F32)
    wqT_d = din("wqT", [Dm, Dm], F32)
    wkT_d = din("wkT", [Dm, Dm], F32)
    wvT_d = din("wvT", [Dm, Dm], F32)
    woT_d = din("woT", [Dm, Dm], F32)
    wfq_d = din("wfq", [H * HD, FD], F32)
    wfk_d = din("wfk", [H * HD, FD], F32)
    wg8_d = din("wg8", [Dm, Ii], U8)
    wu8_d = din("wu8", [Dm, Ii], U8)
    wd8_d = din("wd8", [Ii, Dm], U8)
    scg_d = din("scg", [128, Ii // 128], F32)
    scu_d = din("scu", [128, Ii // 128], F32)
    scd_d = din("scd", [128, Dm // 128], F32)
    ident_d = din("identf", [128, 128], F32)
    maskA_d = din("maskA", [128, 128], F32)
    onesrow_d = din("onesrow", [1, 128], F32)
    m128row_d = din("m128row", [1, 128], F32)
    onescol_d = din("onescol", [128, 1], F32)
    ccmask_d = din("ccmask", [128, 8], F32)
    misc_d = din("misc", [128, 2], F32)
    eps65_d = din("eps65", [1, 65], F32)

    out_d = nc.dram_tensor("out", [T, Dm], F32, kind="ExternalOutput").ap()

    ekT_d = nc.dram_tensor("ekT_s", [128, OT * T], BF16).ap()
    fqT_d = nc.dram_tensor("fqT_s", [128, OT * T], BF16).ap()
    ek_d = nc.dram_tensor("ek_s", [T, H * FD], BF16).ap()
    vp_d = nc.dram_tensor("vp_s", [T, H * 65], BF16).ap()
    xaT_d = nc.dram_tensor("xaT_s", [128, DT * T], BF16).ap()
    yT_dbg = nc.dram_tensor("yT_dbg", [128, NCH * OT * CH], F32, kind="ExternalOutput").ap() if (debug and stage == "c") else None
    h_dbg = nc.dram_tensor("h_dbg", [128, 8 * TCF], F32, kind="ExternalOutput").ap() if (debug and stage == "full") else None
    r2_dbg = nc.dram_tensor("r2_dbg", [128, DT * TCF], F32, kind="ExternalOutput").ap() if (debug and stage == "full") else None
    xs_dbg = nc.dram_tensor("xs_dbg", [1, TCF], F32, kind="ExternalOutput").ap() if (debug and stage == "full") else None
    sg_dbg = nc.dram_tensor("sg_dbg", [128, TCF], F32, kind="ExternalOutput").ap() if (debug and stage == "full") else None
    wub_dbg = nc.dram_tensor("wub_dbg", [128, 16 * TCF], F32, kind="ExternalOutput").ap() if (debug and stage == "full") else None
    ps_dbg = nc.dram_tensor("ps_dbg", [128, TCF], F32, kind="ExternalOutput").ap() if (debug and stage == "full") else None
    psg_sb = None
    uu_dbg = nc.dram_tensor("uu_dbg", [128, TCF], F32, kind="ExternalOutput").ap() if (debug and stage == "full") else None
    xat_dbg = nc.dram_tensor("xat_dbg", [128, DT * TCF], F32, kind="ExternalOutput").ap() if (debug and stage == "full") else None
    r1_dbg = nc.dram_tensor("r1_dbg", [128, Dm], F32, kind="ExternalOutput").ap() if debug else None
    s_dbg = nc.dram_tensor("s_dbg", [64, NCH * H * 65], F32, kind="ExternalOutput").ap() if debug else None
    am8_dbg = nc.dram_tensor("am8_dbg", [128, 4 * CH], F32, kind="ExternalOutput").ap() if debug else None
    sbf_dbg = nc.dram_tensor("sbf_dbg", [128, H * 65], F32, kind="ExternalOutput").ap() if debug else None
    fq8_dbg = nc.dram_tensor("fq8_dbg", [128, OT * CH], F32, kind="ExternalOutput").ap() if debug else None
    vp8_dbg = nc.dram_tensor("vp8_dbg", [128, H * 65], F32, kind="ExternalOutput").ap() if debug else None
    st_dbg = nc.dram_tensor("st_dbg", [128, 48], F32, kind="ExternalOutput").ap() if debug else None
    am_dbg = nc.dram_tensor("am_dbg", [128, 4 * CH], F32, kind="ExternalOutput").ap() if debug else None
    dn_dbg = nc.dram_tensor("dn_dbg", [1, 4 * CH], F32, kind="ExternalOutput").ap() if debug else None
    dbg_f32 = {}
    if debug:
        for nm, src_ap in (("ekT_f", ekT_d), ("fqT_f", fqT_d), ("ek_f", ek_d),
                           ("vp_f", vp_d), ("xaT_f", xaT_d)):
            dbg_f32[nm] = (nc.dram_tensor(nm, list(src_ap.shape), F32,
                                          kind="ExternalOutput").ap(), src_ap)
    tc_dbg = nc.dram_tensor("tc_dbg", [64, H * 65], F32, kind="ExternalOutput").ap() if debug else None
    s0_dbg = nc.dram_tensor("s0_dbg", [64, H * 65], F32, kind="ExternalOutput").ap() if debug else None

    ag_in = nc.dram_tensor("ag_in", [64, H * 65], F32).ap()
    ag_out = nc.dram_tensor("ag_out", [NCORES * 64, H * 65], F32, addr_space="Shared").ap()

    prog = Prog()
    guards = []

    def enter(g):
        guards.append(g)
        return g.__enter__()

    arena_t = enter(nc.sbuf_tensor("arena", [128, ARENA_B], U8))
    pb = [enter(nc.psum_tensor(f"pb{k}", [128, 512], F32)) for k in range(8)]

    def S(name):
        return Sem(enter(nc.semaphore(name)))

    sPE, sDV, sAC = S("sPE"), S("sDV"), S("sAC")
    sXT = S("sXT")
    sWT = [S("sWT0"), S("sWT1")]
    sWV = [S("sWV0"), S("sWV1")]
    sWF = S("sWF")
    sXN = S("sXN")
    sSP = S("sSP")
    sCH = [S("sCH0"), S("sCH1")]
    sWO = S("sWO")
    sCC = S("sCC")
    sAG = S("sAG")
    sWG = [S("sWG0"), S("sWG1")]
    sWD = [S("sWD0"), S("sWD1")]
    sXA = S("sXA")
    sOU = S("sOU")

    def barrier():
        prog.op("tensor", lambda t: t.nop(), incs=[(sPE, 1)])
        prog.op("vector", lambda v: v.nop(), incs=[(sDV, 1)])
        prog.op("scalar", lambda s: s.nop(), incs=[(sAC, 1)])
        w = [(sPE, sPE.n), (sDV, sDV.n), (sAC, sAC.n), (sOU, sOU.n), (sSP, sSP.n)]
        for e in ("tensor", "vector", "scalar", "sync", "gpsimd"):
            prog.op(e, lambda x: x.nop(), waits=list(w))

    A = Arena(arena_t[:, :], 0)
    ident_bf = A.alloc(128, 128, BF16)
    ident_f32 = A.alloc(128, 128, F32)
    mask_bf = A.alloc(128, 128, BF16)
    ones_row_bf = A.alloc(1, 128, BF16)
    ones_row_f32 = A.alloc(1, 128, F32)
    ones_hi_t = A.alloc(128, 64, F32)
    ones_row_f32_hi = ones_hi_t[64:65, :]
    m128_row_bf = A.alloc(1, 128, BF16)
    ones_col_bf = A.alloc(128, 1, BF16)
    ccmask_sb = A.alloc(128, 8, F32)
    misc_sb = A.alloc(128, 2, F32)
    eps65_bf = A.alloc(1, 65, BF16)
    stats = A.alloc(128, 48, F32)
    S_sb = A.alloc(64, H * 65, F32)
    S_bf = A.alloc(128, H * 65, BF16)
    Tc_sb = A.alloc(64, H * 65, F32)
    d_scg = A.alloc(128, 64, F32)
    d_scu = A.alloc(128, 64, F32)
    d_scd = A.alloc(128, 16, F32)
    pers_end = A.off

    A.off = pers_end
    a_xT = A.alloc(128, DT * HF, BF16)
    a_kqT = A.alloc(128, OT * HF, BF16)
    a_ekT_st = A.alloc(128, 2 * HF, BF16)
    a_fqT_st = A.alloc(128, 2 * HF, BF16)
    a_ek = A.alloc(128, 8 * H * FD, BF16)
    a_vp = A.alloc(128, 8 * H * 65, BF16)
    a_wT = A.alloc(128, 2 * DT * 128, BF16)
    a_wV0 = A.alloc(128, DT * 512, BF16)
    a_wfq = A.alloc(128, H * FD, BF16)
    a_wfk = A.alloc(128, H * FD, BF16)
    a_sk = A.alloc(128, 8 * H, F32)
    a_rsk = A.alloc(128, 8 * H, F32)
    assert A.off <= ARENA_B, A.off
    a_wV1 = a_kqT[:, 0:DT * 512]

    A.off = pers_end
    c_woT = A.alloc(128, OT * Dm, BF16)
    c_ekT = [A.alloc(128, OT * CH, BF16) for _ in range(2)]
    c_fqT = [A.alloc(128, OT * CH, BF16) for _ in range(2)]
    c_ek = [A.alloc(128, H * FD, BF16) for _ in range(2)]
    c_vp = [A.alloc(128, H * 65, BF16) for _ in range(2)]
    c_xn = [A.alloc(128, Dm, F32) for _ in range(2)]
    c_am = A.alloc(128, 16 * CH, BF16)
    c_dnrc = A.alloc(128, 4096, F32)
    c_rcb = c_dnrc[0:64, 0:2048]
    c_dn = c_dnrc[64:65, 0:2048]
    c_rc = c_dnrc[64:65, 2048:4096]
    c_yn = A.alloc(128, 8 * CH, BF16)
    c_yT2 = [A.alloc(128, OT * CH, BF16) for _ in range(2)]
    c_r1 = A.alloc(128, Dm, F32)
    c_sq = A.alloc(128, Dm, BF16)
    c_xa = A.alloc(128, Dm, BF16)
    c_xaT = A.alloc(128, DT * CH, BF16)
    assert A.off <= ARENA_B, A.off
    b_rk = [c_r1[:, 0:1040], c_xn[0][:, 0:1040]]
    b_acc = c_xn[1][:, 0:1040]

    A.off = pers_end
    off_xa = A.off
    d_xaT = A.alloc(128, DT * TCF, BF16)
    end_xa = A.off
    A.off = off_xa
    d_r2t = A.alloc(128, Dm, F32)
    d_sq = A.alloc(128, Dm, F32)
    assert A.off == end_xa
    d_wg = [A.alloc(128, 16 * TCF, BF16) for _ in range(2)]
    off_wu = A.off
    d_wu = [A.alloc(128, 16 * TCF, BF16) for _ in range(2)]
    end_wu = A.off
    A.off = off_wu
    d_r2T = A.alloc(128, DT * TCF, F32)
    assert A.off == end_wu
    d_h = A.alloc(128, 64 * TCF, BF16)
    d_sg = [A.alloc(128, TCF, BF16) for _ in range(2)]
    d_uu = [A.alloc(128, TCF, BF16) for _ in range(2)]
    d_xsum = A.alloc(1, TCF, BF16)
    d_hsum = A.alloc(1, TCF, BF16)
    d_dbgst = A.alloc(128, TCF, F32) if debug else None
    assert A.off <= ARENA_B, A.off
    d_wd = [d_wg[0][:, 0:4096], d_wg[1][:, 0:4096]]

    def ld(dst, src):
        return lambda g: g.dma_start(out=dst, in_=src)

    for dst, src in (
            (ident_bf[:, :], ident_d[:, :]),
            (ident_f32[:, :], ident_d[:, :]),
            (mask_bf[:, :], maskA_d[:, :]),
            (ones_row_bf[:, :], onesrow_d[:, :]),
            (ones_row_f32[:, :], onesrow_d[:, :]),
            (ones_row_f32_hi[:, :], onesrow_d[:, 0:64]),
            (m128_row_bf[:, :], m128row_d[:, :]),
            (ones_col_bf[:, :], onescol_d[:, :]),
            (ccmask_sb[:, :], ccmask_d[:, :]),
            (misc_sb[:, :], misc_d[:, :]),
            (eps65_bf[:, :], eps65_d[:, :]),
            (a_wfq[0:64, :].rearrange("p (h f) -> p h f", h=H),
             wfq_d.rearrange("(h p) f -> p h f", p=64)),
            (a_wfq[64:128, :].rearrange("p (h f) -> p h f", h=H),
             wfq_d.rearrange("(h p) f -> p h f", p=64)),
            (a_wfk[0:64, :].rearrange("p (h f) -> p h f", h=H),
             wfk_d.rearrange("(h p) f -> p h f", p=64)),
            (a_wfk[64:128, :].rearrange("p (h f) -> p h f", h=H),
             wfk_d.rearrange("(h p) f -> p h f", p=64)),
            (d_scg[:, :], scg_d[:, :]),
            (d_scu[:, :], scu_d[:, :]),
            (d_scd[:, :], scd_d[:, :]),
    ):
        prog.op("gpsimd", ld(dst, src), incs=[(sWF, 16)])
    WF_ALL = sWF.n

    prog.op("vector", lambda v: v.memset(Tc_sb[:, :], 0.0), incs=[(sDV, 1)])

    bank_free = [[] for _ in range(8)]

    def pe_wait_bank(k):
        w = bank_free[k]
        bank_free[k] = []
        return w

    wT_free = [[], []]
    wV_free = [[], []]
    ekst_free = [[], []]
    fqst_free = [[], []]
    prev_half = {"sp": 0, "dv": 0}

    def half_A(hf):
        t0 = hf * HF
        xw = [(sPE, sPE.n)] if hf else []
        for j in range(4):
            src = xT_d[:, t0:t0 + HF].rearrange("(dt p) t -> p dt t", p=128)[:, j * 4:(j + 1) * 4, :]
            dst = a_xT.rearrange("p (dt t) -> p dt t", dt=DT)[:, j * 4:(j + 1) * 4, :]
            prog.op("gpsimd", ld(dst, src), waits=xw if j == 0 else [], incs=[(sXT, 16)])
        xt_ready = sXT.n

        for which, w_dram in (("k", wkT_d), ("q", wqT_d)):
            for ot in range(OT):
                par = ot % 2
                src = w_dram[:, ot * 128:(ot + 1) * 128].rearrange("(dt p) o -> p dt o", p=128)
                dst = a_wT[:, par * DT * 128:(par + 1) * DT * 128].rearrange("p (dt o) -> p dt o", dt=DT)
                prog.op("gpsimd", ld(dst, src), waits=wT_free[par], incs=[(sWT[par], 16)])
                wT_free[par] = []
                w_ready = sWT[par].n
                for nt in range(HF // 512):
                    bk = (nt + ot) % 2
                    pw = pe_wait_bank(bk) + ([(sWT[par], w_ready)] if nt == 0 else [])
                    if nt == 0 and ot == 0:
                        pw.append((sXT, xt_ready))
                    for dt in range(DT):
                        lhs = a_wT[:, par * DT * 128 + dt * 128: par * DT * 128 + (dt + 1) * 128]
                        rhs = a_xT[:, dt * HF + nt * 512: dt * HF + (nt + 1) * 512]
                        prog.op("tensor",
                                (lambda l, r, b, s0, s1: lambda t: t.matmul(
                                    pb[b][:, :], l, r, start=s0, stop=s1))(
                                    lhs, rhs, bk, dt == 0, dt == DT - 1),
                                waits=pw if dt == 0 else [],
                                incs=[(sPE, 1)] if dt == DT - 1 else [])
                    mmv = sPE.n
                    dst_kq = a_kqT[:, ot * HF + nt * 512: ot * HF + (nt + 1) * 512]
                    prog.op("vector",
                            (lambda d, b: lambda v: v.tensor_copy(d, pb[b][:, :]))(dst_kq, bk),
                            waits=[(sPE, mmv)], incs=[(sDV, 1)])
                    bank_free[bk] = [(sDV, sDV.n)]
                wT_free[par] = [(sPE, sPE.n)]

            for h in range(H):
                ot, pbase = h // 2, (h % 2) * 64
                wf = a_wfk if which == "k" else a_wfq
                stt = a_ekT_st if which == "k" else a_fqT_st
                stf = ekst_free if which == "k" else fqst_free
                for nt in range(HF // 512):
                    bk = 2 + (h + nt) % 2
                    pw = pe_wait_bank(bk)
                    if h == 0 and nt == 0:
                        pw += [(sWF, WF_ALL), (sDV, sDV.n)]
                    prog.op("tensor",
                            (lambda hh, oo, pp, nn, b, w: lambda t: t.matmul(
                                pb[b][0:64, 0:512],
                                w[pp:pp + 64, hh * FD:(hh + 1) * FD],
                                a_kqT[pp:pp + 64, oo * HF + nn * 512: oo * HF + (nn + 1) * 512],
                                start=True, stop=True))(h, ot, pbase, nt, bk, wf),
                            waits=pw, incs=[(sPE, 1)])
                    mmv = sPE.n
                    aw = [(sPE, mmv)]
                    if pbase == 0 and nt == 0:
                        aw += stf[ot % 2]
                        stf[ot % 2] = []
                    dstst = stt[pbase:pbase + 64, (ot % 2) * HF + nt * 512:(ot % 2) * HF + (nt + 1) * 512]
                    prog.op("scalar",
                            (lambda d, b: lambda s: s.activation(d, pb[b][0:64, 0:512], AF.Exp))(dstst, bk),
                            waits=aw, incs=[(sAC, 1)])
                    bank_free[bk] = [(sAC, sAC.n)]
                if which == "k":
                    for tt in range(HF // CH):
                        bk = 4 + (h + tt) % 2
                        pw = pe_wait_bank(bk)
                        prog.op("tensor",
                                (lambda hh, oo, pp, t_, b: lambda t: t.matmul(
                                    pb[b][:, 0:64],
                                    a_kqT[pp:pp + 64, oo * HF + t_ * CH: oo * HF + (t_ + 1) * CH],
                                    a_wfk[pp:pp + 64, hh * FD:(hh + 1) * FD],
                                    start=True, stop=True))(h, ot, pbase, tt, bk),
                                waits=pw, incs=[(sPE, 1)])
                        mmv = sPE.n
                        aw = [(sPE, mmv)]
                        if h == 0 and tt == 0 and hf == 1:
                            aw.append((sSP, prev_half["sp"]))
                        prog.op("scalar",
                                (lambda hh, t_, b: lambda s: s.activation(
                                    a_ek[:, t_ * H * FD + hh * FD: t_ * H * FD + (hh + 1) * FD],
                                    pb[b][:, 0:64], AF.Exp,
                                    accum_out=a_sk[:, t_ * H + hh: t_ * H + hh + 1]))(h, tt, bk),
                                waits=aw, incs=[(sAC, 1)])
                        bank_free[bk] = [(sAC, sAC.n)]
                if h % 2 == 1:
                    dd = ekT_d if which == "k" else fqT_d
                    acv = sAC.n
                    prog.op("scalar",
                            (lambda o, s_, d_: lambda s: s.dma_start(
                                out=d_[:, o * T + t0: o * T + t0 + HF],
                                in_=s_[:, (o % 2) * HF:(o % 2 + 1) * HF]))(ot, stt, dd),
                            waits=[(sAC, acv)], incs=[(sSP, 16)])
                    stf[ot % 2] = [(sSP, sSP.n)]

            if which == "k":
                kfeat_pe = sPE.n
                prog.op("vector",
                        lambda v: v.reciprocal(a_rsk[:, 0:8 * H], a_sk[:, 0:8 * H]),
                        waits=[(sAC, sAC.n)], incs=[(sDV, 1)])
                rsk_ready = sDV.n

                for oc in range(4):
                    par = oc % 2
                    wv_buf = a_wV0 if par == 0 else a_wV1
                    src = wvT_d[:, oc * 512:(oc + 1) * 512].rearrange("(dt p) o -> p dt o", p=128)
                    dst = wv_buf.rearrange("p (dt o) -> p dt o", dt=DT)
                    w = list(wV_free[par])
                    if par == 1:
                        w.append((sPE, kfeat_pe))
                    prog.op("gpsimd", ld(dst, src), waits=w, incs=[(sWV[par], 16)])
                    wV_free[par] = []
                    wv_ready = sWV[par].n
                    for tt in range(HF // CH):
                        bk = (oc + tt) % 2
                        pw = pe_wait_bank(bk) + ([(sWV[par], wv_ready)] if tt == 0 else [])
                        for dt in range(DT):
                            lhs = a_xT[:, dt * HF + tt * CH: dt * HF + (tt + 1) * CH]
                            rhs = wv_buf[:, dt * 512:(dt + 1) * 512]
                            prog.op("tensor",
                                    (lambda l, r, b, s0, s1: lambda t: t.matmul(
                                        pb[b][:, :],
                                        l, r, start=s0, stop=s1))(lhs, rhs, bk, dt == 0, dt == DT - 1),
                                    waits=pw if dt == 0 else [],
                                    incs=[(sPE, 1)] if dt == DT - 1 else [])
                        mmv = sPE.n
                        for hh in range(8):
                            h = oc * 8 + hh
                            aw = []
                            if hh == 0:
                                aw = [(sPE, mmv), (sDV, rsk_ready)]
                                if tt == 0 and oc == 0 and hf == 1:
                                    aw.append((sSP, prev_half["sp"]))
                            prog.op("scalar",
                                    (lambda h_, t_, b, hh_: lambda s: s.activation(
                                        a_vp[:, t_ * H * 65 + h_ * 65: t_ * H * 65 + h_ * 65 + 64],
                                        pb[b][:, hh_ * 64:(hh_ + 1) * 64], AF.Copy,
                                        scale=a_rsk[:, t_ * H + h_: t_ * H + h_ + 1]))(h, tt, bk, hh),
                                    waits=aw, incs=[(sAC, 1)] if hh == 7 else [])
                        bank_free[bk] = [(sAC, sAC.n)]
                    wV_free[par] = [(sPE, sPE.n)]
                for tt in range(HF // CH):
                    prog.op("vector",
                            (lambda t_: lambda v: v.tensor_copy(
                                a_vp[:, t_ * H * 65 + 64: t_ * H * 65 + 64 + (H - 1) * 65 + 1: 65],
                                a_rsk[:, t_ * H:(t_ + 1) * H]))(tt),
                            waits=[(sAC, sAC.n)] if tt == 0 else [], incs=[(sDV, 1)])
                vp_done_dv = sDV.n

                for tt in range(HF // CH):
                    for hg in range(8):
                        bk = 6 + (tt * 8 + hg) % 2
                        pw = pe_wait_bank(bk)
                        if tt == 0 and hg == 0:
                            pw += [(sDV, vp_done_dv)]
                        for j in range(4):
                            h = hg * 4 + j
                            prog.op("tensor",
                                    (lambda h_, t_, b, j_: lambda t: t.matmul(
                                        pb[b][0:64, j_ * 65:(j_ + 1) * 65],
                                        a_ek[:, t_ * H * FD + h_ * FD: t_ * H * FD + (h_ + 1) * FD],
                                        a_vp[:, t_ * H * 65 + h_ * 65: t_ * H * 65 + (h_ + 1) * 65],
                                        start=True, stop=True))(h, tt, bk, j),
                                    waits=pw if j == 0 else [],
                                    incs=[(sPE, 1)] if j == 3 else [])
                        mmv = sPE.n
                        aw = [(sPE, mmv)]
                        if tt == 0 and hg == 0 and hf == 0:
                            aw.append((sDV, 1))
                        prog.op("vector",
                                (lambda b, g_: lambda v: v.tensor_tensor(
                                    Tc_sb[:, g_ * 260:(g_ + 1) * 260],
                                    pb[b][0:64, 0:260],
                                    Tc_sb[:, g_ * 260:(g_ + 1) * 260], ALU.add))(bk, hg),
                                waits=aw, incs=[(sDV, 1)])
                        bank_free[bk] = [(sDV, sDV.n)]

                dvv = sDV.n
                for tt in range(HF // CH):
                    prog.op("scalar",
                            (lambda t_: lambda s: s.dma_start(
                                out=ek_d[t0 + t_ * CH: t0 + (t_ + 1) * CH, :],
                                in_=a_ek[:, t_ * H * FD:(t_ + 1) * H * FD]))(tt),
                            waits=[(sDV, dvv)] if tt == 0 else [], incs=[(sSP, 16)])
                    prog.op("scalar",
                            (lambda t_: lambda s: s.dma_start(
                                out=vp_d[t0 + t_ * CH: t0 + (t_ + 1) * CH, :],
                                in_=a_vp[:, t_ * H * 65:(t_ + 1) * H * 65]))(tt),
                            incs=[(sSP, 16)])

    half_A(0)
    prev_half = {"sp": sSP.n, "dv": sDV.n}
    half_A(1)
    a_done_dv = sDV.n
    a_done_sp = sSP.n
    a_done_ac = sAC.n
    a_done_pe = sPE.n

    if stage == "a":
        if debug:
            prog.op("sync", lambda s: s.dma_start(out=tc_dbg[:, :], in_=Tc_sb[:, :]),
                    waits=[(sDV, a_done_dv)], incs=[(sOU, 16)])
            for nm in ("ekT_f", "fqT_f", "ek_f", "vp_f"):
                o, i = dbg_f32[nm]
                prog.op("gpsimd", (lambda o_, i_: lambda g: g.dma_start(out=o_[:, :], in_=i_[:, :]))(o, i),
                        waits=[(sSP, a_done_sp)], incs=[(sOU, 16)])
        prog.op("sync", lambda s: s.nop(),
                waits=[(sOU, sOU.n), (sSP, a_done_sp), (sDV, a_done_dv), (sAC, a_done_ac)])
        _finalize(nc, prog, guards)
        return nc

    prog.op("sync", lambda s: s.dma_start(out=ag_in[:, :], in_=Tc_sb[:, :]),
            waits=[(sDV, a_done_dv)], incs=[(sXN, 16)])
    prog.op("gpsimd",
            lambda g: g.collective_compute(
                "AllGather", ALU.bypass, replica_groups=[list(range(NCORES))],
                ins=[ag_in[:, :].opt()], outs=[ag_out[:, :].opt()]),
            waits=[(sXN, sXN.n)], incs=[(sCC, 1)])
    prog.op("vector", lambda v: v.memset(b_acc[:, :], 0.0),
            waits=[(sWF, WF_ALL), (sSP, a_done_sp)], incs=[(sDV, 1)])
    rd_free = [[], []]
    for rk in range(NCORES):
        par = rk % 2
        src = ag_out[rk * 64:(rk + 1) * 64, :].rearrange("r (two c) -> (r two) c", two=2)
        w = [(sCC, 1), (sSP, a_done_sp)] if rk == 0 else []
        w += rd_free[par]
        rd_free[par] = []
        prog.op("sync",
                (lambda s_, d_: lambda s: s.dma_start(out=d_, in_=s_))(src, b_rk[par][:, :]),
                waits=w, incs=[(sAG, 16)])
        agv = sAG.n
        prog.op("vector",
                (lambda p_, r_: lambda v: v.scalar_tensor_tensor(
                    b_acc[:, :], b_rk[p_][:, :],
                    ccmask_sb[:, r_:r_ + 1],
                    b_acc[:, :], ALU.mult, ALU.add))(par, rk),
                waits=[(sAG, agv), (sDV, sDV.n)], incs=[(sDV, 1)])
        rd_free[par] = [(sDV, sDV.n)]
    prog.op("sync", lambda s: s.dma_start(
        out=ag_in[:, :].rearrange("r (two c) -> (r two) c", two=2), in_=b_acc[:, :]),
            waits=[(sDV, sDV.n)], incs=[(sXN, 16)])
    prog.op("sync", lambda s: s.dma_start(out=S_sb[:, :], in_=ag_in[:, :]),
            waits=[(sXN, sXN.n)], incs=[(sXN, 16)])
    sinit_xn = sXN.n
    prog.op("scalar", lambda s: s.activation(S_bf[0:64, :], S_sb[:, :], AF.Copy),
            waits=[(sXN, sinit_xn)], incs=[(sAC, 1)])
    prog.op("scalar", lambda s: s.activation(S_bf[64:128, :], S_sb[:, :], AF.Copy),
            incs=[(sAC, 1)])
    sbf_ready = sAC.n

    if debug:
        prog.op("sync", lambda s: s.dma_start(out=s0_dbg[:, :], in_=S_sb[:, :]),
                waits=[(sXN, sinit_xn)], incs=[(sOU, 16)])
        prog.op("sync", lambda s: s.dma_start(out=tc_dbg[:, :], in_=Tc_sb[:, :]),
                incs=[(sOU, 16)])

    for ot in range(OT):
        prog.op("gpsimd",
                (lambda o: lambda g: g.dma_start(
                    out=c_woT[:, o * Dm:(o + 1) * Dm],
                    in_=woT_d[o * 128:(o + 1) * 128, :]))(ot),
                waits=[(sPE, a_done_pe), (sAC, a_done_ac)] if ot == 0 else [],
                incs=[(sWO, 16)])
    wo_ready = sWO.n

    ch_free = [[], []]

    def load_chunk(ch):
        par = ch % 2
        c0 = ch * CH
        w = list(ch_free[par]) + ([(sSP, a_done_sp)] if ch < 2 else [])
        ch_free[par] = []
        if ch < 2:
            w.append((sDV, sDV.n))
        prog.op("sync",
                (lambda p, c: lambda s: s.dma_start(
                    out=c_ekT[p].rearrange("p (o t) -> p o t", o=OT),
                    in_=ekT_d.rearrange("p (o t) -> p o t", o=OT)[:, :, c:c + CH]))(par, c0),
                waits=w, incs=[(sCH[par], 16)])
        prog.op("sync",
                (lambda p, c: lambda s: s.dma_start(
                    out=c_fqT[p].rearrange("p (o t) -> p o t", o=OT),
                    in_=fqT_d.rearrange("p (o t) -> p o t", o=OT)[:, :, c:c + CH]))(par, c0),
                incs=[(sCH[par], 16)])
        prog.op("sync",
                (lambda p, c: lambda s: s.dma_start(
                    out=c_ek[p][:, :], in_=ek_d[c:c + CH, :]))(par, c0),
                incs=[(sCH[par], 16)])
        prog.op("sync",
                (lambda p, c: lambda s: s.dma_start(
                    out=c_vp[p][:, :], in_=vp_d[c:c + CH, :]))(par, c0),
                incs=[(sCH[par], 16)])
        prog.op("sync",
                (lambda p, c: lambda s: s.dma_start(
                    out=c_xn[p][:, :], in_=xn_d[c:c + CH, :]))(par, c0),
                incs=[(sCH[par], 16)])

    load_chunk(0)
    load_chunk(1)

    sbf_wait = sbf_ready
    last = {"bc_mm": [0, 0], "mult_dv": [0, 0]}
    wo_pe_par = [0, 0]
    prev_xat_sp = 0
    prev_tr_pe = 0
    prev_sq_ac = 0

    for ch in range(NCH):
        par = ch % 2
        chv = sCH[par].n
        if debug and stage == "c" and ch == 8:
            prog.op("gpsimd", (lambda p: lambda g: g.dma_start(out=sbf_dbg[:, :], in_=S_bf[:, :]))(par),
                    waits=[(sAC, sbf_wait)], incs=[(sOU, 16)])
            prog.op("gpsimd", (lambda p: lambda g: g.dma_start(out=fq8_dbg[:, :], in_=c_fqT[p][:, :]))(par),
                    waits=[(sCH[par], chv)], incs=[(sOU, 16)])
            prog.op("gpsimd", (lambda p: lambda g: g.dma_start(out=vp8_dbg[:, :], in_=c_vp[p][:, :]))(par),
                    incs=[(sOU, 16)])
        stage_list = []
        QS = [dict() for _ in range(8)]

        def _mk(g):
            pg = g % 2
            B = (0, 1, 2, 3) if pg == 0 else (4, 5, 6, 7)
            st = QS[g]

            def st_at():
                pw = []
                for b in B:
                    pw += pe_wait_bank(b)
                if g == 0:
                    pw += [(sCH[par], chv)]
                    if ch == 0:
                        pw += [(sWF, WF_ALL)]
                for j in range(4):
                    h = g * 4 + j
                    ot, pbase = h // 2, (h % 2) * 64
                    prog.op("tensor",
                            (lambda p, o, pbs, b: lambda t: t.matmul(
                                pb[b][:, 0:CH],
                                c_ekT[p][pbs:pbs + 64, o * CH:(o + 1) * CH],
                                c_fqT[p][pbs:pbs + 64, o * CH:(o + 1) * CH],
                                start=True, stop=True))(par, ot, pbase, B[j]),
                            waits=pw if j == 0 else [], incs=[(sPE, 1)] if j == 3 else [])
                st["at_pe"] = sPE.n

            def st_mask():
                for j in range(4):
                    h = g * 4 + j
                    prog.op("vector",
                            (lambda b, s_: lambda v: v.tensor_tensor(
                                c_am[:, s_ * CH:(s_ + 1) * CH],
                                pb[b][:, 0:CH], mask_bf[:, :], ALU.mult))(B[j], (g % 4) * 4 + j),
                            waits=[(sPE, st["at_pe"])] if j == 0 else [],
                            incs=[(sDV, 1)] if j == 3 else [])
                st["am_dv"] = sDV.n

            def st_ynum():
                pw = [(sDV, st["am_dv"])]
                if g == 0:
                    pw += [(sAC, sbf_wait)]
                    if ch == 0:
                        pw += [(sXN, sinit_xn)]
                for j in range(4):
                    h = g * 4 + j
                    ot, pbase = h // 2, (h % 2) * 64
                    b = B[j]
                    prog.op("tensor",
                            (lambda p, hh, b_, s_: lambda t: t.matmul(
                                pb[b_][0:65, 0:CH],
                                c_vp[p][:, hh * 65:(hh + 1) * 65],
                                c_am[:, s_ * CH:(s_ + 1) * CH], start=True, stop=False))(par, h, b, (g % 4) * 4 + j),
                            waits=pw if j == 0 else [])
                    prog.op("tensor",
                            (lambda p, hh, o, pbs, b_: lambda t: t.matmul(
                                pb[b_][0:65, 0:CH],
                                S_bf[pbs:pbs + 64, hh * 65:(hh + 1) * 65],
                                c_fqT[p][pbs:pbs + 64, o * CH:(o + 1) * CH],
                                start=False, stop=False))(par, h, ot, pbase, b))
                    prog.op("tensor",
                            (lambda b_: lambda t: t.matmul(
                                pb[b_][0:65, 0:CH], eps65_bf[:, :], ones_row_bf[:, 0:CH],
                                start=False, stop=True))(b),
                            incs=[(sPE, 1)] if j == 3 else [])
                st["yn_pe"] = sPE.n

            def st_evrc():
                for j in range(4):
                    prog.op("vector",
                            (lambda b, jj, p_: lambda v: v.tensor_copy(
                                c_yn[0:64, p_ * 4 * CH + jj * CH: p_ * 4 * CH + (jj + 1) * CH],
                                pb[b][0:64, 0:CH]))(B[j], j, pg),
                            waits=[(sPE, st["yn_pe"]), (sDV, sDV.n)] if j == 0 else [])
                    prog.op("vector",
                            (lambda b, jj, p_: lambda v: v.reciprocal(
                                c_rc[:, p_ * 4 * CH + jj * CH: p_ * 4 * CH + (jj + 1) * CH],
                                pb[b][64:65, 0:CH]))(B[j], j, pg),
                            waits=[(sPE, last["bc_mm"][pg])] if j == 0 else [],
                            incs=[(sDV, 1)] if j == 3 else [])
                st["rc_dv"] = sDV.n

            def st_bc():
                pw = [(sDV, st["rc_dv"])]
                for j in range(4):
                    prog.op("tensor",
                            (lambda b, jj, p_: lambda t: t.matmul(
                                pb[b][0:64, 0:CH], ones_row_f32_hi[:, :],
                                c_rc[:, p_ * 4 * CH + jj * CH: p_ * 4 * CH + (jj + 1) * CH],
                                start=True, stop=True))(B[j], j, pg),
                            waits=pw if j == 0 else [], incs=[(sPE, 1)] if j == 3 else [])
                st["bc_pe"] = sPE.n
                last["bc_mm"][pg] = sPE.n

            def st_rcb():
                aw = [(sPE, st["bc_pe"]), (sDV, last["mult_dv"][pg])]
                for j in range(4):
                    prog.op("scalar",
                            (lambda b, jj, p_: lambda s: s.activation(
                                c_rcb[:, p_ * 4 * CH + jj * CH: p_ * 4 * CH + (jj + 1) * CH],
                                pb[b][0:64, 0:CH], AF.Copy))(B[j], j, pg),
                            waits=aw if j == 0 else [], incs=[(sAC, 1)] if j == 3 else [])
                st["rcb_ac"] = sAC.n
                for b in B:
                    bank_free[b] = [(sAC, sAC.n)]

            def st_mult():
                aw = [(sAC, st["rcb_ac"])]
                if g == 0:
                    aw.append((sPE, wo_pe_par[par]))
                for j in range(4):
                    h = g * 4 + j
                    ot, pbase = h // 2, (h % 2) * 64
                    prog.op("vector",
                            (lambda jj, p_, o, pbs, cp_: lambda v: v.tensor_tensor(
                                c_yT2[cp_][pbs:pbs + 64, o * CH:(o + 1) * CH],
                                c_yn[0:64, p_ * 4 * CH + jj * CH: p_ * 4 * CH + (jj + 1) * CH],
                                c_rcb[:, p_ * 4 * CH + jj * CH: p_ * 4 * CH + (jj + 1) * CH],
                                ALU.mult))(j, pg, ot, pbase, par),
                            waits=aw if j == 0 else [], incs=[(sDV, 1)] if j == 3 else [])
                last["mult_dv"][pg] = sDV.n

            return [st_at, st_mask, st_ynum, st_evrc, st_bc, st_rcb, st_mult]

        for g in range(8):
            for s, fn in enumerate(_mk(g)):
                stage_list.append((4 * g + s, g, fn))
        stage_list.sort(key=lambda x: (x[0], x[1]))
        for _, _, fn in stage_list:
            fn()
        heads_dv = sDV.n
        if debug and stage == "c":
            prog.op("gpsimd",
                    (lambda c: lambda g: g.dma_start(
                        out=yT_dbg[:, c * OT * CH:(c + 1) * OT * CH], in_=c_yT2[c % 2][:, :]))(ch),
                    waits=[(sDV, heads_dv)], incs=[(sOU, 16)])

        for hg in range(8):
            bk = 6 + hg % 2
            pw = pe_wait_bank(bk)
            for j in range(4):
                h = hg * 4 + j
                prog.op("tensor",
                        (lambda p, h_, b, j_: lambda t: t.matmul(
                            pb[b][0:64, j_ * 65:(j_ + 1) * 65],
                            c_ek[p][:, h_ * FD:(h_ + 1) * FD],
                            c_vp[p][:, h_ * 65:(h_ + 1) * 65],
                            start=True, stop=True))(par, h, bk, j),
                        waits=pw if j == 0 else [],
                        incs=[(sPE, 1)] if j == 3 else [])
            mmv = sPE.n
            prog.op("vector",
                    (lambda b, g_: lambda v: v.tensor_tensor(
                        S_sb[:, g_ * 260:(g_ + 1) * 260],
                        pb[b][0:64, 0:260],
                        S_sb[:, g_ * 260:(g_ + 1) * 260], ALU.add))(bk, hg),
                    waits=[(sPE, mmv)], incs=[(sDV, 1)])
            bank_free[bk] = [(sDV, sDV.n)]
        states_pe = sPE.n
        if debug and stage == "c":
            prog.op("gpsimd",
                    (lambda c: lambda g: g.dma_start(
                        out=s_dbg[:, c * H * 65:(c + 1) * H * 65], in_=S_sb[:, :]))(ch),
                    waits=[(sDV, sDV.n)], incs=[(sOU, 16)])
        prog.op("scalar", lambda s: s.activation(S_bf[0:64, :], S_sb[:, :], AF.Copy),
                waits=[(sDV, sDV.n), (sPE, states_pe)], incs=[(sAC, 1)])
        prog.op("scalar", lambda s: s.activation(S_bf[64:128, :], S_sb[:, :], AF.Copy),
                incs=[(sAC, 1)])
        sbf_wait = sAC.n

        for ncx in range(4):
            bk = 4 + ncx % 2
            pw = pe_wait_bank(bk) + ([(sDV, heads_dv), (sWO, wo_ready)] if ncx == 0 else [])
            for ot in range(OT):
                prog.op("tensor",
                        (lambda o, n_, b, cp_: lambda t: t.matmul(
                            pb[b][:, :],
                            c_yT2[cp_][:, o * CH:(o + 1) * CH],
                            c_woT[:, o * Dm + n_ * 512: o * Dm + (n_ + 1) * 512],
                            start=(o == 0), stop=(o == OT - 1)))(ot, ncx, bk, par),
                        waits=pw if ot == 0 else [],
                        incs=[(sPE, 1)] if ot == OT - 1 else [])
            mmv = sPE.n
            aw = [(sPE, mmv), (sCH[par], chv)]
            if ncx == 0:
                aw.append((sAC, prev_sq_ac))
            prog.op("vector",
                    (lambda p, n_, b: lambda v: v.tensor_tensor(
                        c_r1[:, n_ * 512:(n_ + 1) * 512], pb[b][:, :],
                        c_xn[p][:, n_ * 512:(n_ + 1) * 512], ALU.add))(par, ncx, bk),
                    waits=aw, incs=[(sDV, 1)])
            bank_free[bk] = [(sDV, sDV.n)]
        wo_pe = sPE.n
        wo_pe_par[par] = wo_pe
        r1_dv = sDV.n
        if debug and stage == "c" and ch == 0:
            prog.op("gpsimd", lambda g: g.dma_start(out=r1_dbg[:, :], in_=c_r1[:, :]),
                    waits=[(sDV, r1_dv)], incs=[(sOU, 16)])

        prog.op("vector", lambda v: v.reduce_sum(stats[:, 0:1], c_r1[:, :], axis=AX.X),
                incs=[(sDV, 1)])
        prog.op("vector", lambda v: v.tensor_scalar_mul(stats[:, 0:1], stats[:, 0:1], 1.0 / Dm),
                waits=[(sDV, sDV.n)], incs=[(sDV, 1)])
        prog.op("scalar", lambda s: s.activation(c_sq[:, :], c_r1[:, :], AF.Square,
                                                 accum_out=stats[:, 12:13]),
                waits=[(sDV, r1_dv)], incs=[(sAC, 1)])
        sq_ac = sAC.n
        prev_sq_ac = sq_ac
        prog.op("vector", lambda v: v.tensor_tensor(stats[:, 4:5], stats[:, 0:1], stats[:, 0:1], ALU.mult),
                waits=[(sDV, sDV.n)], incs=[(sDV, 1)])
        prog.op("vector", lambda v: v.tensor_scalar_add(stats[:, 4:5], stats[:, 4:5], -EPS_LN),
                waits=[(sDV, sDV.n)], incs=[(sDV, 1)])
        prog.op("vector", lambda v: v.scalar_tensor_tensor(
            stats[:, 8:9], stats[:, 12:13], 1.0 / Dm, stats[:, 4:5], ALU.mult, ALU.subtract),
                waits=[(sAC, sq_ac), (sDV, sDV.n)], incs=[(sDV, 1)])
        var_dv = sDV.n
        prog.op("scalar", lambda s: s.activation(stats[:, 16:17], stats[:, 8:9], AF.Ln),
                waits=[(sDV, var_dv)], incs=[(sAC, 1)])
        prog.op("scalar", lambda s: s.activation(stats[:, 20:21], stats[:, 16:17], AF.Exp, scale=misc_sb[:, 0:1]),
                waits=[(sAC, sAC.n)], incs=[(sAC, 1)])
        rstd_ac = sAC.n
        if debug and stage == "c" and ch == 0:
            prog.op("gpsimd", lambda g: g.dma_start(out=st_dbg[:, :], in_=stats[:, :]),
                    waits=[(sAC, rstd_ac)], incs=[(sOU, 16)])
        aw = [(sAC, rstd_ac), (sPE, prev_tr_pe)]
        prog.op("vector", lambda v: v.tensor_scalar(
            c_xa[:, :], c_r1[:, :], stats[:, 0:1], stats[:, 20:21], ALU.subtract, ALU.mult),
                waits=aw, incs=[(sDV, 1)])
        xa_dv = sDV.n

        for dt in range(DT):
            bk = 6 + dt % 2
            pw = pe_wait_bank(bk) + ([(sDV, xa_dv)] if dt < 2 else [])
            prog.op("tensor",
                    (lambda d, b: lambda t: t.transpose(
                        pb[b][:, 0:64].bitcast(BF16), c_xa[:, d * 128:(d + 1) * 128],
                        ident_bf[:, :]))(dt, bk),
                    waits=pw, incs=[(sPE, 1)])
            mmv = sPE.n
            aw = [(sPE, mmv)]
            if dt == 0:
                aw.append((sSP, prev_xat_sp))
            prog.op("vector",
                    (lambda d, b: lambda v: v.tensor_copy(
                        c_xaT[:, d * CH:(d + 1) * CH], pb[b][:, 0:64].bitcast(BF16)))(dt, bk),
                    waits=aw, incs=[(sDV, 1)])
            bank_free[bk] = [(sDV, sDV.n)]
        prev_tr_pe = sPE.n
        xat_dv = sDV.n
        prog.op("scalar",
                (lambda c: lambda s: s.dma_start(
                    out=xaT_d.rearrange("p (d t) -> p d t", d=DT)[:, :, c * CH:(c + 1) * CH],
                    in_=c_xaT.rearrange("p (d t) -> p d t", d=DT)))(ch),
                waits=[(sDV, xat_dv)], incs=[(sSP, 16)])
        prev_xat_sp = sSP.n

        if serial:
            barrier()
        ch_free[par] = [(sPE, states_pe), (sDV, r1_dv)]
        if ch + 2 < NCH:
            load_chunk(ch + 2)

    c_done_sp = sSP.n
    c_done_pe = sPE.n
    c_done_dv = sDV.n
    c_done_ac = sAC.n

    if stage == "c":
        if debug:
            for nm in ("xaT_f", "fqT_f", "ekT_f"):
                o, i = dbg_f32[nm]
                prog.op("gpsimd", (lambda o_, i_: lambda g: g.dma_start(out=o_[:, :], in_=i_[:, :]))(o, i),
                        waits=[(sSP, c_done_sp)], incs=[(sOU, 16)])
        prog.op("sync", lambda s: s.nop(),
                waits=[(sSP, c_done_sp), (sDV, sDV.n), (sOU, sOU.n)])
        _finalize(nc, prog, guards)
        return nc

    wg_free = [[], []]
    wd_free = [[], []]
    xa_free = []
    gateup_pe_done = 0
    last_mult_dv = [0, 0]

    for tc in range(NTC):
        tb = tc * TCF
        w = list(xa_free) + ([(sSP, c_done_sp), (sPE, c_done_pe), (sDV, c_done_dv),
                              (sAC, c_done_ac)] if tc == 0 else [])
        xa_free = []
        prog.op("sync",
                (lambda t_: lambda s: s.dma_start(
                    out=d_xaT.rearrange("p (d t) -> p d t", d=DT),
                    in_=xaT_d.rearrange("p (d t) -> p d t", d=DT)[:, :, t_ * TCF:(t_ + 1) * TCF]))(tc),
                waits=w, incs=[(sXA, 16)])
        xa_ready = sXA.n

        pw = pe_wait_bank(4) + [(sXA, xa_ready)]
        for dt in range(DT):
            prog.op("tensor",
                    (lambda d: lambda t: t.matmul(
                        pb[4][0:1, 0:TCF], ones_col_bf[:, :],
                        d_xaT[:, d * TCF:(d + 1) * TCF],
                        start=(d == 0), stop=(d == DT - 1)))(dt),
                    waits=pw if dt == 0 else [],
                    incs=[(sPE, 1)] if dt == DT - 1 else [])
        mmv = sPE.n
        prog.op("scalar", lambda s: s.activation(d_xsum[:, :], pb[4][0:1, 0:TCF], AF.Copy),
                waits=[(sPE, mmv)], incs=[(sAC, 1)])
        xsum_ac = sAC.n
        bank_free[4] = [(sAC, xsum_ac)]
        if debug and stage == "full" and tc == 0:
            prog.op("gpsimd", lambda g: g.dma_start(out=xs_dbg[:, :], in_=d_xsum[:, :]),
                    waits=[(sAC, xsum_ac)], incs=[(sOU, 16)])

        for ic in range(16):
            par = ic % 2
            for wdram, buf in ((wg8_d, d_wg[par]), (wu8_d, d_wu[par])):
                w2 = list(wg_free[par]) if wdram is wg8_d else []
                if tc == 0 and ic < 2 and wdram is wg8_d:
                    w2 += [(sPE, c_done_pe), (sDV, c_done_dv), (sAC, c_done_ac),
                           (sSP, c_done_sp)]
                prog.op("gpsimd",
                        (lambda wd_, b_, i_: lambda g: g.dma_start(
                            out=b_.rearrange("p (ht i) -> p ht i", ht=16),
                            in_=wd_[:, i_ * TCF:(i_ + 1) * TCF].rearrange(
                                "(ht p) i -> p ht i", p=128)))(wdram, buf, ic),
                        waits=w2,
                        incs=[(sWG[par], 16)])
            wg_free[par] = []
            wg_ready = sWG[par].n
            if debug and stage == "full" and tc == 0 and ic == 0:
                prog.op("gpsimd", lambda g: g.dma_start(out=wub_dbg[:, :], in_=d_wu[0][:, :]),
                        waits=[(sWG[0], wg_ready)], incs=[(sOU, 16)])
            for isub in range(4):
                it = ic * 4 + isub
                sp_ = isub % 2
                for bk, buf, scl, act, dst in (
                        (isub % 2, d_wg[par], d_scg, AF.Silu, d_sg[sp_]),
                        (2 + isub % 2, d_wu[par], d_scu, AF.Copy, d_uu[sp_])):
                    pw = pe_wait_bank(bk)
                    if isub == 0 and bk < 2:
                        pw += [(sWG[par], wg_ready), (sAC, xsum_ac)]
                    elif isub == 0:
                        pw += [(sWG[par], wg_ready)]
                    for ht in range(16):
                        prog.op("tensor",
                                (lambda b, bf_, h_, s_: lambda t: t.matmul(
                                    pb[b][:, 0:TCF],
                                    bf_[:, h_ * TCF + s_ * 128: h_ * TCF + s_ * 128 + 128],
                                    d_xaT[:, h_ * TCF:(h_ + 1) * TCF],
                                    start=(h_ == 0), stop=False))(bk, buf, ht, isub),
                                waits=pw if ht == 0 else [])
                    prog.op("tensor",
                            (lambda b: lambda t: t.matmul(
                                pb[b][:, 0:TCF], m128_row_bf[:, :], d_xsum[:, :],
                                start=False, stop=True))(bk),
                            incs=[(sPE, 1)])
                    mmv = sPE.n
                    rawdump = (debug and stage == "full" and tc == 0 and ic == 0
                               and isub == 0 and bk >= 2)
                    if rawdump:
                        prog.op("scalar",
                                (lambda b, a_, ds_: lambda s: s.activation(
                                    ds_[:, :], pb[b][:, 0:TCF], a_))(bk, act, dst),
                                waits=[(sPE, mmv), (sDV, last_mult_dv[sp_])], incs=[(sAC, 1)])
                    else:
                        prog.op("scalar",
                                (lambda b, sc_, a_, ds_, i_: lambda s: s.activation(
                                    ds_[:, :], pb[b][:, 0:TCF], a_,
                                    scale=sc_[:, i_: i_ + 1]))(bk, scl, act, dst, it),
                                waits=[(sPE, mmv), (sDV, last_mult_dv[sp_])], incs=[(sAC, 1)])
                    bank_free[bk] = [(sAC, sAC.n)]
                if debug and stage == "full" and tc == 0 and it == 0:
                    prog.op("gpsimd", (lambda s_: lambda g: g.dma_start(out=sg_dbg[:, :], in_=d_sg[s_][:, :]))(sp_),
                            waits=[(sAC, sAC.n)], incs=[(sOU, 16)])
                    prog.op("gpsimd", (lambda s_: lambda g: g.dma_start(out=uu_dbg[:, :], in_=d_uu[s_][:, :]))(sp_),
                            incs=[(sOU, 16)])
                    prog.op("gpsimd", lambda g: g.dma_start(out=xat_dbg[:, :], in_=d_xaT[:, :]),
                            incs=[(sOU, 16)])
                prog.op("vector",
                        (lambda i_, s_: lambda v: v.tensor_tensor(
                            d_h[:, i_ * TCF:(i_ + 1) * TCF], d_sg[s_][:, :], d_uu[s_][:, :],
                            ALU.mult))(it, sp_),
                        waits=[(sAC, sAC.n), (sOU, sOU.n) if (debug and stage == "full" and tc == 0 and it == 0) else (sAC, 0)], incs=[(sDV, 1)])
                last_mult_dv[sp_] = sDV.n
            wg_free[par] = [(sPE, sPE.n), (sAC, sAC.n)]
        h_dv = sDV.n
        gateup_pe_done = sPE.n
        if debug and stage == "full" and tc == 0:
            prog.op("gpsimd", lambda g: g.dma_start(out=h_dbg[:, :], in_=d_h[:, 0:8 * TCF]),
                    waits=[(sDV, h_dv)], incs=[(sOU, 16)])

        pw = pe_wait_bank(4) + [(sDV, h_dv)]
        for it in range(64):
            prog.op("tensor",
                    (lambda i_: lambda t: t.matmul(
                        pb[4][0:1, 0:TCF], ones_col_bf[:, :],
                        d_h[:, i_ * TCF:(i_ + 1) * TCF],
                        start=(i_ == 0), stop=(i_ == 63)))(it),
                    waits=pw if it == 0 else [],
                    incs=[(sPE, 1)] if it == 63 else [])
        mmv = sPE.n
        prog.op("scalar", lambda s: s.activation(d_hsum[:, :], pb[4][0:1, 0:TCF], AF.Copy),
                waits=[(sPE, mmv)], incs=[(sAC, 1)])
        hsum_ac = sAC.n
        bank_free[4] = [(sAC, hsum_ac)]

        for dh in range(2):
            pw_all = []
            for dto in range(8):
                pw_all += pe_wait_bank(dto)
            first_mm = [True] * 8
            for ic in range(16):
                par = ic % 2
                w = list(wd_free[par])
                if dh == 0 and ic < 2:
                    w += [(sPE, gateup_pe_done)]
                prog.op("gpsimd",
                        (lambda i_, d_, p_: lambda g: g.dma_start(
                            out=d_wd[p_].rearrange("p (itl d) -> p itl d", itl=4),
                            in_=wd8_d[i_ * TCF:(i_ + 1) * TCF,
                                      d_ * 1024:(d_ + 1) * 1024].rearrange(
                                "(itl p) d -> p itl d", p=128)))(ic, dh, par),
                        waits=w, incs=[(sWD[par], 16)])
                wd_free[par] = []
                wd_ready = sWD[par].n
                for itl in range(4):
                    it = ic * 4 + itl
                    for dto in range(8):
                        pw = []
                        if itl == 0 and dto == 0:
                            pw = [(sWD[par], wd_ready), (sDV, h_dv)] + (pw_all if ic == 0 else [])
                        prog.op("tensor",
                                (lambda p_, il, d_o, i_, st: lambda t: t.matmul(
                                    pb[d_o][:, 0:TCF],
                                    d_wd[p_][:, il * 1024 + d_o * 128: il * 1024 + (d_o + 1) * 128],
                                    d_h[:, i_ * TCF:(i_ + 1) * TCF],
                                    start=st, stop=False))(par, itl, dto, it, first_mm[dto]),
                                waits=pw,
                                incs=[(sPE, 1)] if (itl == 3 and dto == 7) else [])
                        first_mm[dto] = False
                wd_free[par] = [(sPE, sPE.n)]
            for dto in range(8):
                prog.op("tensor",
                        (lambda d_o: lambda t: t.matmul(
                            pb[d_o][:, 0:TCF], m128_row_bf[:, :], d_hsum[:, :],
                            start=False, stop=True))(dto),
                        waits=[(sAC, hsum_ac)] if dto == 0 else [],
                        incs=[(sPE, 1)] if dto == 7 else [])
            mmv = sPE.n
            for dto in range(8):
                dt = dh * 8 + dto
                prog.op("vector",
                        (lambda d_o, d_: lambda v: v.scalar_tensor_tensor(
                            d_r2T[:, d_ * TCF:(d_ + 1) * TCF], pb[d_o][:, 0:TCF],
                            d_scd[:, d_: d_ + 1], d_xaT[:, d_ * TCF:(d_ + 1) * TCF],
                            ALU.mult, ALU.add))(dto, dt),
                        waits=[(sPE, mmv)] if dto == 0 else [], incs=[(sDV, 1)])
                bank_free[dto] = [(sDV, sDV.n)]
        r2_dv = sDV.n
        if debug and stage == "full" and tc == 0:
            prog.op("gpsimd", lambda g: g.dma_start(out=r2_dbg[:, :], in_=d_r2T[:, :]),
                    waits=[(sDV, r2_dv)], incs=[(sOU, 16)])

        for ttl in range(TCF // 128):
            for dt in range(DT):
                bk = dt % 2
                pw = pe_wait_bank(bk) + ([(sDV, r2_dv)] if (ttl == 0 and dt < 2) else [])
                prog.op("tensor",
                        (lambda d_, tl, b: lambda t: t.transpose(
                            pb[b][:, 0:128], d_r2T[:, d_ * TCF + tl * 128: d_ * TCF + (tl + 1) * 128],
                            ident_f32[:, :]))(dt, ttl, bk),
                        waits=pw, incs=[(sPE, 1)])
                mmv = sPE.n
                aw = [(sPE, mmv)]
                if dt == 0:
                    aw.append((sAC, prev_sq_ac))
                prog.op("vector",
                        (lambda d_, b: lambda v: v.tensor_copy(
                            d_r2t[:, d_ * 128:(d_ + 1) * 128], pb[b][:, 0:128]))(dt, bk),
                        waits=aw, incs=[(sDV, 1)])
                bank_free[bk] = [(sDV, sDV.n)]
            tile_dv = sDV.n
            prog.op("vector", lambda v: v.reduce_sum(stats[:, 24:25], d_r2t[:, :], axis=AX.X),
                    incs=[(sDV, 1)])
            prog.op("vector", lambda v: v.tensor_scalar_mul(stats[:, 24:25], stats[:, 24:25], 1.0 / Dm),
                    waits=[(sDV, sDV.n)], incs=[(sDV, 1)])
            prog.op("scalar", lambda s: s.activation(d_sq[:, :], d_r2t[:, :], AF.Square,
                                                     accum_out=stats[:, 36:37]),
                    waits=[(sDV, tile_dv), (sOU, sOU.n)], incs=[(sAC, 1)])
            sq_ac = sAC.n
            prev_sq_ac = sq_ac
            prog.op("vector", lambda v: v.tensor_tensor(stats[:, 28:29], stats[:, 24:25], stats[:, 24:25], ALU.mult),
                    waits=[(sDV, sDV.n)], incs=[(sDV, 1)])
            prog.op("vector", lambda v: v.tensor_scalar_add(stats[:, 28:29], stats[:, 28:29], -EPS_LN),
                    waits=[(sDV, sDV.n)], incs=[(sDV, 1)])
            prog.op("vector", lambda v: v.scalar_tensor_tensor(
                stats[:, 32:33], stats[:, 36:37], 1.0 / Dm, stats[:, 28:29], ALU.mult, ALU.subtract),
                    waits=[(sAC, sq_ac), (sDV, sDV.n)], incs=[(sDV, 1)])
            var_dv = sDV.n
            prog.op("scalar", lambda s: s.activation(stats[:, 40:41], stats[:, 32:33], AF.Ln),
                    waits=[(sDV, var_dv)], incs=[(sAC, 1)])
            prog.op("scalar", lambda s: s.activation(stats[:, 44:45], stats[:, 40:41], AF.Exp, scale=misc_sb[:, 0:1]),
                    waits=[(sAC, sAC.n)], incs=[(sAC, 1)])
            rstd_ac = sAC.n
            prog.op("vector", lambda v: v.tensor_scalar(
                d_sq[:, :], d_r2t[:, :], stats[:, 24:25], stats[:, 44:45], ALU.subtract, ALU.mult),
                    waits=[(sAC, rstd_ac)], incs=[(sDV, 1)])
            out_dv = sDV.n
            trow = tb + ttl * 128
            prog.op("sync",
                    (lambda r: lambda s: s.dma_start(
                        out=out_d[r:r + 128, :], in_=d_sq[:, :]))(trow),
                    waits=[(sDV, out_dv)], incs=[(sOU, 16)])
        xa_free = [(sPE, sPE.n), (sDV, sDV.n), (sOU, sOU.n)]
        wg_free[0] = [(sPE, sPE.n), (sDV, sDV.n)]
        wg_free[1] = [(sPE, sPE.n), (sDV, sDV.n)]

    prog.op("sync", lambda s: s.nop(), waits=[(sOU, sOU.n), (sSP, sSP.n)])

    _finalize(nc, prog, guards)
    return nc


def _finalize(nc, prog, guards):
    blk = nc.Block()
    block = blk.__enter__()
    try:
        block.sync(lambda e: prog.replay("sync", e))
        block.tensor(lambda e: prog.replay("tensor", e))
        block.vector(lambda e: prog.replay("vector", e))
        block.scalar(lambda e: prog.replay("scalar", e))
        block.gpsimd(lambda e: prog.replay("gpsimd", e))
    finally:
        blk.__exit__(None, None, None)
        for g in reversed(guards):
            g.__exit__(None, None, None)


_NC_CACHE = {}


def _get_nc():
    key = (STAGE, DEBUG, SERIAL)
    if key not in _NC_CACHE:
        _NC_CACHE[key] = build_nc(*key)
    return _NC_CACHE[key]


def _u8_view(a_int32):
    a = np.ascontiguousarray(a_int32)
    return np.ascontiguousarray(a.view(np.uint8).reshape(a.shape + (4,))[..., 0])


def make_in_maps(inputs):
    x = np.asarray(inputs["L0_attn_hidden_states"], np.float32)
    wqT = np.ascontiguousarray(np.asarray(inputs["L0_attn_Wq"], np.float32).T)
    wkT = np.ascontiguousarray(np.asarray(inputs["L0_attn_Wk"], np.float32).T)
    wvT = np.ascontiguousarray(np.asarray(inputs["L0_attn_Wv"], np.float32).T)
    woT = np.ascontiguousarray(np.asarray(inputs["L0_attn_Wo"], np.float32).T)
    wfq = np.ascontiguousarray(np.asarray(inputs["L0_attn_Wfq"], np.float32).reshape(H * HD, FD))
    wfk = np.ascontiguousarray(np.asarray(inputs["L0_attn_Wfk"], np.float32).reshape(H * HD, FD))
    wg8 = _u8_view(np.asarray(inputs["L0_ffn_gate_w"]).T)
    wu8 = _u8_view(np.asarray(inputs["L0_ffn_up_w"]).T)
    wd8 = _u8_view(np.asarray(inputs["L0_ffn_down_w"]).T)
    scg = np.ascontiguousarray(np.asarray(inputs["L0_ffn_gate_scale"], np.float32).reshape(-1, 128).T)
    scu = np.ascontiguousarray(np.asarray(inputs["L0_ffn_up_scale"], np.float32).reshape(-1, 128).T)
    scd = np.ascontiguousarray(np.asarray(inputs["L0_ffn_down_scale"], np.float32).reshape(-1, 128).T)
    ident = np.eye(128, dtype=np.float32)
    maskA = np.triu(np.ones((128, 128), np.float32))
    onesrow = np.ones((1, 128), np.float32)
    m128row = np.full((1, 128), -128.0, np.float32)
    onescol = np.ones((128, 1), np.float32)
    misc = np.zeros((128, 2), np.float32)
    misc[:, 0] = -0.5
    eps65 = np.zeros((1, 65), np.float32)
    eps65[0, 64] = EPS_ATTN

    in_maps = []
    for c in range(NCORES):
        b, q = c // 4, c % 4
        t0 = q * T
        xs = x[b, t0:t0 + T, :]
        ccm = np.zeros((128, 8), np.float32)
        for j in range(NCORES):
            if j // 4 == b and (j % 4) < q:
                ccm[:, j] = 1.0
        in_maps.append({
            "xT": np.ascontiguousarray(xs.T),
            "xn": np.ascontiguousarray(xs),
            "wqT": wqT, "wkT": wkT, "wvT": wvT, "woT": woT,
            "wfq": wfq, "wfk": wfk,
            "wg8": wg8, "wu8": wu8, "wd8": wd8,
            "scg": scg, "scu": scu, "scd": scd,
            "identf": ident, "maskA": maskA,
            "onesrow": onesrow, "m128row": m128row, "onescol": onescol,
            "ccmask": ccm, "misc": misc, "eps65": eps65,
        })
    return in_maps


def kernel(**inputs):
    nc = _get_nc()
    in_maps = make_in_maps(inputs)
    res = run_bass_kernel_spmd(nc, in_maps, core_ids=list(range(NCORES)))
    out = np.empty((Bb, L, Dm), np.float32)
    for c in range(NCORES):
        b, q = c // 4, c % 4
        out[b, q * T:(q + 1) * T, :] = res.results[c]["out"]
    return out


# version 06
# speedup vs baseline: 1.0571x; 1.0571x over previous
import numpy as np

import concourse.bass as bass
import concourse.mybir as mybir
from concourse.bass_utils import run_bass_kernel_spmd

F32 = mybir.dt.float32
BF16 = mybir.dt.bfloat16
U8 = mybir.dt.uint8
AF = mybir.ActivationFunctionType
ALU = mybir.AluOpType
AX = mybir.AxisListType

NCORES = 8
Bb, L, Dm = 2, 8192, 2048
H, HD, FD = 32, 64, 64
Ii = 8192
T = 2048
HF = 1024
CH = 128
NCH = T // CH
DT = Dm // 128
OT = Dm // 128
TCF = 512
NTC = T // TCF
EPS_ATTN = 1e-12
EPS_LN = 1e-5

STAGE = "full"
DEBUG = False
SERIAL = False


class Sem:
    def __init__(self, h):
        self.h = h
        self.n = 0


class Prog:
    ENGINES = ("sync", "tensor", "vector", "scalar", "gpsimd")

    def __init__(self):
        self.ops = {e: [] for e in self.ENGINES}

    def op(self, eng, builder, waits=(), incs=()):
        waits = [(s, v) for (s, v) in waits if v > 0]
        incs = [(s, n) for (s, n) in incs]
        for s, n in incs:
            s.n += n
        self.ops[eng].append((builder, waits, incs))

    def replay(self, name, eng):
        for builder, waits, incs in self.ops[name]:
            for s, v in waits:
                eng.wait_ge(s.h, v)
            ins = builder(eng)
            if incs:
                assert ins is not None
                for s, n in incs:
                    ins = ins.then_inc(s.h, n)


class Arena:

    def __init__(self, tile, base=0):
        self.tile = tile
        self.off = base

    def alloc(self, parts, elems, dtype, align=512):
        esz = {F32: 4, BF16: 2, U8: 1}[dtype]
        self.off = (self.off + align - 1) // align * align
        nbytes = elems * esz
        ap = self.tile[0:parts, self.off:self.off + nbytes]
        if dtype != U8:
            ap = ap.bitcast(dtype)
        self.off += nbytes
        return ap


ARENA_B = 204544


def build_nc(stage=STAGE, debug=DEBUG, serial=SERIAL):
    nc = bass.Bass("TRN2", target_bir_lowering=False)

    def din(name, shape, dt):
        return nc.dram_tensor(name, shape, dt, kind="ExternalInput").ap()

    xT_d = din("xT", [Dm, T], F32)
    xn_d = din("xn", [T, Dm], F32)
    wqT_d = din("wqT", [Dm, Dm], F32)
    wkT_d = din("wkT", [Dm, Dm], F32)
    wvT_d = din("wvT", [Dm, Dm], F32)
    woT_d = din("woT", [Dm, Dm], F32)
    wfq_d = din("wfq", [H * HD, FD], F32)
    wfk_d = din("wfk", [H * HD, FD], F32)
    wg8_d = din("wg8", [Dm, Ii], U8)
    wu8_d = din("wu8", [Dm, Ii], U8)
    wd8_d = din("wd8", [Ii, Dm], U8)
    scg_d = din("scg", [128, Ii // 128], F32)
    scu_d = din("scu", [128, Ii // 128], F32)
    scd_d = din("scd", [128, Dm // 128], F32)
    ident_d = din("identf", [128, 128], F32)
    maskA_d = din("maskA", [128, 128], F32)
    onesrow_d = din("onesrow", [1, 128], F32)
    m128row_d = din("m128row", [1, 128], F32)
    onescol_d = din("onescol", [128, 1], F32)
    ccmask_d = din("ccmask", [128, 8], F32)
    misc_d = din("misc", [128, 2], F32)
    eps65_d = din("eps65", [1, 65], F32)

    out_d = nc.dram_tensor("out", [T, Dm], F32, kind="ExternalOutput").ap()

    ekT_d = nc.dram_tensor("ekT_s", [128, OT * T], BF16).ap()
    fqT_d = nc.dram_tensor("fqT_s", [128, OT * T], BF16).ap()
    ek_d = nc.dram_tensor("ek_s", [T, H * FD], BF16).ap()
    vp_d = nc.dram_tensor("vp_s", [T, H * 65], BF16).ap()
    xaT_d = nc.dram_tensor("xaT_s", [128, DT * T], BF16).ap()
    yT_dbg = nc.dram_tensor("yT_dbg", [128, NCH * OT * CH], F32, kind="ExternalOutput").ap() if (debug and stage == "c") else None
    h_dbg = nc.dram_tensor("h_dbg", [128, 8 * TCF], F32, kind="ExternalOutput").ap() if (debug and stage == "full") else None
    r2_dbg = nc.dram_tensor("r2_dbg", [128, DT * TCF], F32, kind="ExternalOutput").ap() if (debug and stage == "full") else None
    xs_dbg = nc.dram_tensor("xs_dbg", [1, TCF], F32, kind="ExternalOutput").ap() if (debug and stage == "full") else None
    sg_dbg = nc.dram_tensor("sg_dbg", [128, TCF], F32, kind="ExternalOutput").ap() if (debug and stage == "full") else None
    wub_dbg = nc.dram_tensor("wub_dbg", [128, 16 * TCF], F32, kind="ExternalOutput").ap() if (debug and stage == "full") else None
    ps_dbg = nc.dram_tensor("ps_dbg", [128, TCF], F32, kind="ExternalOutput").ap() if (debug and stage == "full") else None
    psg_sb = None
    uu_dbg = nc.dram_tensor("uu_dbg", [128, TCF], F32, kind="ExternalOutput").ap() if (debug and stage == "full") else None
    xat_dbg = nc.dram_tensor("xat_dbg", [128, DT * TCF], F32, kind="ExternalOutput").ap() if (debug and stage == "full") else None
    r1_dbg = nc.dram_tensor("r1_dbg", [128, Dm], F32, kind="ExternalOutput").ap() if debug else None
    s_dbg = nc.dram_tensor("s_dbg", [64, NCH * H * 65], F32, kind="ExternalOutput").ap() if debug else None
    am8_dbg = nc.dram_tensor("am8_dbg", [128, 4 * CH], F32, kind="ExternalOutput").ap() if debug else None
    sbf_dbg = nc.dram_tensor("sbf_dbg", [128, H * 65], F32, kind="ExternalOutput").ap() if debug else None
    fq8_dbg = nc.dram_tensor("fq8_dbg", [128, OT * CH], F32, kind="ExternalOutput").ap() if debug else None
    vp8_dbg = nc.dram_tensor("vp8_dbg", [128, H * 65], F32, kind="ExternalOutput").ap() if debug else None
    st_dbg = nc.dram_tensor("st_dbg", [128, 48], F32, kind="ExternalOutput").ap() if debug else None
    am_dbg = nc.dram_tensor("am_dbg", [128, 4 * CH], F32, kind="ExternalOutput").ap() if debug else None
    dn_dbg = nc.dram_tensor("dn_dbg", [1, 4 * CH], F32, kind="ExternalOutput").ap() if debug else None
    dbg_f32 = {}
    if debug:
        for nm, src_ap in (("ekT_f", ekT_d), ("fqT_f", fqT_d), ("ek_f", ek_d),
                           ("vp_f", vp_d), ("xaT_f", xaT_d)):
            dbg_f32[nm] = (nc.dram_tensor(nm, list(src_ap.shape), F32,
                                          kind="ExternalOutput").ap(), src_ap)
    tc_dbg = nc.dram_tensor("tc_dbg", [64, H * 65], F32, kind="ExternalOutput").ap() if debug else None
    s0_dbg = nc.dram_tensor("s0_dbg", [64, H * 65], F32, kind="ExternalOutput").ap() if debug else None

    ag_in = nc.dram_tensor("ag_in", [64, H * 65], F32).ap()
    ag_out = nc.dram_tensor("ag_out", [NCORES * 64, H * 65], F32, addr_space="Shared").ap()

    prog = Prog()
    guards = []

    def enter(g):
        guards.append(g)
        return g.__enter__()

    arena_t = enter(nc.sbuf_tensor("arena", [128, ARENA_B], U8))
    pb = [enter(nc.psum_tensor(f"pb{k}", [128, 512], F32)) for k in range(8)]

    def S(name):
        return Sem(enter(nc.semaphore(name)))

    sPE, sDV, sAC = S("sPE"), S("sDV"), S("sAC")
    sXT = S("sXT")
    sWT = [S("sWT0"), S("sWT1")]
    sWV = [S("sWV0"), S("sWV1")]
    sWF = S("sWF")
    sXN = S("sXN")
    sSP = S("sSP")
    sCH = [S("sCH0"), S("sCH1")]
    sWO = S("sWO")
    sCC = S("sCC")
    sAG = S("sAG")
    sWG = [S("sWG0"), S("sWG1")]
    sWD = [S("sWD0"), S("sWD1")]
    sXA = S("sXA")
    sOU = S("sOU")

    def barrier():
        prog.op("tensor", lambda t: t.nop(), incs=[(sPE, 1)])
        prog.op("vector", lambda v: v.nop(), incs=[(sDV, 1)])
        prog.op("scalar", lambda s: s.nop(), incs=[(sAC, 1)])
        w = [(sPE, sPE.n), (sDV, sDV.n), (sAC, sAC.n), (sOU, sOU.n), (sSP, sSP.n)]
        for e in ("tensor", "vector", "scalar", "sync", "gpsimd"):
            prog.op(e, lambda x: x.nop(), waits=list(w))

    A = Arena(arena_t[:, :], 0)
    ident_bf = A.alloc(128, 128, BF16)
    ident_f32 = A.alloc(128, 128, F32)
    mask_bf = A.alloc(128, 128, BF16)
    ones_row_bf = A.alloc(1, 128, BF16)
    ones_row_f32 = A.alloc(1, 128, F32)
    ones_hi_t = A.alloc(128, 64, F32)
    ones_row_f32_hi = ones_hi_t[64:65, :]
    m128_row_bf = A.alloc(1, 128, BF16)
    ones_col_bf = A.alloc(128, 1, BF16)
    ccmask_sb = A.alloc(128, 8, F32)
    misc_sb = A.alloc(128, 2, F32)
    eps65_bf = A.alloc(1, 65, BF16)
    stats = A.alloc(128, 48, F32)
    S_sb = A.alloc(64, H * 65, F32)
    S_bf = A.alloc(128, H * 65, BF16)
    Tc_sb = A.alloc(64, H * 65, F32)
    d_scg = A.alloc(128, 64, F32)
    d_scu = A.alloc(128, 64, F32)
    d_scd = A.alloc(128, 16, F32)
    pers_end = A.off

    A.off = pers_end
    a_xT = A.alloc(128, DT * HF, BF16)
    a_kqT = A.alloc(128, OT * HF, BF16)
    a_ekT_st = A.alloc(128, 2 * HF, BF16)
    a_fqT_st = A.alloc(128, 2 * HF, BF16)
    a_ek = A.alloc(128, 8 * H * FD, BF16)
    a_vp = A.alloc(128, 8 * H * 65, BF16)
    a_wT = A.alloc(128, 2 * DT * 128, BF16)
    a_wV0 = A.alloc(128, DT * 512, BF16)
    a_wfq = A.alloc(128, H * FD, BF16)
    a_wfk = A.alloc(128, H * FD, BF16)
    a_sk = A.alloc(128, 8 * H, F32)
    a_rsk = A.alloc(128, 8 * H, F32)
    assert A.off <= ARENA_B, A.off
    a_wV1 = a_kqT[:, 0:DT * 512]

    A.off = pers_end
    c_woT = A.alloc(128, OT * Dm, BF16)
    c_ekT = [A.alloc(128, OT * CH, BF16) for _ in range(2)]
    c_fqT = [A.alloc(128, OT * CH, BF16) for _ in range(2)]
    c_ek = [A.alloc(128, H * FD, BF16) for _ in range(2)]
    c_vp = [A.alloc(128, H * 65, BF16) for _ in range(2)]
    c_xn = [A.alloc(128, Dm, F32) for _ in range(2)]
    c_am = A.alloc(128, H * CH, BF16)
    c_dnrc = A.alloc(128, 4096, F32)
    c_rcb = c_dnrc[0:64, 0:2048]
    c_dn = c_dnrc[64:65, 0:2048]
    c_rc = c_dnrc[64:65, 2048:4096]
    c_yn = A.alloc(128, 8 * CH, BF16)
    c_yT = A.alloc(128, OT * CH, BF16)
    c_r1 = A.alloc(128, Dm, F32)
    c_sq = A.alloc(128, Dm, BF16)
    c_xa = A.alloc(128, Dm, BF16)
    c_xaT = A.alloc(128, DT * CH, BF16)
    assert A.off <= ARENA_B, A.off
    b_rk = [c_r1[:, 0:1040], c_xn[0][:, 0:1040]]
    b_acc = c_xn[1][:, 0:1040]

    A.off = pers_end
    off_xa = A.off
    d_xaT = A.alloc(128, DT * TCF, BF16)
    end_xa = A.off
    A.off = off_xa
    d_r2t = A.alloc(128, Dm, F32)
    d_sq = A.alloc(128, Dm, F32)
    assert A.off == end_xa
    d_wg = [A.alloc(128, 16 * TCF, BF16) for _ in range(2)]
    off_wu = A.off
    d_wu = [A.alloc(128, 16 * TCF, BF16) for _ in range(2)]
    end_wu = A.off
    A.off = off_wu
    d_r2T = A.alloc(128, DT * TCF, F32)
    assert A.off == end_wu
    d_h = A.alloc(128, 64 * TCF, BF16)
    d_sg = [A.alloc(128, TCF, BF16) for _ in range(2)]
    d_uu = [A.alloc(128, TCF, BF16) for _ in range(2)]
    d_xsum = A.alloc(1, TCF, BF16)
    d_hsum = A.alloc(1, TCF, BF16)
    d_dbgst = A.alloc(128, TCF, F32) if debug else None
    assert A.off <= ARENA_B, A.off
    d_wd = [d_wg[0][:, 0:4096], d_wg[1][:, 0:4096]]

    def ld(dst, src):
        return lambda g: g.dma_start(out=dst, in_=src)

    for dst, src in (
            (ident_bf[:, :], ident_d[:, :]),
            (ident_f32[:, :], ident_d[:, :]),
            (mask_bf[:, :], maskA_d[:, :]),
            (ones_row_bf[:, :], onesrow_d[:, :]),
            (ones_row_f32[:, :], onesrow_d[:, :]),
            (ones_row_f32_hi[:, :], onesrow_d[:, 0:64]),
            (m128_row_bf[:, :], m128row_d[:, :]),
            (ones_col_bf[:, :], onescol_d[:, :]),
            (ccmask_sb[:, :], ccmask_d[:, :]),
            (misc_sb[:, :], misc_d[:, :]),
            (eps65_bf[:, :], eps65_d[:, :]),
            (a_wfq[0:64, :].rearrange("p (h f) -> p h f", h=H),
             wfq_d.rearrange("(h p) f -> p h f", p=64)),
            (a_wfq[64:128, :].rearrange("p (h f) -> p h f", h=H),
             wfq_d.rearrange("(h p) f -> p h f", p=64)),
            (a_wfk[0:64, :].rearrange("p (h f) -> p h f", h=H),
             wfk_d.rearrange("(h p) f -> p h f", p=64)),
            (a_wfk[64:128, :].rearrange("p (h f) -> p h f", h=H),
             wfk_d.rearrange("(h p) f -> p h f", p=64)),
            (d_scg[:, :], scg_d[:, :]),
            (d_scu[:, :], scu_d[:, :]),
            (d_scd[:, :], scd_d[:, :]),
    ):
        prog.op("gpsimd", ld(dst, src), incs=[(sWF, 16)])
    WF_ALL = sWF.n

    prog.op("vector", lambda v: v.memset(Tc_sb[:, :], 0.0), incs=[(sDV, 1)])

    bank_free = [[] for _ in range(8)]

    def pe_wait_bank(k):
        w = bank_free[k]
        bank_free[k] = []
        return w

    wT_free = [[], []]
    wV_free = [[], []]
    ekst_free = [[], []]
    fqst_free = [[], []]
    prev_half = {"sp": 0, "dv": 0}

    def half_A(hf):
        t0 = hf * HF
        xw = [(sPE, sPE.n)] if hf else []
        for j in range(4):
            src = xT_d[:, t0:t0 + HF].rearrange("(dt p) t -> p dt t", p=128)[:, j * 4:(j + 1) * 4, :]
            dst = a_xT.rearrange("p (dt t) -> p dt t", dt=DT)[:, j * 4:(j + 1) * 4, :]
            prog.op("gpsimd", ld(dst, src), waits=xw if j == 0 else [], incs=[(sXT, 16)])
        xt_ready = sXT.n

        for which, w_dram in (("k", wkT_d), ("q", wqT_d)):
            for ot in range(OT):
                par = ot % 2
                src = w_dram[:, ot * 128:(ot + 1) * 128].rearrange("(dt p) o -> p dt o", p=128)
                dst = a_wT[:, par * DT * 128:(par + 1) * DT * 128].rearrange("p (dt o) -> p dt o", dt=DT)
                prog.op("gpsimd", ld(dst, src), waits=wT_free[par], incs=[(sWT[par], 16)])
                wT_free[par] = []
                w_ready = sWT[par].n
                for nt in range(HF // 512):
                    bk = (nt + ot) % 2
                    pw = pe_wait_bank(bk) + ([(sWT[par], w_ready)] if nt == 0 else [])
                    if nt == 0 and ot == 0:
                        pw.append((sXT, xt_ready))
                    for dt in range(DT):
                        lhs = a_wT[:, par * DT * 128 + dt * 128: par * DT * 128 + (dt + 1) * 128]
                        rhs = a_xT[:, dt * HF + nt * 512: dt * HF + (nt + 1) * 512]
                        prog.op("tensor",
                                (lambda l, r, b, s0, s1: lambda t: t.matmul(
                                    pb[b][:, :], l, r, start=s0, stop=s1))(
                                    lhs, rhs, bk, dt == 0, dt == DT - 1),
                                waits=pw if dt == 0 else [],
                                incs=[(sPE, 1)] if dt == DT - 1 else [])
                    mmv = sPE.n
                    dst_kq = a_kqT[:, ot * HF + nt * 512: ot * HF + (nt + 1) * 512]
                    prog.op("vector",
                            (lambda d, b: lambda v: v.tensor_copy(d, pb[b][:, :]))(dst_kq, bk),
                            waits=[(sPE, mmv)], incs=[(sDV, 1)])
                    bank_free[bk] = [(sDV, sDV.n)]
                wT_free[par] = [(sPE, sPE.n)]

            for h in range(H):
                ot, pbase = h // 2, (h % 2) * 64
                wf = a_wfk if which == "k" else a_wfq
                stt = a_ekT_st if which == "k" else a_fqT_st
                stf = ekst_free if which == "k" else fqst_free
                for nt in range(HF // 512):
                    bk = 2 + (h + nt) % 2
                    pw = pe_wait_bank(bk)
                    if h == 0 and nt == 0:
                        pw += [(sWF, WF_ALL), (sDV, sDV.n)]
                    prog.op("tensor",
                            (lambda hh, oo, pp, nn, b, w: lambda t: t.matmul(
                                pb[b][0:64, 0:512],
                                w[pp:pp + 64, hh * FD:(hh + 1) * FD],
                                a_kqT[pp:pp + 64, oo * HF + nn * 512: oo * HF + (nn + 1) * 512],
                                start=True, stop=True))(h, ot, pbase, nt, bk, wf),
                            waits=pw, incs=[(sPE, 1)])
                    mmv = sPE.n
                    aw = [(sPE, mmv)]
                    if pbase == 0 and nt == 0:
                        aw += stf[ot % 2]
                        stf[ot % 2] = []
                    dstst = stt[pbase:pbase + 64, (ot % 2) * HF + nt * 512:(ot % 2) * HF + (nt + 1) * 512]
                    prog.op("scalar",
                            (lambda d, b: lambda s: s.activation(d, pb[b][0:64, 0:512], AF.Exp))(dstst, bk),
                            waits=aw, incs=[(sAC, 1)])
                    bank_free[bk] = [(sAC, sAC.n)]
                if which == "k":
                    for tt in range(HF // CH):
                        bk = 4 + (h + tt) % 2
                        pw = pe_wait_bank(bk)
                        prog.op("tensor",
                                (lambda hh, oo, pp, t_, b: lambda t: t.matmul(
                                    pb[b][:, 0:64],
                                    a_kqT[pp:pp + 64, oo * HF + t_ * CH: oo * HF + (t_ + 1) * CH],
                                    a_wfk[pp:pp + 64, hh * FD:(hh + 1) * FD],
                                    start=True, stop=True))(h, ot, pbase, tt, bk),
                                waits=pw, incs=[(sPE, 1)])
                        mmv = sPE.n
                        aw = [(sPE, mmv)]
                        if h == 0 and tt == 0 and hf == 1:
                            aw.append((sSP, prev_half["sp"]))
                        prog.op("scalar",
                                (lambda hh, t_, b: lambda s: s.activation(
                                    a_ek[:, t_ * H * FD + hh * FD: t_ * H * FD + (hh + 1) * FD],
                                    pb[b][:, 0:64], AF.Exp,
                                    accum_out=a_sk[:, t_ * H + hh: t_ * H + hh + 1]))(h, tt, bk),
                                waits=aw, incs=[(sAC, 1)])
                        bank_free[bk] = [(sAC, sAC.n)]
                if h % 2 == 1:
                    dd = ekT_d if which == "k" else fqT_d
                    acv = sAC.n
                    prog.op("scalar",
                            (lambda o, s_, d_: lambda s: s.dma_start(
                                out=d_[:, o * T + t0: o * T + t0 + HF],
                                in_=s_[:, (o % 2) * HF:(o % 2 + 1) * HF]))(ot, stt, dd),
                            waits=[(sAC, acv)], incs=[(sSP, 16)])
                    stf[ot % 2] = [(sSP, sSP.n)]

            if which == "k":
                kfeat_pe = sPE.n
                prog.op("vector",
                        lambda v: v.reciprocal(a_rsk[:, 0:8 * H], a_sk[:, 0:8 * H]),
                        waits=[(sAC, sAC.n)], incs=[(sDV, 1)])
                rsk_ready = sDV.n

                for oc in range(4):
                    par = oc % 2
                    wv_buf = a_wV0 if par == 0 else a_wV1
                    src = wvT_d[:, oc * 512:(oc + 1) * 512].rearrange("(dt p) o -> p dt o", p=128)
                    dst = wv_buf.rearrange("p (dt o) -> p dt o", dt=DT)
                    w = list(wV_free[par])
                    if par == 1:
                        w.append((sPE, kfeat_pe))
                    prog.op("gpsimd", ld(dst, src), waits=w, incs=[(sWV[par], 16)])
                    wV_free[par] = []
                    wv_ready = sWV[par].n
                    for tt in range(HF // CH):
                        bk = (oc + tt) % 2
                        pw = pe_wait_bank(bk) + ([(sWV[par], wv_ready)] if tt == 0 else [])
                        for dt in range(DT):
                            lhs = a_xT[:, dt * HF + tt * CH: dt * HF + (tt + 1) * CH]
                            rhs = wv_buf[:, dt * 512:(dt + 1) * 512]
                            prog.op("tensor",
                                    (lambda l, r, b, s0, s1: lambda t: t.matmul(
                                        pb[b][:, :],
                                        l, r, start=s0, stop=s1))(lhs, rhs, bk, dt == 0, dt == DT - 1),
                                    waits=pw if dt == 0 else [],
                                    incs=[(sPE, 1)] if dt == DT - 1 else [])
                        mmv = sPE.n
                        for hh in range(8):
                            h = oc * 8 + hh
                            aw = []
                            if hh == 0:
                                aw = [(sPE, mmv), (sDV, rsk_ready)]
                                if tt == 0 and oc == 0 and hf == 1:
                                    aw.append((sSP, prev_half["sp"]))
                            prog.op("scalar",
                                    (lambda h_, t_, b, hh_: lambda s: s.activation(
                                        a_vp[:, t_ * H * 65 + h_ * 65: t_ * H * 65 + h_ * 65 + 64],
                                        pb[b][:, hh_ * 64:(hh_ + 1) * 64], AF.Copy,
                                        scale=a_rsk[:, t_ * H + h_: t_ * H + h_ + 1]))(h, tt, bk, hh),
                                    waits=aw, incs=[(sAC, 1)] if hh == 7 else [])
                        bank_free[bk] = [(sAC, sAC.n)]
                    wV_free[par] = [(sPE, sPE.n)]
                for tt in range(HF // CH):
                    prog.op("vector",
                            (lambda t_: lambda v: v.tensor_copy(
                                a_vp[:, t_ * H * 65 + 64: t_ * H * 65 + 64 + (H - 1) * 65 + 1: 65],
                                a_rsk[:, t_ * H:(t_ + 1) * H]))(tt),
                            waits=[(sAC, sAC.n)] if tt == 0 else [], incs=[(sDV, 1)])
                vp_done_dv = sDV.n

                for tt in range(HF // CH):
                    for hg in range(8):
                        bk = 6 + (tt * 8 + hg) % 2
                        pw = pe_wait_bank(bk)
                        if tt == 0 and hg == 0:
                            pw += [(sDV, vp_done_dv)]
                        for j in range(4):
                            h = hg * 4 + j
                            prog.op("tensor",
                                    (lambda h_, t_, b, j_: lambda t: t.matmul(
                                        pb[b][0:64, j_ * 65:(j_ + 1) * 65],
                                        a_ek[:, t_ * H * FD + h_ * FD: t_ * H * FD + (h_ + 1) * FD],
                                        a_vp[:, t_ * H * 65 + h_ * 65: t_ * H * 65 + (h_ + 1) * 65],
                                        start=True, stop=True))(h, tt, bk, j),
                                    waits=pw if j == 0 else [],
                                    incs=[(sPE, 1)] if j == 3 else [])
                        mmv = sPE.n
                        aw = [(sPE, mmv)]
                        if tt == 0 and hg == 0 and hf == 0:
                            aw.append((sDV, 1))
                        prog.op("vector",
                                (lambda b, g_: lambda v: v.tensor_tensor(
                                    Tc_sb[:, g_ * 260:(g_ + 1) * 260],
                                    pb[b][0:64, 0:260],
                                    Tc_sb[:, g_ * 260:(g_ + 1) * 260], ALU.add))(bk, hg),
                                waits=aw, incs=[(sDV, 1)])
                        bank_free[bk] = [(sDV, sDV.n)]

                dvv = sDV.n
                for tt in range(HF // CH):
                    prog.op("scalar",
                            (lambda t_: lambda s: s.dma_start(
                                out=ek_d[t0 + t_ * CH: t0 + (t_ + 1) * CH, :],
                                in_=a_ek[:, t_ * H * FD:(t_ + 1) * H * FD]))(tt),
                            waits=[(sDV, dvv)] if tt == 0 else [], incs=[(sSP, 16)])
                    prog.op("scalar",
                            (lambda t_: lambda s: s.dma_start(
                                out=vp_d[t0 + t_ * CH: t0 + (t_ + 1) * CH, :],
                                in_=a_vp[:, t_ * H * 65:(t_ + 1) * H * 65]))(tt),
                            incs=[(sSP, 16)])

    half_A(0)
    prev_half = {"sp": sSP.n, "dv": sDV.n}
    half_A(1)
    a_done_dv = sDV.n
    a_done_sp = sSP.n
    a_done_ac = sAC.n
    a_done_pe = sPE.n

    if stage == "a":
        if debug:
            prog.op("sync", lambda s: s.dma_start(out=tc_dbg[:, :], in_=Tc_sb[:, :]),
                    waits=[(sDV, a_done_dv)], incs=[(sOU, 16)])
            for nm in ("ekT_f", "fqT_f", "ek_f", "vp_f"):
                o, i = dbg_f32[nm]
                prog.op("gpsimd", (lambda o_, i_: lambda g: g.dma_start(out=o_[:, :], in_=i_[:, :]))(o, i),
                        waits=[(sSP, a_done_sp)], incs=[(sOU, 16)])
        prog.op("sync", lambda s: s.nop(),
                waits=[(sOU, sOU.n), (sSP, a_done_sp), (sDV, a_done_dv), (sAC, a_done_ac)])
        _finalize(nc, prog, guards)
        return nc

    prog.op("sync", lambda s: s.dma_start(out=ag_in[:, :], in_=Tc_sb[:, :]),
            waits=[(sDV, a_done_dv)], incs=[(sXN, 16)])
    prog.op("gpsimd",
            lambda g: g.collective_compute(
                "AllGather", ALU.bypass, replica_groups=[list(range(NCORES))],
                ins=[ag_in[:, :].opt()], outs=[ag_out[:, :].opt()]),
            waits=[(sXN, sXN.n)], incs=[(sCC, 1)])
    prog.op("vector", lambda v: v.memset(b_acc[:, :], 0.0),
            waits=[(sWF, WF_ALL), (sSP, a_done_sp)], incs=[(sDV, 1)])
    rd_free = [[], []]
    for rk in range(NCORES):
        par = rk % 2
        src = ag_out[rk * 64:(rk + 1) * 64, :].rearrange("r (two c) -> (r two) c", two=2)
        w = [(sCC, 1), (sSP, a_done_sp)] if rk == 0 else []
        w += rd_free[par]
        rd_free[par] = []
        prog.op("sync",
                (lambda s_, d_: lambda s: s.dma_start(out=d_, in_=s_))(src, b_rk[par][:, :]),
                waits=w, incs=[(sAG, 16)])
        agv = sAG.n
        prog.op("vector",
                (lambda p_, r_: lambda v: v.scalar_tensor_tensor(
                    b_acc[:, :], b_rk[p_][:, :],
                    ccmask_sb[:, r_:r_ + 1],
                    b_acc[:, :], ALU.mult, ALU.add))(par, rk),
                waits=[(sAG, agv), (sDV, sDV.n)], incs=[(sDV, 1)])
        rd_free[par] = [(sDV, sDV.n)]
    prog.op("sync", lambda s: s.dma_start(
        out=ag_in[:, :].rearrange("r (two c) -> (r two) c", two=2), in_=b_acc[:, :]),
            waits=[(sDV, sDV.n)], incs=[(sXN, 16)])
    prog.op("sync", lambda s: s.dma_start(out=S_sb[:, :], in_=ag_in[:, :]),
            waits=[(sXN, sXN.n)], incs=[(sXN, 16)])
    sinit_xn = sXN.n
    prog.op("scalar", lambda s: s.activation(S_bf[0:64, :], S_sb[:, :], AF.Copy),
            waits=[(sXN, sinit_xn)], incs=[(sAC, 1)])
    prog.op("scalar", lambda s: s.activation(S_bf[64:128, :], S_sb[:, :], AF.Copy),
            incs=[(sAC, 1)])
    sbf_ready = sAC.n

    if debug:
        prog.op("sync", lambda s: s.dma_start(out=s0_dbg[:, :], in_=S_sb[:, :]),
                waits=[(sXN, sinit_xn)], incs=[(sOU, 16)])
        prog.op("sync", lambda s: s.dma_start(out=tc_dbg[:, :], in_=Tc_sb[:, :]),
                incs=[(sOU, 16)])

    for ot in range(OT):
        prog.op("gpsimd",
                (lambda o: lambda g: g.dma_start(
                    out=c_woT[:, o * Dm:(o + 1) * Dm],
                    in_=woT_d[o * 128:(o + 1) * 128, :]))(ot),
                waits=[(sPE, a_done_pe), (sAC, a_done_ac)] if ot == 0 else [],
                incs=[(sWO, 16)])
    wo_ready = sWO.n

    ch_free = [[], []]

    def load_chunk(ch):
        par = ch % 2
        c0 = ch * CH
        w = list(ch_free[par]) + ([(sSP, a_done_sp)] if ch < 2 else [])
        ch_free[par] = []
        if ch < 2:
            w.append((sDV, sDV.n))
        prog.op("sync",
                (lambda p, c: lambda s: s.dma_start(
                    out=c_ekT[p].rearrange("p (o t) -> p o t", o=OT),
                    in_=ekT_d.rearrange("p (o t) -> p o t", o=OT)[:, :, c:c + CH]))(par, c0),
                waits=w, incs=[(sCH[par], 16)])
        prog.op("sync",
                (lambda p, c: lambda s: s.dma_start(
                    out=c_fqT[p].rearrange("p (o t) -> p o t", o=OT),
                    in_=fqT_d.rearrange("p (o t) -> p o t", o=OT)[:, :, c:c + CH]))(par, c0),
                incs=[(sCH[par], 16)])
        prog.op("sync",
                (lambda p, c: lambda s: s.dma_start(
                    out=c_ek[p][:, :], in_=ek_d[c:c + CH, :]))(par, c0),
                incs=[(sCH[par], 16)])
        prog.op("sync",
                (lambda p, c: lambda s: s.dma_start(
                    out=c_vp[p][:, :], in_=vp_d[c:c + CH, :]))(par, c0),
                incs=[(sCH[par], 16)])
        prog.op("sync",
                (lambda p, c: lambda s: s.dma_start(
                    out=c_xn[p][:, :], in_=xn_d[c:c + CH, :]))(par, c0),
                incs=[(sCH[par], 16)])

    load_chunk(0)
    load_chunk(1)

    sbf_wait = sbf_ready
    last = {"rc_dv": [0, 0], "mult_dv": [0, 0], "rc_pe_war": [0, 0]}
    prev_wo_pe = 0
    prev_xat_sp = 0
    prev_tr_pe = 0
    prev_sq_ac = 0

    for ch in range(NCH):
        par = ch % 2
        chv = sCH[par].n
        if debug and stage == "c" and ch == 8:
            prog.op("gpsimd", (lambda p: lambda g: g.dma_start(out=sbf_dbg[:, :], in_=S_bf[:, :]))(par),
                    waits=[(sAC, sbf_wait)], incs=[(sOU, 16)])
            prog.op("gpsimd", (lambda p: lambda g: g.dma_start(out=fq8_dbg[:, :], in_=c_fqT[p][:, :]))(par),
                    waits=[(sCH[par], chv)], incs=[(sOU, 16)])
            prog.op("gpsimd", (lambda p: lambda g: g.dma_start(out=vp8_dbg[:, :], in_=c_vp[p][:, :]))(par),
                    incs=[(sOU, 16)])
        for q in range(4):
            pq = q % 2
            pw = []
            for b in range(8):
                pw += pe_wait_bank(b)
            if q == 0:
                pw += [(sCH[par], chv)]
                if ch == 0:
                    pw += [(sWF, WF_ALL)]
            for j in range(8):
                h = q * 8 + j
                ot, pbase = h // 2, (h % 2) * 64
                prog.op("tensor",
                        (lambda p, o, pbs, b: lambda t: t.matmul(
                            pb[b][:, 0:CH],
                            c_ekT[p][pbs:pbs + 64, o * CH:(o + 1) * CH],
                            c_fqT[p][pbs:pbs + 64, o * CH:(o + 1) * CH],
                            start=True, stop=True))(par, ot, pbase, j),
                        waits=pw if j == 0 else [], incs=[(sPE, 1)] if j == 7 else [])
            at_pe = sPE.n
            for j in range(8):
                h = q * 8 + j
                prog.op("vector",
                        (lambda b, hh: lambda v: v.tensor_tensor(
                            c_am[:, hh * CH:(hh + 1) * CH],
                            pb[b][:, 0:CH], mask_bf[:, :], ALU.mult))(j, h),
                        waits=[(sPE, at_pe)] if j == 0 else [], incs=[(sDV, 1)] if j == 7 else [])
            am_dv = sDV.n
            pw = [(sDV, am_dv)]
            if q == 0:
                pw += [(sAC, sbf_wait)]
                if ch == 0:
                    pw += [(sXN, sinit_xn)]
            for j in range(8):
                h = q * 8 + j
                ot, pbase = h // 2, (h % 2) * 64
                prog.op("tensor",
                        (lambda p, hh, b: lambda t: t.matmul(
                            pb[b][0:65, 0:CH],
                            c_vp[p][:, hh * 65:(hh + 1) * 65],
                            c_am[:, hh * CH:(hh + 1) * CH], start=True, stop=False))(par, h, j),
                        waits=pw if j == 0 else [])
                prog.op("tensor",
                        (lambda p, hh, o, pbs, b: lambda t: t.matmul(
                            pb[b][0:65, 0:CH],
                            S_bf[pbs:pbs + 64, hh * 65:(hh + 1) * 65],
                            c_fqT[p][pbs:pbs + 64, o * CH:(o + 1) * CH],
                            start=False, stop=False))(par, h, ot, pbase, j))
                prog.op("tensor",
                        (lambda b: lambda t: t.matmul(
                            pb[b][0:65, 0:CH], eps65_bf[:, :], ones_row_bf[:, 0:CH],
                            start=False, stop=True))(j),
                        incs=[(sPE, 1)] if j == 7 else [])
            yn_pe = sPE.n
            for j in range(8):
                prog.op("vector",
                        (lambda b, jj: lambda v: v.tensor_copy(
                            c_yn[0:64, jj * CH:(jj + 1) * CH], pb[b][0:64, 0:CH]))(j, j),
                        waits=[(sPE, yn_pe), (sDV, sDV.n)] if j == 0 else [])
                prog.op("vector",
                        (lambda b, jj, p_: lambda v: v.reciprocal(
                            c_rc[:, p_ * 8 * CH + jj * CH: p_ * 8 * CH + (jj + 1) * CH],
                            pb[b][64:65, 0:CH]))(j, j, pq),
                        waits=[(sPE, last["rc_pe_war"][pq])] if j == 0 else [],
                        incs=[(sDV, 1)] if j == 7 else [])
            ynsb_dv = sDV.n
            rc_dv = sDV.n
            last["rc_dv"][pq] = rc_dv
            pw = [(sDV, ynsb_dv), (sDV, rc_dv)]
            for j in range(8):
                prog.op("tensor",
                        (lambda b, jj, p_: lambda t: t.matmul(
                            pb[b][0:64, 0:CH], ones_row_f32_hi[:, :],
                            c_rc[:, p_ * 8 * CH + jj * CH: p_ * 8 * CH + (jj + 1) * CH],
                            start=True, stop=True))(j, j, pq),
                        waits=pw if j == 0 else [], incs=[(sPE, 1)] if j == 7 else [])
            bc_pe = sPE.n
            last["rc_pe_war"][pq] = bc_pe
            aw = [(sPE, bc_pe), (sDV, last["mult_dv"][pq])]
            for j in range(8):
                prog.op("scalar",
                        (lambda b, jj, p_: lambda s: s.activation(
                            c_rcb[:, p_ * 8 * CH + jj * CH: p_ * 8 * CH + (jj + 1) * CH],
                            pb[b][0:64, 0:CH], AF.Copy))(j, j, pq),
                        waits=aw if j == 0 else [], incs=[(sAC, 1)] if j == 7 else [])
            rcb_ac = sAC.n
            for b in range(8):
                bank_free[b] = [(sAC, rcb_ac)]
            aw = [(sAC, rcb_ac)]
            if q == 0:
                aw.append((sPE, prev_wo_pe))
            for j in range(8):
                h = q * 8 + j
                ot, pbase = h // 2, (h % 2) * 64
                prog.op("vector",
                        (lambda jj, p_, o, pbs: lambda v: v.tensor_tensor(
                            c_yT[pbs:pbs + 64, o * CH:(o + 1) * CH],
                            c_yn[0:64, jj * CH:(jj + 1) * CH],
                            c_rcb[:, p_ * 8 * CH + jj * CH: p_ * 8 * CH + (jj + 1) * CH],
                            ALU.mult))(j, pq, ot, pbase),
                        waits=aw if j == 0 else [], incs=[(sDV, 1)] if j == 7 else [])
            last["mult_dv"][pq] = sDV.n
        heads_dv = sDV.n
        if debug and stage == "c":
            prog.op("gpsimd",
                    (lambda c: lambda g: g.dma_start(
                        out=yT_dbg[:, c * OT * CH:(c + 1) * OT * CH], in_=c_yT[:, :]))(ch),
                    waits=[(sDV, heads_dv)], incs=[(sOU, 16)])

        for hg in range(8):
            bk = 6 + hg % 2
            pw = pe_wait_bank(bk)
            for j in range(4):
                h = hg * 4 + j
                prog.op("tensor",
                        (lambda p, h_, b, j_: lambda t: t.matmul(
                            pb[b][0:64, j_ * 65:(j_ + 1) * 65],
                            c_ek[p][:, h_ * FD:(h_ + 1) * FD],
                            c_vp[p][:, h_ * 65:(h_ + 1) * 65],
                            start=True, stop=True))(par, h, bk, j),
                        waits=pw if j == 0 else [],
                        incs=[(sPE, 1)] if j == 3 else [])
            mmv = sPE.n
            prog.op("vector",
                    (lambda b, g_: lambda v: v.tensor_tensor(
                        S_sb[:, g_ * 260:(g_ + 1) * 260],
                        pb[b][0:64, 0:260],
                        S_sb[:, g_ * 260:(g_ + 1) * 260], ALU.add))(bk, hg),
                    waits=[(sPE, mmv)], incs=[(sDV, 1)])
            bank_free[bk] = [(sDV, sDV.n)]
        states_pe = sPE.n
        if debug and stage == "c":
            prog.op("gpsimd",
                    (lambda c: lambda g: g.dma_start(
                        out=s_dbg[:, c * H * 65:(c + 1) * H * 65], in_=S_sb[:, :]))(ch),
                    waits=[(sDV, sDV.n)], incs=[(sOU, 16)])
        prog.op("scalar", lambda s: s.activation(S_bf[0:64, :], S_sb[:, :], AF.Copy),
                waits=[(sDV, sDV.n), (sPE, states_pe)], incs=[(sAC, 1)])
        prog.op("scalar", lambda s: s.activation(S_bf[64:128, :], S_sb[:, :], AF.Copy),
                incs=[(sAC, 1)])
        sbf_wait = sAC.n

        for ncx in range(4):
            bk = 4 + ncx % 2
            pw = pe_wait_bank(bk) + ([(sDV, heads_dv), (sWO, wo_ready)] if ncx == 0 else [])
            for ot in range(OT):
                prog.op("tensor",
                        (lambda o, n_, b: lambda t: t.matmul(
                            pb[b][:, :],
                            c_yT[:, o * CH:(o + 1) * CH],
                            c_woT[:, o * Dm + n_ * 512: o * Dm + (n_ + 1) * 512],
                            start=(o == 0), stop=(o == OT - 1)))(ot, ncx, bk),
                        waits=pw if ot == 0 else [],
                        incs=[(sPE, 1)] if ot == OT - 1 else [])
            mmv = sPE.n
            aw = [(sPE, mmv), (sCH[par], chv)]
            if ncx == 0:
                aw.append((sAC, prev_sq_ac))
            prog.op("vector",
                    (lambda p, n_, b: lambda v: v.tensor_tensor(
                        c_r1[:, n_ * 512:(n_ + 1) * 512], pb[b][:, :],
                        c_xn[p][:, n_ * 512:(n_ + 1) * 512], ALU.add))(par, ncx, bk),
                    waits=aw, incs=[(sDV, 1)])
            bank_free[bk] = [(sDV, sDV.n)]
        wo_pe = sPE.n
        prev_wo_pe = wo_pe
        r1_dv = sDV.n
        if debug and stage == "c" and ch == 0:
            prog.op("gpsimd", lambda g: g.dma_start(out=r1_dbg[:, :], in_=c_r1[:, :]),
                    waits=[(sDV, r1_dv)], incs=[(sOU, 16)])

        prog.op("vector", lambda v: v.reduce_sum(stats[:, 0:1], c_r1[:, :], axis=AX.X),
                incs=[(sDV, 1)])
        prog.op("vector", lambda v: v.tensor_scalar_mul(stats[:, 0:1], stats[:, 0:1], 1.0 / Dm),
                waits=[(sDV, sDV.n)], incs=[(sDV, 1)])
        prog.op("scalar", lambda s: s.activation(c_sq[:, :], c_r1[:, :], AF.Square,
                                                 accum_out=stats[:, 12:13]),
                waits=[(sDV, r1_dv)], incs=[(sAC, 1)])
        sq_ac = sAC.n
        prev_sq_ac = sq_ac
        prog.op("vector", lambda v: v.tensor_tensor(stats[:, 4:5], stats[:, 0:1], stats[:, 0:1], ALU.mult),
                waits=[(sDV, sDV.n)], incs=[(sDV, 1)])
        prog.op("vector", lambda v: v.tensor_scalar_add(stats[:, 4:5], stats[:, 4:5], -EPS_LN),
                waits=[(sDV, sDV.n)], incs=[(sDV, 1)])
        prog.op("vector", lambda v: v.scalar_tensor_tensor(
            stats[:, 8:9], stats[:, 12:13], 1.0 / Dm, stats[:, 4:5], ALU.mult, ALU.subtract),
                waits=[(sAC, sq_ac), (sDV, sDV.n)], incs=[(sDV, 1)])
        var_dv = sDV.n
        prog.op("scalar", lambda s: s.activation(stats[:, 16:17], stats[:, 8:9], AF.Ln),
                waits=[(sDV, var_dv)], incs=[(sAC, 1)])
        prog.op("scalar", lambda s: s.activation(stats[:, 20:21], stats[:, 16:17], AF.Exp, scale=misc_sb[:, 0:1]),
                waits=[(sAC, sAC.n)], incs=[(sAC, 1)])
        rstd_ac = sAC.n
        if debug and stage == "c" and ch == 0:
            prog.op("gpsimd", lambda g: g.dma_start(out=st_dbg[:, :], in_=stats[:, :]),
                    waits=[(sAC, rstd_ac)], incs=[(sOU, 16)])
        aw = [(sAC, rstd_ac), (sPE, prev_tr_pe)]
        prog.op("vector", lambda v: v.tensor_scalar(
            c_xa[:, :], c_r1[:, :], stats[:, 0:1], stats[:, 20:21], ALU.subtract, ALU.mult),
                waits=aw, incs=[(sDV, 1)])
        xa_dv = sDV.n

        for dt in range(DT):
            bk = 6 + dt % 2
            pw = pe_wait_bank(bk) + ([(sDV, xa_dv)] if dt < 2 else [])
            prog.op("tensor",
                    (lambda d, b: lambda t: t.transpose(
                        pb[b][:, 0:64].bitcast(BF16), c_xa[:, d * 128:(d + 1) * 128],
                        ident_bf[:, :]))(dt, bk),
                    waits=pw, incs=[(sPE, 1)])
            mmv = sPE.n
            aw = [(sPE, mmv)]
            if dt == 0:
                aw.append((sSP, prev_xat_sp))
            prog.op("vector",
                    (lambda d, b: lambda v: v.tensor_copy(
                        c_xaT[:, d * CH:(d + 1) * CH], pb[b][:, 0:64].bitcast(BF16)))(dt, bk),
                    waits=aw, incs=[(sDV, 1)])
            bank_free[bk] = [(sDV, sDV.n)]
        prev_tr_pe = sPE.n
        xat_dv = sDV.n
        prog.op("scalar",
                (lambda c: lambda s: s.dma_start(
                    out=xaT_d.rearrange("p (d t) -> p d t", d=DT)[:, :, c * CH:(c + 1) * CH],
                    in_=c_xaT.rearrange("p (d t) -> p d t", d=DT)))(ch),
                waits=[(sDV, xat_dv)], incs=[(sSP, 16)])
        prev_xat_sp = sSP.n

        if serial:
            barrier()
        ch_free[par] = [(sPE, states_pe), (sDV, r1_dv)]
        if ch + 2 < NCH:
            load_chunk(ch + 2)

    c_done_sp = sSP.n
    c_done_pe = sPE.n
    c_done_dv = sDV.n
    c_done_ac = sAC.n

    if stage == "c":
        if debug:
            for nm in ("xaT_f", "fqT_f", "ekT_f"):
                o, i = dbg_f32[nm]
                prog.op("gpsimd", (lambda o_, i_: lambda g: g.dma_start(out=o_[:, :], in_=i_[:, :]))(o, i),
                        waits=[(sSP, c_done_sp)], incs=[(sOU, 16)])
        prog.op("sync", lambda s: s.nop(),
                waits=[(sSP, c_done_sp), (sDV, sDV.n), (sOU, sOU.n)])
        _finalize(nc, prog, guards)
        return nc

    wg_free = [[], []]
    wd_free = [[], []]
    xa_free = []
    gateup_pe_done = 0
    last_mult_dv = [0, 0]

    for tc in range(NTC):
        tb = tc * TCF
        w = list(xa_free) + ([(sSP, c_done_sp), (sPE, c_done_pe), (sDV, c_done_dv),
                              (sAC, c_done_ac)] if tc == 0 else [])
        xa_free = []
        prog.op("sync",
                (lambda t_: lambda s: s.dma_start(
                    out=d_xaT.rearrange("p (d t) -> p d t", d=DT),
                    in_=xaT_d.rearrange("p (d t) -> p d t", d=DT)[:, :, t_ * TCF:(t_ + 1) * TCF]))(tc),
                waits=w, incs=[(sXA, 16)])
        xa_ready = sXA.n

        pw = pe_wait_bank(4) + [(sXA, xa_ready)]
        for dt in range(DT):
            prog.op("tensor",
                    (lambda d: lambda t: t.matmul(
                        pb[4][0:1, 0:TCF], ones_col_bf[:, :],
                        d_xaT[:, d * TCF:(d + 1) * TCF],
                        start=(d == 0), stop=(d == DT - 1)))(dt),
                    waits=pw if dt == 0 else [],
                    incs=[(sPE, 1)] if dt == DT - 1 else [])
        mmv = sPE.n
        prog.op("scalar", lambda s: s.activation(d_xsum[:, :], pb[4][0:1, 0:TCF], AF.Copy),
                waits=[(sPE, mmv)], incs=[(sAC, 1)])
        xsum_ac = sAC.n
        bank_free[4] = [(sAC, xsum_ac)]
        if debug and stage == "full" and tc == 0:
            prog.op("gpsimd", lambda g: g.dma_start(out=xs_dbg[:, :], in_=d_xsum[:, :]),
                    waits=[(sAC, xsum_ac)], incs=[(sOU, 16)])

        for ic in range(16):
            par = ic % 2
            for wdram, buf in ((wg8_d, d_wg[par]), (wu8_d, d_wu[par])):
                w2 = list(wg_free[par]) if wdram is wg8_d else []
                if tc == 0 and ic < 2 and wdram is wg8_d:
                    w2 += [(sPE, c_done_pe), (sDV, c_done_dv), (sAC, c_done_ac),
                           (sSP, c_done_sp)]
                prog.op("gpsimd",
                        (lambda wd_, b_, i_: lambda g: g.dma_start(
                            out=b_.rearrange("p (ht i) -> p ht i", ht=16),
                            in_=wd_[:, i_ * TCF:(i_ + 1) * TCF].rearrange(
                                "(ht p) i -> p ht i", p=128)))(wdram, buf, ic),
                        waits=w2,
                        incs=[(sWG[par], 16)])
            wg_free[par] = []
            wg_ready = sWG[par].n
            if debug and stage == "full" and tc == 0 and ic == 0:
                prog.op("gpsimd", lambda g: g.dma_start(out=wub_dbg[:, :], in_=d_wu[0][:, :]),
                        waits=[(sWG[0], wg_ready)], incs=[(sOU, 16)])
            for isub in range(4):
                it = ic * 4 + isub
                sp_ = isub % 2
                for bk, buf, scl, act, dst in (
                        (isub % 2, d_wg[par], d_scg, AF.Silu, d_sg[sp_]),
                        (2 + isub % 2, d_wu[par], d_scu, AF.Copy, d_uu[sp_])):
                    pw = pe_wait_bank(bk)
                    if isub == 0 and bk < 2:
                        pw += [(sWG[par], wg_ready), (sAC, xsum_ac)]
                    elif isub == 0:
                        pw += [(sWG[par], wg_ready)]
                    for ht in range(16):
                        prog.op("tensor",
                                (lambda b, bf_, h_, s_: lambda t: t.matmul(
                                    pb[b][:, 0:TCF],
                                    bf_[:, h_ * TCF + s_ * 128: h_ * TCF + s_ * 128 + 128],
                                    d_xaT[:, h_ * TCF:(h_ + 1) * TCF],
                                    start=(h_ == 0), stop=False))(bk, buf, ht, isub),
                                waits=pw if ht == 0 else [])
                    prog.op("tensor",
                            (lambda b: lambda t: t.matmul(
                                pb[b][:, 0:TCF], m128_row_bf[:, :], d_xsum[:, :],
                                start=False, stop=True))(bk),
                            incs=[(sPE, 1)])
                    mmv = sPE.n
                    rawdump = (debug and stage == "full" and tc == 0 and ic == 0
                               and isub == 0 and bk >= 2)
                    if rawdump:
                        prog.op("scalar",
                                (lambda b, a_, ds_: lambda s: s.activation(
                                    ds_[:, :], pb[b][:, 0:TCF], a_))(bk, act, dst),
                                waits=[(sPE, mmv), (sDV, last_mult_dv[sp_])], incs=[(sAC, 1)])
                    else:
                        prog.op("scalar",
                                (lambda b, sc_, a_, ds_, i_: lambda s: s.activation(
                                    ds_[:, :], pb[b][:, 0:TCF], a_,
                                    scale=sc_[:, i_: i_ + 1]))(bk, scl, act, dst, it),
                                waits=[(sPE, mmv), (sDV, last_mult_dv[sp_])], incs=[(sAC, 1)])
                    bank_free[bk] = [(sAC, sAC.n)]
                if debug and stage == "full" and tc == 0 and it == 0:
                    prog.op("gpsimd", (lambda s_: lambda g: g.dma_start(out=sg_dbg[:, :], in_=d_sg[s_][:, :]))(sp_),
                            waits=[(sAC, sAC.n)], incs=[(sOU, 16)])
                    prog.op("gpsimd", (lambda s_: lambda g: g.dma_start(out=uu_dbg[:, :], in_=d_uu[s_][:, :]))(sp_),
                            incs=[(sOU, 16)])
                    prog.op("gpsimd", lambda g: g.dma_start(out=xat_dbg[:, :], in_=d_xaT[:, :]),
                            incs=[(sOU, 16)])
                prog.op("vector",
                        (lambda i_, s_: lambda v: v.tensor_tensor(
                            d_h[:, i_ * TCF:(i_ + 1) * TCF], d_sg[s_][:, :], d_uu[s_][:, :],
                            ALU.mult))(it, sp_),
                        waits=[(sAC, sAC.n), (sOU, sOU.n) if (debug and stage == "full" and tc == 0 and it == 0) else (sAC, 0)], incs=[(sDV, 1)])
                last_mult_dv[sp_] = sDV.n
            wg_free[par] = [(sPE, sPE.n), (sAC, sAC.n)]
        h_dv = sDV.n
        gateup_pe_done = sPE.n
        if debug and stage == "full" and tc == 0:
            prog.op("gpsimd", lambda g: g.dma_start(out=h_dbg[:, :], in_=d_h[:, 0:8 * TCF]),
                    waits=[(sDV, h_dv)], incs=[(sOU, 16)])

        pw = pe_wait_bank(4) + [(sDV, h_dv)]
        for it in range(64):
            prog.op("tensor",
                    (lambda i_: lambda t: t.matmul(
                        pb[4][0:1, 0:TCF], ones_col_bf[:, :],
                        d_h[:, i_ * TCF:(i_ + 1) * TCF],
                        start=(i_ == 0), stop=(i_ == 63)))(it),
                    waits=pw if it == 0 else [],
                    incs=[(sPE, 1)] if it == 63 else [])
        mmv = sPE.n
        prog.op("scalar", lambda s: s.activation(d_hsum[:, :], pb[4][0:1, 0:TCF], AF.Copy),
                waits=[(sPE, mmv)], incs=[(sAC, 1)])
        hsum_ac = sAC.n
        bank_free[4] = [(sAC, hsum_ac)]

        for dh in range(2):
            pw_all = []
            for dto in range(8):
                pw_all += pe_wait_bank(dto)
            first_mm = [True] * 8
            for ic in range(16):
                par = ic % 2
                w = list(wd_free[par])
                if dh == 0 and ic < 2:
                    w += [(sPE, gateup_pe_done)]
                prog.op("gpsimd",
                        (lambda i_, d_, p_: lambda g: g.dma_start(
                            out=d_wd[p_].rearrange("p (itl d) -> p itl d", itl=4),
                            in_=wd8_d[i_ * TCF:(i_ + 1) * TCF,
                                      d_ * 1024:(d_ + 1) * 1024].rearrange(
                                "(itl p) d -> p itl d", p=128)))(ic, dh, par),
                        waits=w, incs=[(sWD[par], 16)])
                wd_free[par] = []
                wd_ready = sWD[par].n
                for itl in range(4):
                    it = ic * 4 + itl
                    for dto in range(8):
                        pw = []
                        if itl == 0 and dto == 0:
                            pw = [(sWD[par], wd_ready), (sDV, h_dv)] + (pw_all if ic == 0 else [])
                        prog.op("tensor",
                                (lambda p_, il, d_o, i_, st: lambda t: t.matmul(
                                    pb[d_o][:, 0:TCF],
                                    d_wd[p_][:, il * 1024 + d_o * 128: il * 1024 + (d_o + 1) * 128],
                                    d_h[:, i_ * TCF:(i_ + 1) * TCF],
                                    start=st, stop=False))(par, itl, dto, it, first_mm[dto]),
                                waits=pw,
                                incs=[(sPE, 1)] if (itl == 3 and dto == 7) else [])
                        first_mm[dto] = False
                wd_free[par] = [(sPE, sPE.n)]
            for dto in range(8):
                prog.op("tensor",
                        (lambda d_o: lambda t: t.matmul(
                            pb[d_o][:, 0:TCF], m128_row_bf[:, :], d_hsum[:, :],
                            start=False, stop=True))(dto),
                        waits=[(sAC, hsum_ac)] if dto == 0 else [],
                        incs=[(sPE, 1)] if dto == 7 else [])
            mmv = sPE.n
            for dto in range(8):
                dt = dh * 8 + dto
                prog.op("vector",
                        (lambda d_o, d_: lambda v: v.scalar_tensor_tensor(
                            d_r2T[:, d_ * TCF:(d_ + 1) * TCF], pb[d_o][:, 0:TCF],
                            d_scd[:, d_: d_ + 1], d_xaT[:, d_ * TCF:(d_ + 1) * TCF],
                            ALU.mult, ALU.add))(dto, dt),
                        waits=[(sPE, mmv)] if dto == 0 else [], incs=[(sDV, 1)])
                bank_free[dto] = [(sDV, sDV.n)]
        r2_dv = sDV.n
        if debug and stage == "full" and tc == 0:
            prog.op("gpsimd", lambda g: g.dma_start(out=r2_dbg[:, :], in_=d_r2T[:, :]),
                    waits=[(sDV, r2_dv)], incs=[(sOU, 16)])

        for ttl in range(TCF // 128):
            for dt in range(DT):
                bk = dt % 2
                pw = pe_wait_bank(bk) + ([(sDV, r2_dv)] if (ttl == 0 and dt < 2) else [])
                prog.op("tensor",
                        (lambda d_, tl, b: lambda t: t.transpose(
                            pb[b][:, 0:128], d_r2T[:, d_ * TCF + tl * 128: d_ * TCF + (tl + 1) * 128],
                            ident_f32[:, :]))(dt, ttl, bk),
                        waits=pw, incs=[(sPE, 1)])
                mmv = sPE.n
                aw = [(sPE, mmv)]
                if dt == 0:
                    aw.append((sAC, prev_sq_ac))
                prog.op("vector",
                        (lambda d_, b: lambda v: v.tensor_copy(
                            d_r2t[:, d_ * 128:(d_ + 1) * 128], pb[b][:, 0:128]))(dt, bk),
                        waits=aw, incs=[(sDV, 1)])
                bank_free[bk] = [(sDV, sDV.n)]
            tile_dv = sDV.n
            prog.op("vector", lambda v: v.reduce_sum(stats[:, 24:25], d_r2t[:, :], axis=AX.X),
                    incs=[(sDV, 1)])
            prog.op("vector", lambda v: v.tensor_scalar_mul(stats[:, 24:25], stats[:, 24:25], 1.0 / Dm),
                    waits=[(sDV, sDV.n)], incs=[(sDV, 1)])
            prog.op("scalar", lambda s: s.activation(d_sq[:, :], d_r2t[:, :], AF.Square,
                                                     accum_out=stats[:, 36:37]),
                    waits=[(sDV, tile_dv), (sOU, sOU.n)], incs=[(sAC, 1)])
            sq_ac = sAC.n
            prev_sq_ac = sq_ac
            prog.op("vector", lambda v: v.tensor_tensor(stats[:, 28:29], stats[:, 24:25], stats[:, 24:25], ALU.mult),
                    waits=[(sDV, sDV.n)], incs=[(sDV, 1)])
            prog.op("vector", lambda v: v.tensor_scalar_add(stats[:, 28:29], stats[:, 28:29], -EPS_LN),
                    waits=[(sDV, sDV.n)], incs=[(sDV, 1)])
            prog.op("vector", lambda v: v.scalar_tensor_tensor(
                stats[:, 32:33], stats[:, 36:37], 1.0 / Dm, stats[:, 28:29], ALU.mult, ALU.subtract),
                    waits=[(sAC, sq_ac), (sDV, sDV.n)], incs=[(sDV, 1)])
            var_dv = sDV.n
            prog.op("scalar", lambda s: s.activation(stats[:, 40:41], stats[:, 32:33], AF.Ln),
                    waits=[(sDV, var_dv)], incs=[(sAC, 1)])
            prog.op("scalar", lambda s: s.activation(stats[:, 44:45], stats[:, 40:41], AF.Exp, scale=misc_sb[:, 0:1]),
                    waits=[(sAC, sAC.n)], incs=[(sAC, 1)])
            rstd_ac = sAC.n
            prog.op("vector", lambda v: v.tensor_scalar(
                d_sq[:, :], d_r2t[:, :], stats[:, 24:25], stats[:, 44:45], ALU.subtract, ALU.mult),
                    waits=[(sAC, rstd_ac)], incs=[(sDV, 1)])
            out_dv = sDV.n
            trow = tb + ttl * 128
            prog.op("sync",
                    (lambda r: lambda s: s.dma_start(
                        out=out_d[r:r + 128, :], in_=d_sq[:, :]))(trow),
                    waits=[(sDV, out_dv)], incs=[(sOU, 16)])
        xa_free = [(sPE, sPE.n), (sDV, sDV.n), (sOU, sOU.n)]
        wg_free[0] = [(sPE, sPE.n), (sDV, sDV.n)]
        wg_free[1] = [(sPE, sPE.n), (sDV, sDV.n)]

    prog.op("sync", lambda s: s.nop(), waits=[(sOU, sOU.n), (sSP, sSP.n)])

    _finalize(nc, prog, guards)
    return nc


def _finalize(nc, prog, guards):
    blk = nc.Block()
    block = blk.__enter__()
    try:
        block.sync(lambda e: prog.replay("sync", e))
        block.tensor(lambda e: prog.replay("tensor", e))
        block.vector(lambda e: prog.replay("vector", e))
        block.scalar(lambda e: prog.replay("scalar", e))
        block.gpsimd(lambda e: prog.replay("gpsimd", e))
    finally:
        blk.__exit__(None, None, None)
        for g in reversed(guards):
            g.__exit__(None, None, None)


_NC_CACHE = {}


def _get_nc():
    key = (STAGE, DEBUG, SERIAL)
    if key not in _NC_CACHE:
        _NC_CACHE[key] = build_nc(*key)
    return _NC_CACHE[key]


def _u8_view(a_int32):
    a = np.ascontiguousarray(a_int32)
    return np.ascontiguousarray(a.view(np.uint8).reshape(a.shape + (4,))[..., 0])


def make_in_maps(inputs):
    x = np.asarray(inputs["L0_attn_hidden_states"], np.float32)
    wqT = np.ascontiguousarray(np.asarray(inputs["L0_attn_Wq"], np.float32).T)
    wkT = np.ascontiguousarray(np.asarray(inputs["L0_attn_Wk"], np.float32).T)
    wvT = np.ascontiguousarray(np.asarray(inputs["L0_attn_Wv"], np.float32).T)
    woT = np.ascontiguousarray(np.asarray(inputs["L0_attn_Wo"], np.float32).T)
    wfq = np.ascontiguousarray(np.asarray(inputs["L0_attn_Wfq"], np.float32).reshape(H * HD, FD))
    wfk = np.ascontiguousarray(np.asarray(inputs["L0_attn_Wfk"], np.float32).reshape(H * HD, FD))
    wg8 = _u8_view(np.asarray(inputs["L0_ffn_gate_w"]).T)
    wu8 = _u8_view(np.asarray(inputs["L0_ffn_up_w"]).T)
    wd8 = _u8_view(np.asarray(inputs["L0_ffn_down_w"]).T)
    scg = np.ascontiguousarray(np.asarray(inputs["L0_ffn_gate_scale"], np.float32).reshape(-1, 128).T)
    scu = np.ascontiguousarray(np.asarray(inputs["L0_ffn_up_scale"], np.float32).reshape(-1, 128).T)
    scd = np.ascontiguousarray(np.asarray(inputs["L0_ffn_down_scale"], np.float32).reshape(-1, 128).T)
    ident = np.eye(128, dtype=np.float32)
    maskA = np.triu(np.ones((128, 128), np.float32))
    onesrow = np.ones((1, 128), np.float32)
    m128row = np.full((1, 128), -128.0, np.float32)
    onescol = np.ones((128, 1), np.float32)
    misc = np.zeros((128, 2), np.float32)
    misc[:, 0] = -0.5
    eps65 = np.zeros((1, 65), np.float32)
    eps65[0, 64] = EPS_ATTN

    in_maps = []
    for c in range(NCORES):
        b, q = c // 4, c % 4
        t0 = q * T
        xs = x[b, t0:t0 + T, :]
        ccm = np.zeros((128, 8), np.float32)
        for j in range(NCORES):
            if j // 4 == b and (j % 4) < q:
                ccm[:, j] = 1.0
        in_maps.append({
            "xT": np.ascontiguousarray(xs.T),
            "xn": np.ascontiguousarray(xs),
            "wqT": wqT, "wkT": wkT, "wvT": wvT, "woT": woT,
            "wfq": wfq, "wfk": wfk,
            "wg8": wg8, "wu8": wu8, "wd8": wd8,
            "scg": scg, "scu": scu, "scd": scd,
            "identf": ident, "maskA": maskA,
            "onesrow": onesrow, "m128row": m128row, "onescol": onescol,
            "ccmask": ccm, "misc": misc, "eps65": eps65,
        })
    return in_maps


def kernel(**inputs):
    nc = _get_nc()
    in_maps = make_in_maps(inputs)
    res = run_bass_kernel_spmd(nc, in_maps, core_ids=list(range(NCORES)))
    out = np.empty((Bb, L, Dm), np.float32)
    for c in range(NCORES):
        b, q = c // 4, c % 4
        out[b, q * T:(q + 1) * T, :] = res.results[c]["out"]
    return out


# version 07
# speedup vs baseline: 1.0674x; 1.0097x over previous
import numpy as np

import concourse.bass as bass
import concourse.mybir as mybir
from concourse.bass_utils import run_bass_kernel_spmd

F32 = mybir.dt.float32
BF16 = mybir.dt.bfloat16
U8 = mybir.dt.uint8
AF = mybir.ActivationFunctionType
ALU = mybir.AluOpType
AX = mybir.AxisListType

NCORES = 8
Bb, L, Dm = 2, 8192, 2048
H, HD, FD = 32, 64, 64
Ii = 8192
T = 2048
HF = 1024
CH = 128
NCH = T // CH
DT = Dm // 128
OT = Dm // 128
TCF = 512
NTC = T // TCF
EPS_ATTN = 1e-12
EPS_LN = 1e-5

STAGE = "full"
DEBUG = False
SERIAL = False


class Sem:
    def __init__(self, h):
        self.h = h
        self.n = 0


class Prog:
    ENGINES = ("sync", "tensor", "vector", "scalar", "gpsimd")

    def __init__(self):
        self.ops = {e: [] for e in self.ENGINES}

    def op(self, eng, builder, waits=(), incs=()):
        waits = [(s, v) for (s, v) in waits if v > 0]
        incs = [(s, n) for (s, n) in incs]
        for s, n in incs:
            s.n += n
        self.ops[eng].append((builder, waits, incs))

    def replay(self, name, eng):
        for builder, waits, incs in self.ops[name]:
            for s, v in waits:
                eng.wait_ge(s.h, v)
            ins = builder(eng)
            if incs:
                assert ins is not None
                for s, n in incs:
                    ins = ins.then_inc(s.h, n)


class Arena:

    def __init__(self, tile, base=0):
        self.tile = tile
        self.off = base

    def alloc(self, parts, elems, dtype, align=512):
        esz = {F32: 4, BF16: 2, U8: 1}[dtype]
        self.off = (self.off + align - 1) // align * align
        nbytes = elems * esz
        ap = self.tile[0:parts, self.off:self.off + nbytes]
        if dtype != U8:
            ap = ap.bitcast(dtype)
        self.off += nbytes
        return ap


ARENA_B = 204544


def build_nc(stage=STAGE, debug=DEBUG, serial=SERIAL):
    nc = bass.Bass("TRN2", target_bir_lowering=False)

    def din(name, shape, dt):
        return nc.dram_tensor(name, shape, dt, kind="ExternalInput").ap()

    xT_d = din("xT", [Dm, T], F32)
    xn_d = din("xn", [T, Dm], F32)
    wqT_d = din("wqT", [Dm, Dm], F32)
    wkT_d = din("wkT", [Dm, Dm], F32)
    wvT_d = din("wvT", [Dm, Dm], F32)
    woT_d = din("woT", [Dm, Dm], F32)
    wfq_d = din("wfq", [H * HD, FD], F32)
    wfk_d = din("wfk", [H * HD, FD], F32)
    wg8_d = din("wg8", [Dm, Ii], U8)
    wu8_d = din("wu8", [Dm, Ii], U8)
    wd8_d = din("wd8", [Ii, Dm], U8)
    scg_d = din("scg", [128, Ii // 128], F32)
    scu_d = din("scu", [128, Ii // 128], F32)
    scd_d = din("scd", [128, Dm // 128], F32)
    ident_d = din("identf", [128, 128], F32)
    maskA_d = din("maskA", [128, 128], F32)
    onesrow_d = din("onesrow", [1, 128], F32)
    m128row_d = din("m128row", [1, 128], F32)
    onescol_d = din("onescol", [128, 1], F32)
    ccmask_d = din("ccmask", [128, 8], F32)
    misc_d = din("misc", [128, 2], F32)
    eps65_d = din("eps65", [1, 65], F32)

    out_d = nc.dram_tensor("out", [T, Dm], F32, kind="ExternalOutput").ap()

    ekT_d = nc.dram_tensor("ekT_s", [128, OT * T], BF16).ap()
    fqT_d = nc.dram_tensor("fqT_s", [128, OT * T], BF16).ap()
    ek_d = nc.dram_tensor("ek_s", [T, H * FD], BF16).ap()
    vp_d = nc.dram_tensor("vp_s", [T, H * 65], BF16).ap()
    xaT_d = nc.dram_tensor("xaT_s", [128, DT * T], BF16).ap()
    yT_dbg = nc.dram_tensor("yT_dbg", [128, NCH * OT * CH], F32, kind="ExternalOutput").ap() if (debug and stage == "c") else None
    h_dbg = nc.dram_tensor("h_dbg", [128, 8 * TCF], F32, kind="ExternalOutput").ap() if (debug and stage == "full") else None
    r2_dbg = nc.dram_tensor("r2_dbg", [128, DT * TCF], F32, kind="ExternalOutput").ap() if (debug and stage == "full") else None
    xs_dbg = nc.dram_tensor("xs_dbg", [1, TCF], F32, kind="ExternalOutput").ap() if (debug and stage == "full") else None
    sg_dbg = nc.dram_tensor("sg_dbg", [128, TCF], F32, kind="ExternalOutput").ap() if (debug and stage == "full") else None
    wub_dbg = nc.dram_tensor("wub_dbg", [128, 16 * TCF], F32, kind="ExternalOutput").ap() if (debug and stage == "full") else None
    ps_dbg = nc.dram_tensor("ps_dbg", [128, TCF], F32, kind="ExternalOutput").ap() if (debug and stage == "full") else None
    psg_sb = None
    uu_dbg = nc.dram_tensor("uu_dbg", [128, TCF], F32, kind="ExternalOutput").ap() if (debug and stage == "full") else None
    xat_dbg = nc.dram_tensor("xat_dbg", [128, DT * TCF], F32, kind="ExternalOutput").ap() if (debug and stage == "full") else None
    r1_dbg = nc.dram_tensor("r1_dbg", [128, Dm], F32, kind="ExternalOutput").ap() if debug else None
    s_dbg = nc.dram_tensor("s_dbg", [64, NCH * H * 65], F32, kind="ExternalOutput").ap() if debug else None
    am8_dbg = nc.dram_tensor("am8_dbg", [128, 4 * CH], F32, kind="ExternalOutput").ap() if debug else None
    sbf_dbg = nc.dram_tensor("sbf_dbg", [128, H * 65], F32, kind="ExternalOutput").ap() if debug else None
    fq8_dbg = nc.dram_tensor("fq8_dbg", [128, OT * CH], F32, kind="ExternalOutput").ap() if debug else None
    vp8_dbg = nc.dram_tensor("vp8_dbg", [128, H * 65], F32, kind="ExternalOutput").ap() if debug else None
    st_dbg = nc.dram_tensor("st_dbg", [128, 48], F32, kind="ExternalOutput").ap() if debug else None
    am_dbg = nc.dram_tensor("am_dbg", [128, 4 * CH], F32, kind="ExternalOutput").ap() if debug else None
    dn_dbg = nc.dram_tensor("dn_dbg", [1, 4 * CH], F32, kind="ExternalOutput").ap() if debug else None
    dbg_f32 = {}
    if debug:
        for nm, src_ap in (("ekT_f", ekT_d), ("fqT_f", fqT_d), ("ek_f", ek_d),
                           ("vp_f", vp_d), ("xaT_f", xaT_d)):
            dbg_f32[nm] = (nc.dram_tensor(nm, list(src_ap.shape), F32,
                                          kind="ExternalOutput").ap(), src_ap)
    tc_dbg = nc.dram_tensor("tc_dbg", [64, H * 65], F32, kind="ExternalOutput").ap() if debug else None
    s0_dbg = nc.dram_tensor("s0_dbg", [64, H * 65], F32, kind="ExternalOutput").ap() if debug else None

    ag_in = nc.dram_tensor("ag_in", [64, H * 65], F32).ap()
    ag_out = nc.dram_tensor("ag_out", [NCORES * 64, H * 65], F32, addr_space="Shared").ap()

    prog = Prog()
    guards = []

    def enter(g):
        guards.append(g)
        return g.__enter__()

    arena_t = enter(nc.sbuf_tensor("arena", [128, ARENA_B], U8))
    pb = [enter(nc.psum_tensor(f"pb{k}", [128, 512], F32)) for k in range(8)]

    def S(name):
        return Sem(enter(nc.semaphore(name)))

    sPE, sDV, sAC = S("sPE"), S("sDV"), S("sAC")
    sXT = S("sXT")
    sWT = [S("sWT0"), S("sWT1")]
    sWV = [S("sWV0"), S("sWV1")]
    sWF = S("sWF")
    sXN = S("sXN")
    sSP = S("sSP")
    sCH = [S("sCH0"), S("sCH1")]
    sWO = S("sWO")
    sCC = S("sCC")
    sAG = S("sAG")
    sWG = [S("sWG0"), S("sWG1")]
    sWD = [S("sWD0"), S("sWD1")]
    sXA = S("sXA")
    sOU = S("sOU")

    def barrier():
        prog.op("tensor", lambda t: t.nop(), incs=[(sPE, 1)])
        prog.op("vector", lambda v: v.nop(), incs=[(sDV, 1)])
        prog.op("scalar", lambda s: s.nop(), incs=[(sAC, 1)])
        w = [(sPE, sPE.n), (sDV, sDV.n), (sAC, sAC.n), (sOU, sOU.n), (sSP, sSP.n)]
        for e in ("tensor", "vector", "scalar", "sync", "gpsimd"):
            prog.op(e, lambda x: x.nop(), waits=list(w))

    A = Arena(arena_t[:, :], 0)
    ident_bf = A.alloc(128, 128, BF16)
    ident_f32 = A.alloc(128, 128, F32)
    mask_bf = A.alloc(128, 128, BF16)
    ones_row_bf = A.alloc(1, 128, BF16)
    ones_row_f32 = A.alloc(1, 128, F32)
    ones_hi_t = A.alloc(128, 64, F32)
    ones_row_f32_hi = ones_hi_t[64:65, :]
    m128_row_bf = A.alloc(1, 128, BF16)
    ones_col_bf = A.alloc(128, 1, BF16)
    ccmask_sb = A.alloc(128, 8, F32)
    misc_sb = A.alloc(128, 2, F32)
    eps65_bf = A.alloc(1, 65, BF16)
    stats = A.alloc(128, 48, F32)
    S_sb = A.alloc(64, H * 65, F32)
    S_bf = A.alloc(128, H * 65, BF16)
    Tc_sb = A.alloc(64, H * 65, F32)
    d_scg = A.alloc(128, 64, F32)
    d_scu = A.alloc(128, 64, F32)
    d_scd = A.alloc(128, 16, F32)
    pers_end = A.off

    A.off = pers_end
    a_xT = A.alloc(128, DT * HF, BF16)
    a_kqT = A.alloc(128, OT * HF, BF16)
    a_ekT_st = A.alloc(128, 2 * HF, BF16)
    a_fqT_st = A.alloc(128, 2 * HF, BF16)
    a_ek = A.alloc(128, 8 * H * FD, BF16)
    a_vp = A.alloc(128, 8 * H * 65, BF16)
    a_wT = A.alloc(128, 2 * DT * 128, BF16)
    a_wV0 = A.alloc(128, DT * 512, BF16)
    a_wfq = A.alloc(128, H * FD, BF16)
    a_wfk = A.alloc(128, H * FD, BF16)
    a_sk = A.alloc(128, 8 * H, F32)
    a_rsk = A.alloc(128, 8 * H, F32)
    assert A.off <= ARENA_B, A.off
    a_wV1 = a_kqT[:, 0:DT * 512]

    A.off = pers_end
    c_woT = A.alloc(128, OT * Dm, BF16)
    c_ekT = [A.alloc(128, OT * CH, BF16) for _ in range(2)]
    c_fqT = [A.alloc(128, OT * CH, BF16) for _ in range(2)]
    c_ek = [A.alloc(128, H * FD, BF16) for _ in range(2)]
    c_vp = [A.alloc(128, H * 65, BF16) for _ in range(2)]
    c_xn = [A.alloc(128, Dm, F32) for _ in range(2)]
    c_am = A.alloc(128, H * CH, BF16)
    c_dnrc = A.alloc(128, 4096, F32)
    c_rcb = c_dnrc[0:64, 0:2048]
    c_dn = c_dnrc[64:65, 0:2048]
    c_rc = c_dnrc[64:65, 2048:4096]
    c_yn = A.alloc(128, 8 * CH, BF16)
    c_yT = A.alloc(128, OT * CH, BF16)
    c_r1 = A.alloc(128, Dm, F32)
    c_sq = A.alloc(128, Dm, BF16)
    c_xa = A.alloc(128, Dm, BF16)
    c_xaT = A.alloc(128, DT * CH, BF16)
    assert A.off <= ARENA_B, A.off
    b_rk = [c_r1[:, 0:1040], c_xn[0][:, 0:1040]]
    b_acc = c_xn[1][:, 0:1040]

    A.off = pers_end
    off_xa = A.off
    d_xaT = A.alloc(128, DT * TCF, BF16)
    end_xa = A.off
    A.off = off_xa
    d_r2t = A.alloc(128, Dm, F32)
    d_sq = A.alloc(128, Dm, F32)
    assert A.off == end_xa
    d_wg = [A.alloc(128, 16 * TCF, BF16) for _ in range(2)]
    off_wu = A.off
    d_wu = [A.alloc(128, 16 * TCF, BF16) for _ in range(2)]
    end_wu = A.off
    A.off = off_wu
    d_r2T = A.alloc(128, DT * TCF, F32)
    assert A.off == end_wu
    d_h = A.alloc(128, 64 * TCF, BF16)
    d_sg = [A.alloc(128, TCF, BF16) for _ in range(2)]
    d_uu = [A.alloc(128, TCF, BF16) for _ in range(2)]
    d_xsum = A.alloc(1, TCF, BF16)
    d_hsum = A.alloc(1, TCF, BF16)
    d_dbgst = A.alloc(128, TCF, F32) if debug else None
    assert A.off <= ARENA_B, A.off
    d_wd = [d_wg[0][:, 0:4096], d_wg[1][:, 0:4096]]

    def ld(dst, src):
        return lambda g: g.dma_start(out=dst, in_=src)

    for dst, src in (
            (ident_bf[:, :], ident_d[:, :]),
            (ident_f32[:, :], ident_d[:, :]),
            (mask_bf[:, :], maskA_d[:, :]),
            (ones_row_bf[:, :], onesrow_d[:, :]),
            (ones_row_f32[:, :], onesrow_d[:, :]),
            (ones_row_f32_hi[:, :], onesrow_d[:, 0:64]),
            (m128_row_bf[:, :], m128row_d[:, :]),
            (ones_col_bf[:, :], onescol_d[:, :]),
            (ccmask_sb[:, :], ccmask_d[:, :]),
            (misc_sb[:, :], misc_d[:, :]),
            (eps65_bf[:, :], eps65_d[:, :]),
            (a_wfq[0:64, :].rearrange("p (h f) -> p h f", h=H),
             wfq_d.rearrange("(h p) f -> p h f", p=64)),
            (a_wfq[64:128, :].rearrange("p (h f) -> p h f", h=H),
             wfq_d.rearrange("(h p) f -> p h f", p=64)),
            (a_wfk[0:64, :].rearrange("p (h f) -> p h f", h=H),
             wfk_d.rearrange("(h p) f -> p h f", p=64)),
            (a_wfk[64:128, :].rearrange("p (h f) -> p h f", h=H),
             wfk_d.rearrange("(h p) f -> p h f", p=64)),
            (d_scg[:, :], scg_d[:, :]),
            (d_scu[:, :], scu_d[:, :]),
            (d_scd[:, :], scd_d[:, :]),
    ):
        prog.op("gpsimd", ld(dst, src), incs=[(sWF, 16)])
    WF_ALL = sWF.n

    prog.op("vector", lambda v: v.memset(Tc_sb[:, :], 0.0), incs=[(sDV, 1)])

    bank_free = [[] for _ in range(8)]

    def pe_wait_bank(k):
        w = bank_free[k]
        bank_free[k] = []
        return w

    wT_free = [[], []]
    wV_free = [[], []]
    ekst_free = [[], []]
    fqst_free = [[], []]
    prev_half = {"sp": 0, "dv": 0}

    def half_A(hf):
        t0 = hf * HF
        xw = [(sPE, sPE.n)] if hf else []
        for j in range(4):
            src = xT_d[:, t0:t0 + HF].rearrange("(dt p) t -> p dt t", p=128)[:, j * 4:(j + 1) * 4, :]
            dst = a_xT.rearrange("p (dt t) -> p dt t", dt=DT)[:, j * 4:(j + 1) * 4, :]
            prog.op("gpsimd", ld(dst, src), waits=xw if j == 0 else [], incs=[(sXT, 16)])
        xt_ready = sXT.n

        for which, w_dram in (("k", wkT_d), ("q", wqT_d)):
            for ot in range(OT):
                par = ot % 2
                src = w_dram[:, ot * 128:(ot + 1) * 128].rearrange("(dt p) o -> p dt o", p=128)
                dst = a_wT[:, par * DT * 128:(par + 1) * DT * 128].rearrange("p (dt o) -> p dt o", dt=DT)
                prog.op("gpsimd", ld(dst, src), waits=wT_free[par], incs=[(sWT[par], 16)])
                wT_free[par] = []
                w_ready = sWT[par].n
                for nt in range(HF // 512):
                    bk = (nt + ot) % 2
                    pw = pe_wait_bank(bk) + ([(sWT[par], w_ready)] if nt == 0 else [])
                    if nt == 0 and ot == 0:
                        pw.append((sXT, xt_ready))
                    for dt in range(DT):
                        lhs = a_wT[:, par * DT * 128 + dt * 128: par * DT * 128 + (dt + 1) * 128]
                        rhs = a_xT[:, dt * HF + nt * 512: dt * HF + (nt + 1) * 512]
                        prog.op("tensor",
                                (lambda l, r, b, s0, s1: lambda t: t.matmul(
                                    pb[b][:, :], l, r, start=s0, stop=s1))(
                                    lhs, rhs, bk, dt == 0, dt == DT - 1),
                                waits=pw if dt == 0 else [],
                                incs=[(sPE, 1)] if dt == DT - 1 else [])
                    mmv = sPE.n
                    dst_kq = a_kqT[:, ot * HF + nt * 512: ot * HF + (nt + 1) * 512]
                    prog.op("vector",
                            (lambda d, b: lambda v: v.tensor_copy(d, pb[b][:, :]))(dst_kq, bk),
                            waits=[(sPE, mmv)], incs=[(sDV, 1)])
                    bank_free[bk] = [(sDV, sDV.n)]
                wT_free[par] = [(sPE, sPE.n)]

            for h in range(H):
                ot, pbase = h // 2, (h % 2) * 64
                wf = a_wfk if which == "k" else a_wfq
                stt = a_ekT_st if which == "k" else a_fqT_st
                stf = ekst_free if which == "k" else fqst_free
                for nt in range(HF // 512):
                    bk = 2 + (h + nt) % 2
                    pw = pe_wait_bank(bk)
                    if h == 0 and nt == 0:
                        pw += [(sWF, WF_ALL), (sDV, sDV.n)]
                    prog.op("tensor",
                            (lambda hh, oo, pp, nn, b, w: lambda t: t.matmul(
                                pb[b][0:64, 0:512],
                                w[pp:pp + 64, hh * FD:(hh + 1) * FD],
                                a_kqT[pp:pp + 64, oo * HF + nn * 512: oo * HF + (nn + 1) * 512],
                                start=True, stop=True))(h, ot, pbase, nt, bk, wf),
                            waits=pw, incs=[(sPE, 1)])
                    mmv = sPE.n
                    aw = [(sPE, mmv)]
                    if pbase == 0 and nt == 0:
                        aw += stf[ot % 2]
                        stf[ot % 2] = []
                    dstst = stt[pbase:pbase + 64, (ot % 2) * HF + nt * 512:(ot % 2) * HF + (nt + 1) * 512]
                    prog.op("scalar",
                            (lambda d, b: lambda s: s.activation(d, pb[b][0:64, 0:512], AF.Exp))(dstst, bk),
                            waits=aw, incs=[(sAC, 1)])
                    bank_free[bk] = [(sAC, sAC.n)]
                if which == "k":
                    for tt in range(HF // CH):
                        bk = 4 + (h + tt) % 2
                        pw = pe_wait_bank(bk)
                        prog.op("tensor",
                                (lambda hh, oo, pp, t_, b: lambda t: t.matmul(
                                    pb[b][:, 0:64],
                                    a_kqT[pp:pp + 64, oo * HF + t_ * CH: oo * HF + (t_ + 1) * CH],
                                    a_wfk[pp:pp + 64, hh * FD:(hh + 1) * FD],
                                    start=True, stop=True))(h, ot, pbase, tt, bk),
                                waits=pw, incs=[(sPE, 1)])
                        mmv = sPE.n
                        aw = [(sPE, mmv)]
                        if h == 0 and tt == 0 and hf == 1:
                            aw.append((sSP, prev_half["sp"]))
                        prog.op("scalar",
                                (lambda hh, t_, b: lambda s: s.activation(
                                    a_ek[:, t_ * H * FD + hh * FD: t_ * H * FD + (hh + 1) * FD],
                                    pb[b][:, 0:64], AF.Exp,
                                    accum_out=a_sk[:, t_ * H + hh: t_ * H + hh + 1]))(h, tt, bk),
                                waits=aw, incs=[(sAC, 1)])
                        bank_free[bk] = [(sAC, sAC.n)]
                if h % 2 == 1:
                    dd = ekT_d if which == "k" else fqT_d
                    acv = sAC.n
                    prog.op("scalar",
                            (lambda o, s_, d_: lambda s: s.dma_start(
                                out=d_[:, o * T + t0: o * T + t0 + HF],
                                in_=s_[:, (o % 2) * HF:(o % 2 + 1) * HF]))(ot, stt, dd),
                            waits=[(sAC, acv)], incs=[(sSP, 16)])
                    stf[ot % 2] = [(sSP, sSP.n)]

            if which == "k":
                kfeat_pe = sPE.n
                prog.op("vector",
                        lambda v: v.reciprocal(a_rsk[:, 0:8 * H], a_sk[:, 0:8 * H]),
                        waits=[(sAC, sAC.n)], incs=[(sDV, 1)])
                rsk_ready = sDV.n

                for oc in range(4):
                    par = oc % 2
                    wv_buf = a_wV0 if par == 0 else a_wV1
                    src = wvT_d[:, oc * 512:(oc + 1) * 512].rearrange("(dt p) o -> p dt o", p=128)
                    dst = wv_buf.rearrange("p (dt o) -> p dt o", dt=DT)
                    w = list(wV_free[par])
                    if par == 1:
                        w.append((sPE, kfeat_pe))
                    prog.op("gpsimd", ld(dst, src), waits=w, incs=[(sWV[par], 16)])
                    wV_free[par] = []
                    wv_ready = sWV[par].n
                    for tt in range(HF // CH):
                        bk = (oc + tt) % 2
                        pw = pe_wait_bank(bk) + ([(sWV[par], wv_ready)] if tt == 0 else [])
                        for dt in range(DT):
                            lhs = a_xT[:, dt * HF + tt * CH: dt * HF + (tt + 1) * CH]
                            rhs = wv_buf[:, dt * 512:(dt + 1) * 512]
                            prog.op("tensor",
                                    (lambda l, r, b, s0, s1: lambda t: t.matmul(
                                        pb[b][:, :],
                                        l, r, start=s0, stop=s1))(lhs, rhs, bk, dt == 0, dt == DT - 1),
                                    waits=pw if dt == 0 else [],
                                    incs=[(sPE, 1)] if dt == DT - 1 else [])
                        mmv = sPE.n
                        for hh in range(8):
                            h = oc * 8 + hh
                            aw = []
                            if hh == 0:
                                aw = [(sPE, mmv), (sDV, rsk_ready)]
                                if tt == 0 and oc == 0 and hf == 1:
                                    aw.append((sSP, prev_half["sp"]))
                            prog.op("scalar",
                                    (lambda h_, t_, b, hh_: lambda s: s.activation(
                                        a_vp[:, t_ * H * 65 + h_ * 65: t_ * H * 65 + h_ * 65 + 64],
                                        pb[b][:, hh_ * 64:(hh_ + 1) * 64], AF.Copy,
                                        scale=a_rsk[:, t_ * H + h_: t_ * H + h_ + 1]))(h, tt, bk, hh),
                                    waits=aw, incs=[(sAC, 1)] if hh == 7 else [])
                        bank_free[bk] = [(sAC, sAC.n)]
                    wV_free[par] = [(sPE, sPE.n)]
                for tt in range(HF // CH):
                    prog.op("vector",
                            (lambda t_: lambda v: v.tensor_copy(
                                a_vp[:, t_ * H * 65 + 64: t_ * H * 65 + 64 + (H - 1) * 65 + 1: 65],
                                a_rsk[:, t_ * H:(t_ + 1) * H]))(tt),
                            waits=[(sAC, sAC.n)] if tt == 0 else [], incs=[(sDV, 1)])
                vp_done_dv = sDV.n

                for tt in range(HF // CH):
                    for hg in range(8):
                        bk = 6 + (tt * 8 + hg) % 2
                        pw = pe_wait_bank(bk)
                        if tt == 0 and hg == 0:
                            pw += [(sDV, vp_done_dv)]
                        for j in range(4):
                            h = hg * 4 + j
                            prog.op("tensor",
                                    (lambda h_, t_, b, j_: lambda t: t.matmul(
                                        pb[b][0:64, j_ * 65:(j_ + 1) * 65],
                                        a_ek[:, t_ * H * FD + h_ * FD: t_ * H * FD + (h_ + 1) * FD],
                                        a_vp[:, t_ * H * 65 + h_ * 65: t_ * H * 65 + (h_ + 1) * 65],
                                        start=True, stop=True))(h, tt, bk, j),
                                    waits=pw if j == 0 else [],
                                    incs=[(sPE, 1)] if j == 3 else [])
                        mmv = sPE.n
                        aw = [(sPE, mmv)]
                        if tt == 0 and hg == 0 and hf == 0:
                            aw.append((sDV, 1))
                        prog.op("vector",
                                (lambda b, g_: lambda v: v.tensor_tensor(
                                    Tc_sb[:, g_ * 260:(g_ + 1) * 260],
                                    pb[b][0:64, 0:260],
                                    Tc_sb[:, g_ * 260:(g_ + 1) * 260], ALU.add))(bk, hg),
                                waits=aw, incs=[(sDV, 1)])
                        bank_free[bk] = [(sDV, sDV.n)]

                dvv = sDV.n
                for tt in range(HF // CH):
                    prog.op("scalar",
                            (lambda t_: lambda s: s.dma_start(
                                out=ek_d[t0 + t_ * CH: t0 + (t_ + 1) * CH, :],
                                in_=a_ek[:, t_ * H * FD:(t_ + 1) * H * FD]))(tt),
                            waits=[(sDV, dvv)] if tt == 0 else [], incs=[(sSP, 16)])
                    prog.op("scalar",
                            (lambda t_: lambda s: s.dma_start(
                                out=vp_d[t0 + t_ * CH: t0 + (t_ + 1) * CH, :],
                                in_=a_vp[:, t_ * H * 65:(t_ + 1) * H * 65]))(tt),
                            incs=[(sSP, 16)])

    half_A(0)
    prev_half = {"sp": sSP.n, "dv": sDV.n}
    half_A(1)
    a_done_dv = sDV.n
    a_done_sp = sSP.n
    a_done_ac = sAC.n
    a_done_pe = sPE.n

    if stage == "a":
        if debug:
            prog.op("sync", lambda s: s.dma_start(out=tc_dbg[:, :], in_=Tc_sb[:, :]),
                    waits=[(sDV, a_done_dv)], incs=[(sOU, 16)])
            for nm in ("ekT_f", "fqT_f", "ek_f", "vp_f"):
                o, i = dbg_f32[nm]
                prog.op("gpsimd", (lambda o_, i_: lambda g: g.dma_start(out=o_[:, :], in_=i_[:, :]))(o, i),
                        waits=[(sSP, a_done_sp)], incs=[(sOU, 16)])
        prog.op("sync", lambda s: s.nop(),
                waits=[(sOU, sOU.n), (sSP, a_done_sp), (sDV, a_done_dv), (sAC, a_done_ac)])
        _finalize(nc, prog, guards)
        return nc

    prog.op("sync", lambda s: s.dma_start(out=ag_in[:, :], in_=Tc_sb[:, :]),
            waits=[(sDV, a_done_dv)], incs=[(sXN, 16)])
    prog.op("gpsimd",
            lambda g: g.collective_compute(
                "AllGather", ALU.bypass, replica_groups=[list(range(NCORES))],
                ins=[ag_in[:, :].opt()], outs=[ag_out[:, :].opt()]),
            waits=[(sXN, sXN.n)], incs=[(sCC, 1)])
    prog.op("vector", lambda v: v.memset(b_acc[:, :], 0.0),
            waits=[(sWF, WF_ALL), (sSP, a_done_sp)], incs=[(sDV, 1)])
    rd_free = [[], []]
    for rk in range(NCORES):
        par = rk % 2
        src = ag_out[rk * 64:(rk + 1) * 64, :].rearrange("r (two c) -> (r two) c", two=2)
        w = [(sCC, 1), (sSP, a_done_sp)] if rk == 0 else []
        w += rd_free[par]
        rd_free[par] = []
        prog.op("sync",
                (lambda s_, d_: lambda s: s.dma_start(out=d_, in_=s_))(src, b_rk[par][:, :]),
                waits=w, incs=[(sAG, 16)])
        agv = sAG.n
        prog.op("vector",
                (lambda p_, r_: lambda v: v.scalar_tensor_tensor(
                    b_acc[:, :], b_rk[p_][:, :],
                    ccmask_sb[:, r_:r_ + 1],
                    b_acc[:, :], ALU.mult, ALU.add))(par, rk),
                waits=[(sAG, agv), (sDV, sDV.n)], incs=[(sDV, 1)])
        rd_free[par] = [(sDV, sDV.n)]
    prog.op("sync", lambda s: s.dma_start(
        out=ag_in[:, :].rearrange("r (two c) -> (r two) c", two=2), in_=b_acc[:, :]),
            waits=[(sDV, sDV.n)], incs=[(sXN, 16)])
    prog.op("sync", lambda s: s.dma_start(out=S_sb[:, :], in_=ag_in[:, :]),
            waits=[(sXN, sXN.n)], incs=[(sXN, 16)])
    sinit_xn = sXN.n
    prog.op("scalar", lambda s: s.activation(S_bf[0:64, :], S_sb[:, :], AF.Copy),
            waits=[(sXN, sinit_xn)], incs=[(sAC, 1)])
    prog.op("scalar", lambda s: s.activation(S_bf[64:128, :], S_sb[:, :], AF.Copy),
            incs=[(sAC, 1)])
    sbf_ready = sAC.n

    if debug:
        prog.op("sync", lambda s: s.dma_start(out=s0_dbg[:, :], in_=S_sb[:, :]),
                waits=[(sXN, sinit_xn)], incs=[(sOU, 16)])
        prog.op("sync", lambda s: s.dma_start(out=tc_dbg[:, :], in_=Tc_sb[:, :]),
                incs=[(sOU, 16)])

    for ot in range(OT):
        prog.op("gpsimd",
                (lambda o: lambda g: g.dma_start(
                    out=c_woT[:, o * Dm:(o + 1) * Dm],
                    in_=woT_d[o * 128:(o + 1) * 128, :]))(ot),
                waits=[(sPE, a_done_pe), (sAC, a_done_ac)] if ot == 0 else [],
                incs=[(sWO, 16)])
    wo_ready = sWO.n

    ch_free = [[], []]

    def load_chunk(ch):
        par = ch % 2
        c0 = ch * CH
        w = list(ch_free[par]) + ([(sSP, a_done_sp)] if ch < 2 else [])
        ch_free[par] = []
        if ch < 2:
            w.append((sDV, sDV.n))
        prog.op("sync",
                (lambda p, c: lambda s: s.dma_start(
                    out=c_ekT[p].rearrange("p (o t) -> p o t", o=OT),
                    in_=ekT_d.rearrange("p (o t) -> p o t", o=OT)[:, :, c:c + CH]))(par, c0),
                waits=w, incs=[(sCH[par], 16)])
        prog.op("sync",
                (lambda p, c: lambda s: s.dma_start(
                    out=c_fqT[p].rearrange("p (o t) -> p o t", o=OT),
                    in_=fqT_d.rearrange("p (o t) -> p o t", o=OT)[:, :, c:c + CH]))(par, c0),
                incs=[(sCH[par], 16)])
        prog.op("sync",
                (lambda p, c: lambda s: s.dma_start(
                    out=c_ek[p][:, :], in_=ek_d[c:c + CH, :]))(par, c0),
                incs=[(sCH[par], 16)])
        prog.op("sync",
                (lambda p, c: lambda s: s.dma_start(
                    out=c_vp[p][:, :], in_=vp_d[c:c + CH, :]))(par, c0),
                incs=[(sCH[par], 16)])
        prog.op("sync",
                (lambda p, c: lambda s: s.dma_start(
                    out=c_xn[p][:, :], in_=xn_d[c:c + CH, :]))(par, c0),
                incs=[(sCH[par], 16)])

    load_chunk(0)
    load_chunk(1)

    sbf_wait = sbf_ready
    last = {"rc_dv": [0, 0], "mult_dv": [0, 0]}
    prev_wo_pe = 0
    tr_state = {"pe": 0, "sp": 0, "pending": None}
    prev_sq_ac = 0

    for ch in range(NCH):
        par = ch % 2
        chv = sCH[par].n
        if debug and stage == "c" and ch == 8:
            prog.op("gpsimd", (lambda p: lambda g: g.dma_start(out=sbf_dbg[:, :], in_=S_bf[:, :]))(par),
                    waits=[(sAC, sbf_wait)], incs=[(sOU, 16)])
            prog.op("gpsimd", (lambda p: lambda g: g.dma_start(out=fq8_dbg[:, :], in_=c_fqT[p][:, :]))(par),
                    waits=[(sCH[par], chv)], incs=[(sOU, 16)])
            prog.op("gpsimd", (lambda p: lambda g: g.dma_start(out=vp8_dbg[:, :], in_=c_vp[p][:, :]))(par),
                    incs=[(sOU, 16)])
        for q in range(4):
            pq = q % 2
            pw = []
            for b in range(8):
                pw += pe_wait_bank(b)
            if q == 0:
                pw += [(sCH[par], chv)]
                if ch == 0:
                    pw += [(sWF, WF_ALL)]
            for j in range(8):
                h = q * 8 + j
                ot, pbase = h // 2, (h % 2) * 64
                prog.op("tensor",
                        (lambda p, o, pbs, b: lambda t: t.matmul(
                            pb[b][:, 0:CH],
                            c_ekT[p][pbs:pbs + 64, o * CH:(o + 1) * CH],
                            c_fqT[p][pbs:pbs + 64, o * CH:(o + 1) * CH],
                            start=True, stop=True))(par, ot, pbase, j),
                        waits=pw if j == 0 else [], incs=[(sPE, 1)] if j == 7 else [])
            at_pe = sPE.n
            for j in range(8):
                h = q * 8 + j
                prog.op("vector",
                        (lambda b, hh: lambda v: v.tensor_tensor(
                            c_am[:, hh * CH:(hh + 1) * CH],
                            pb[b][:, 0:CH], mask_bf[:, :], ALU.mult))(j, h),
                        waits=[(sPE, at_pe)] if j == 0 else [], incs=[(sDV, 1)] if j == 7 else [])
            am_dv = sDV.n
            pw = [(sDV, am_dv)]
            if q == 0:
                pw += [(sAC, sbf_wait)]
                if ch == 0:
                    pw += [(sXN, sinit_xn)]
            for j in range(8):
                h = q * 8 + j
                ot, pbase = h // 2, (h % 2) * 64
                prog.op("tensor",
                        (lambda p, hh, b: lambda t: t.matmul(
                            pb[b][0:65, 0:CH],
                            c_vp[p][:, hh * 65:(hh + 1) * 65],
                            c_am[:, hh * CH:(hh + 1) * CH], start=True, stop=False))(par, h, j),
                        waits=pw if j == 0 else [])
                prog.op("tensor",
                        (lambda p, hh, o, pbs, b: lambda t: t.matmul(
                            pb[b][0:65, 0:CH],
                            S_bf[pbs:pbs + 64, hh * 65:(hh + 1) * 65],
                            c_fqT[p][pbs:pbs + 64, o * CH:(o + 1) * CH],
                            start=False, stop=False))(par, h, ot, pbase, j))
                prog.op("tensor",
                        (lambda b: lambda t: t.matmul(
                            pb[b][0:65, 0:CH], eps65_bf[:, :], ones_row_bf[:, 0:CH],
                            start=False, stop=True))(j),
                        incs=[(sPE, 1)] if j == 7 else [])
            yn_pe = sPE.n
            aw = [(sPE, yn_pe), (sDV, last["rc_dv"][pq])]
            for j in range(8):
                prog.op("scalar",
                        (lambda b, jj, p_: lambda s: s.activation(
                            c_dn[:, p_ * 8 * CH + jj * CH: p_ * 8 * CH + (jj + 1) * CH],
                            pb[b][64:65, 0:CH], AF.Copy))(j, j, pq),
                        waits=aw if j == 0 else [], incs=[(sAC, 1)] if j == 7 else [])
            dn_ac = sAC.n
            for j in range(8):
                prog.op("vector",
                        (lambda b, jj: lambda v: v.tensor_copy(
                            c_yn[0:64, jj * CH:(jj + 1) * CH], pb[b][0:64, 0:CH]))(j, j),
                        waits=[(sPE, yn_pe), (sDV, sDV.n)] if j == 0 else [],
                        incs=[(sDV, 1)] if j == 7 else [])
            ynsb_dv = sDV.n
            prog.op("vector",
                    (lambda p_: lambda v: v.reciprocal(
                        c_rc[:, p_ * 8 * CH:(p_ + 1) * 8 * CH],
                        c_dn[:, p_ * 8 * CH:(p_ + 1) * 8 * CH]))(pq),
                    waits=[(sAC, dn_ac), (sDV, sDV.n)], incs=[(sDV, 1)])
            rc_dv = sDV.n
            last["rc_dv"][pq] = rc_dv
            pw = [(sDV, ynsb_dv), (sDV, rc_dv)]
            for j in range(8):
                prog.op("tensor",
                        (lambda b, jj, p_: lambda t: t.matmul(
                            pb[b][0:64, 0:CH], ones_row_f32_hi[:, :],
                            c_rc[:, p_ * 8 * CH + jj * CH: p_ * 8 * CH + (jj + 1) * CH],
                            start=True, stop=True))(j, j, pq),
                        waits=pw if j == 0 else [], incs=[(sPE, 1)] if j == 7 else [])
            bc_pe = sPE.n
            aw = [(sPE, bc_pe), (sDV, last["mult_dv"][pq])]
            for j in range(8):
                prog.op("scalar",
                        (lambda b, jj, p_: lambda s: s.activation(
                            c_rcb[:, p_ * 8 * CH + jj * CH: p_ * 8 * CH + (jj + 1) * CH],
                            pb[b][0:64, 0:CH], AF.Copy))(j, j, pq),
                        waits=aw if j == 0 else [], incs=[(sAC, 1)] if j == 7 else [])
            rcb_ac = sAC.n
            for b in range(8):
                bank_free[b] = [(sAC, rcb_ac)]
            aw = [(sAC, rcb_ac)]
            if q == 0:
                aw.append((sPE, prev_wo_pe))
            for j in range(8):
                h = q * 8 + j
                ot, pbase = h // 2, (h % 2) * 64
                prog.op("vector",
                        (lambda jj, p_, o, pbs: lambda v: v.tensor_tensor(
                            c_yT[pbs:pbs + 64, o * CH:(o + 1) * CH],
                            c_yn[0:64, jj * CH:(jj + 1) * CH],
                            c_rcb[:, p_ * 8 * CH + jj * CH: p_ * 8 * CH + (jj + 1) * CH],
                            ALU.mult))(j, pq, ot, pbase),
                        waits=aw if j == 0 else [], incs=[(sDV, 1)] if j == 7 else [])
            last["mult_dv"][pq] = sDV.n
        heads_dv = sDV.n
        if tr_state["pending"] is not None:
            tr_state["pending"]()
            tr_state["pending"] = None
        if debug and stage == "c":
            prog.op("gpsimd",
                    (lambda c: lambda g: g.dma_start(
                        out=yT_dbg[:, c * OT * CH:(c + 1) * OT * CH], in_=c_yT[:, :]))(ch),
                    waits=[(sDV, heads_dv)], incs=[(sOU, 16)])

        for hg in range(8):
            bk = 6 + hg % 2
            pw = pe_wait_bank(bk)
            for j in range(4):
                h = hg * 4 + j
                prog.op("tensor",
                        (lambda p, h_, b, j_: lambda t: t.matmul(
                            pb[b][0:64, j_ * 65:(j_ + 1) * 65],
                            c_ek[p][:, h_ * FD:(h_ + 1) * FD],
                            c_vp[p][:, h_ * 65:(h_ + 1) * 65],
                            start=True, stop=True))(par, h, bk, j),
                        waits=pw if j == 0 else [],
                        incs=[(sPE, 1)] if j == 3 else [])
            mmv = sPE.n
            prog.op("vector",
                    (lambda b, g_: lambda v: v.tensor_tensor(
                        S_sb[:, g_ * 260:(g_ + 1) * 260],
                        pb[b][0:64, 0:260],
                        S_sb[:, g_ * 260:(g_ + 1) * 260], ALU.add))(bk, hg),
                    waits=[(sPE, mmv)], incs=[(sDV, 1)])
            bank_free[bk] = [(sDV, sDV.n)]
        states_pe = sPE.n
        if debug and stage == "c":
            prog.op("gpsimd",
                    (lambda c: lambda g: g.dma_start(
                        out=s_dbg[:, c * H * 65:(c + 1) * H * 65], in_=S_sb[:, :]))(ch),
                    waits=[(sDV, sDV.n)], incs=[(sOU, 16)])
        prog.op("scalar", lambda s: s.activation(S_bf[0:64, :], S_sb[:, :], AF.Copy),
                waits=[(sDV, sDV.n), (sPE, states_pe)], incs=[(sAC, 1)])
        prog.op("scalar", lambda s: s.activation(S_bf[64:128, :], S_sb[:, :], AF.Copy),
                incs=[(sAC, 1)])
        sbf_wait = sAC.n

        for ncx in range(4):
            bk = 4 + ncx % 2
            pw = pe_wait_bank(bk) + ([(sDV, heads_dv), (sWO, wo_ready)] if ncx == 0 else [])
            for ot in range(OT):
                prog.op("tensor",
                        (lambda o, n_, b: lambda t: t.matmul(
                            pb[b][:, :],
                            c_yT[:, o * CH:(o + 1) * CH],
                            c_woT[:, o * Dm + n_ * 512: o * Dm + (n_ + 1) * 512],
                            start=(o == 0), stop=(o == OT - 1)))(ot, ncx, bk),
                        waits=pw if ot == 0 else [],
                        incs=[(sPE, 1)] if ot == OT - 1 else [])
            mmv = sPE.n
            aw = [(sPE, mmv), (sCH[par], chv)]
            if ncx == 0:
                aw.append((sAC, prev_sq_ac))
            prog.op("vector",
                    (lambda p, n_, b: lambda v: v.tensor_tensor(
                        c_r1[:, n_ * 512:(n_ + 1) * 512], pb[b][:, :],
                        c_xn[p][:, n_ * 512:(n_ + 1) * 512], ALU.add))(par, ncx, bk),
                    waits=aw, incs=[(sDV, 1)])
            bank_free[bk] = [(sDV, sDV.n)]
        wo_pe = sPE.n
        prev_wo_pe = wo_pe
        r1_dv = sDV.n
        if debug and stage == "c" and ch == 0:
            prog.op("gpsimd", lambda g: g.dma_start(out=r1_dbg[:, :], in_=c_r1[:, :]),
                    waits=[(sDV, r1_dv)], incs=[(sOU, 16)])

        prog.op("vector", lambda v: v.reduce_sum(stats[:, 0:1], c_r1[:, :], axis=AX.X),
                incs=[(sDV, 1)])
        prog.op("vector", lambda v: v.tensor_scalar_mul(stats[:, 0:1], stats[:, 0:1], 1.0 / Dm),
                waits=[(sDV, sDV.n)], incs=[(sDV, 1)])
        prog.op("scalar", lambda s: s.activation(c_sq[:, :], c_r1[:, :], AF.Square,
                                                 accum_out=stats[:, 12:13]),
                waits=[(sDV, r1_dv)], incs=[(sAC, 1)])
        sq_ac = sAC.n
        prev_sq_ac = sq_ac
        prog.op("vector", lambda v: v.tensor_tensor(stats[:, 4:5], stats[:, 0:1], stats[:, 0:1], ALU.mult),
                waits=[(sDV, sDV.n)], incs=[(sDV, 1)])
        prog.op("vector", lambda v: v.tensor_scalar_add(stats[:, 4:5], stats[:, 4:5], -EPS_LN),
                waits=[(sDV, sDV.n)], incs=[(sDV, 1)])
        prog.op("vector", lambda v: v.scalar_tensor_tensor(
            stats[:, 8:9], stats[:, 12:13], 1.0 / Dm, stats[:, 4:5], ALU.mult, ALU.subtract),
                waits=[(sAC, sq_ac), (sDV, sDV.n)], incs=[(sDV, 1)])
        var_dv = sDV.n
        prog.op("scalar", lambda s: s.activation(stats[:, 16:17], stats[:, 8:9], AF.Ln),
                waits=[(sDV, var_dv)], incs=[(sAC, 1)])
        prog.op("scalar", lambda s: s.activation(stats[:, 20:21], stats[:, 16:17], AF.Exp, scale=misc_sb[:, 0:1]),
                waits=[(sAC, sAC.n)], incs=[(sAC, 1)])
        rstd_ac = sAC.n
        if debug and stage == "c" and ch == 0:
            prog.op("gpsimd", lambda g: g.dma_start(out=st_dbg[:, :], in_=stats[:, :]),
                    waits=[(sAC, rstd_ac)], incs=[(sOU, 16)])
        aw = [(sAC, rstd_ac), (sPE, tr_state["pe"])]
        prog.op("vector", lambda v: v.tensor_scalar(
            c_xa[:, :], c_r1[:, :], stats[:, 0:1], stats[:, 20:21], ALU.subtract, ALU.mult),
                waits=aw, incs=[(sDV, 1)])
        xa_dv = sDV.n

        def make_tr(ch_, xa_dv_):
            def emit_tr():
                global_unused = None
                for dt in range(DT):
                    bk = 6 + dt % 2
                    pw = pe_wait_bank(bk) + ([(sDV, xa_dv_)] if dt < 2 else [])
                    prog.op("tensor",
                            (lambda d, b: lambda t: t.transpose(
                                pb[b][:, 0:64].bitcast(BF16), c_xa[:, d * 128:(d + 1) * 128],
                                ident_bf[:, :]))(dt, bk),
                            waits=pw, incs=[(sPE, 1)])
                    mmv = sPE.n
                    aw2 = [(sPE, mmv)]
                    if dt == 0:
                        aw2.append((sSP, tr_state["sp"]))
                    prog.op("vector",
                            (lambda d, b: lambda v: v.tensor_copy(
                                c_xaT[:, d * CH:(d + 1) * CH], pb[b][:, 0:64].bitcast(BF16)))(dt, bk),
                            waits=aw2, incs=[(sDV, 1)])
                    bank_free[bk] = [(sDV, sDV.n)]
                tr_state["pe"] = sPE.n
                xat_dv = sDV.n
                prog.op("scalar",
                        (lambda c: lambda s: s.dma_start(
                            out=xaT_d.rearrange("p (d t) -> p d t", d=DT)[:, :, c * CH:(c + 1) * CH],
                            in_=c_xaT.rearrange("p (d t) -> p d t", d=DT)))(ch_),
                        waits=[(sDV, xat_dv)], incs=[(sSP, 16)])
                tr_state["sp"] = sSP.n
            return emit_tr
        tr_state["pending"] = make_tr(ch, xa_dv)

        if serial:
            barrier()
        ch_free[par] = [(sPE, states_pe), (sDV, r1_dv)]
        if ch + 2 < NCH:
            load_chunk(ch + 2)

    if tr_state["pending"] is not None:
        tr_state["pending"]()
        tr_state["pending"] = None
    c_done_sp = sSP.n
    c_done_pe = sPE.n
    c_done_dv = sDV.n
    c_done_ac = sAC.n

    if stage == "c":
        if debug:
            for nm in ("xaT_f", "fqT_f", "ekT_f"):
                o, i = dbg_f32[nm]
                prog.op("gpsimd", (lambda o_, i_: lambda g: g.dma_start(out=o_[:, :], in_=i_[:, :]))(o, i),
                        waits=[(sSP, c_done_sp)], incs=[(sOU, 16)])
        prog.op("sync", lambda s: s.nop(),
                waits=[(sSP, c_done_sp), (sDV, sDV.n), (sOU, sOU.n)])
        _finalize(nc, prog, guards)
        return nc

    wg_free = [[], []]
    wd_free = [[], []]
    xa_free = []
    gateup_pe_done = 0
    last_mult_dv = [0, 0]

    for tc in range(NTC):
        tb = tc * TCF
        w = list(xa_free) + ([(sSP, c_done_sp), (sPE, c_done_pe), (sDV, c_done_dv),
                              (sAC, c_done_ac)] if tc == 0 else [])
        xa_free = []
        prog.op("sync",
                (lambda t_: lambda s: s.dma_start(
                    out=d_xaT.rearrange("p (d t) -> p d t", d=DT),
                    in_=xaT_d.rearrange("p (d t) -> p d t", d=DT)[:, :, t_ * TCF:(t_ + 1) * TCF]))(tc),
                waits=w, incs=[(sXA, 16)])
        xa_ready = sXA.n

        pw = pe_wait_bank(4) + [(sXA, xa_ready)]
        for dt in range(DT):
            prog.op("tensor",
                    (lambda d: lambda t: t.matmul(
                        pb[4][0:1, 0:TCF], ones_col_bf[:, :],
                        d_xaT[:, d * TCF:(d + 1) * TCF],
                        start=(d == 0), stop=(d == DT - 1)))(dt),
                    waits=pw if dt == 0 else [],
                    incs=[(sPE, 1)] if dt == DT - 1 else [])
        mmv = sPE.n
        prog.op("scalar", lambda s: s.activation(d_xsum[:, :], pb[4][0:1, 0:TCF], AF.Copy),
                waits=[(sPE, mmv)], incs=[(sAC, 1)])
        xsum_ac = sAC.n
        bank_free[4] = [(sAC, xsum_ac)]
        if debug and stage == "full" and tc == 0:
            prog.op("gpsimd", lambda g: g.dma_start(out=xs_dbg[:, :], in_=d_xsum[:, :]),
                    waits=[(sAC, xsum_ac)], incs=[(sOU, 16)])

        for ic in range(16):
            par = ic % 2
            for wdram, buf in ((wg8_d, d_wg[par]), (wu8_d, d_wu[par])):
                w2 = list(wg_free[par]) if wdram is wg8_d else []
                if tc == 0 and ic < 2 and wdram is wg8_d:
                    w2 += [(sPE, c_done_pe), (sDV, c_done_dv), (sAC, c_done_ac),
                           (sSP, c_done_sp)]
                prog.op("gpsimd",
                        (lambda wd_, b_, i_: lambda g: g.dma_start(
                            out=b_.rearrange("p (ht i) -> p ht i", ht=16),
                            in_=wd_[:, i_ * TCF:(i_ + 1) * TCF].rearrange(
                                "(ht p) i -> p ht i", p=128)))(wdram, buf, ic),
                        waits=w2,
                        incs=[(sWG[par], 16)])
            wg_free[par] = []
            wg_ready = sWG[par].n
            if debug and stage == "full" and tc == 0 and ic == 0:
                prog.op("gpsimd", lambda g: g.dma_start(out=wub_dbg[:, :], in_=d_wu[0][:, :]),
                        waits=[(sWG[0], wg_ready)], incs=[(sOU, 16)])
            for isub in range(4):
                it = ic * 4 + isub
                sp_ = isub % 2
                for bk, buf, scl, act, dst in (
                        (isub % 2, d_wg[par], d_scg, AF.Silu, d_sg[sp_]),
                        (2 + isub % 2, d_wu[par], d_scu, AF.Copy, d_uu[sp_])):
                    pw = pe_wait_bank(bk)
                    if isub == 0 and bk < 2:
                        pw += [(sWG[par], wg_ready), (sAC, xsum_ac)]
                    elif isub == 0:
                        pw += [(sWG[par], wg_ready)]
                    for ht in range(16):
                        prog.op("tensor",
                                (lambda b, bf_, h_, s_: lambda t: t.matmul(
                                    pb[b][:, 0:TCF],
                                    bf_[:, h_ * TCF + s_ * 128: h_ * TCF + s_ * 128 + 128],
                                    d_xaT[:, h_ * TCF:(h_ + 1) * TCF],
                                    start=(h_ == 0), stop=False))(bk, buf, ht, isub),
                                waits=pw if ht == 0 else [])
                    prog.op("tensor",
                            (lambda b: lambda t: t.matmul(
                                pb[b][:, 0:TCF], m128_row_bf[:, :], d_xsum[:, :],
                                start=False, stop=True))(bk),
                            incs=[(sPE, 1)])
                    mmv = sPE.n
                    rawdump = (debug and stage == "full" and tc == 0 and ic == 0
                               and isub == 0 and bk >= 2)
                    if rawdump:
                        prog.op("scalar",
                                (lambda b, a_, ds_: lambda s: s.activation(
                                    ds_[:, :], pb[b][:, 0:TCF], a_))(bk, act, dst),
                                waits=[(sPE, mmv), (sDV, last_mult_dv[sp_])], incs=[(sAC, 1)])
                    else:
                        prog.op("scalar",
                                (lambda b, sc_, a_, ds_, i_: lambda s: s.activation(
                                    ds_[:, :], pb[b][:, 0:TCF], a_,
                                    scale=sc_[:, i_: i_ + 1]))(bk, scl, act, dst, it),
                                waits=[(sPE, mmv), (sDV, last_mult_dv[sp_])], incs=[(sAC, 1)])
                    bank_free[bk] = [(sAC, sAC.n)]
                if debug and stage == "full" and tc == 0 and it == 0:
                    prog.op("gpsimd", (lambda s_: lambda g: g.dma_start(out=sg_dbg[:, :], in_=d_sg[s_][:, :]))(sp_),
                            waits=[(sAC, sAC.n)], incs=[(sOU, 16)])
                    prog.op("gpsimd", (lambda s_: lambda g: g.dma_start(out=uu_dbg[:, :], in_=d_uu[s_][:, :]))(sp_),
                            incs=[(sOU, 16)])
                    prog.op("gpsimd", lambda g: g.dma_start(out=xat_dbg[:, :], in_=d_xaT[:, :]),
                            incs=[(sOU, 16)])
                prog.op("vector",
                        (lambda i_, s_: lambda v: v.tensor_tensor(
                            d_h[:, i_ * TCF:(i_ + 1) * TCF], d_sg[s_][:, :], d_uu[s_][:, :],
                            ALU.mult))(it, sp_),
                        waits=[(sAC, sAC.n), (sOU, sOU.n) if (debug and stage == "full" and tc == 0 and it == 0) else (sAC, 0)], incs=[(sDV, 1)])
                last_mult_dv[sp_] = sDV.n
            wg_free[par] = [(sPE, sPE.n), (sAC, sAC.n)]
        h_dv = sDV.n
        gateup_pe_done = sPE.n
        if debug and stage == "full" and tc == 0:
            prog.op("gpsimd", lambda g: g.dma_start(out=h_dbg[:, :], in_=d_h[:, 0:8 * TCF]),
                    waits=[(sDV, h_dv)], incs=[(sOU, 16)])

        pw = pe_wait_bank(4) + [(sDV, h_dv)]
        for it in range(64):
            prog.op("tensor",
                    (lambda i_: lambda t: t.matmul(
                        pb[4][0:1, 0:TCF], ones_col_bf[:, :],
                        d_h[:, i_ * TCF:(i_ + 1) * TCF],
                        start=(i_ == 0), stop=(i_ == 63)))(it),
                    waits=pw if it == 0 else [],
                    incs=[(sPE, 1)] if it == 63 else [])
        mmv = sPE.n
        prog.op("scalar", lambda s: s.activation(d_hsum[:, :], pb[4][0:1, 0:TCF], AF.Copy),
                waits=[(sPE, mmv)], incs=[(sAC, 1)])
        hsum_ac = sAC.n
        bank_free[4] = [(sAC, hsum_ac)]

        for dh in range(2):
            pw_all = []
            for dto in range(8):
                pw_all += pe_wait_bank(dto)
            first_mm = [True] * 8
            for ic in range(16):
                par = ic % 2
                w = list(wd_free[par])
                if dh == 0 and ic < 2:
                    w += [(sPE, gateup_pe_done)]
                prog.op("gpsimd",
                        (lambda i_, d_, p_: lambda g: g.dma_start(
                            out=d_wd[p_].rearrange("p (itl d) -> p itl d", itl=4),
                            in_=wd8_d[i_ * TCF:(i_ + 1) * TCF,
                                      d_ * 1024:(d_ + 1) * 1024].rearrange(
                                "(itl p) d -> p itl d", p=128)))(ic, dh, par),
                        waits=w, incs=[(sWD[par], 16)])
                wd_free[par] = []
                wd_ready = sWD[par].n
                for itl in range(4):
                    it = ic * 4 + itl
                    for dto in range(8):
                        pw = []
                        if itl == 0 and dto == 0:
                            pw = [(sWD[par], wd_ready), (sDV, h_dv)] + (pw_all if ic == 0 else [])
                        prog.op("tensor",
                                (lambda p_, il, d_o, i_, st: lambda t: t.matmul(
                                    pb[d_o][:, 0:TCF],
                                    d_wd[p_][:, il * 1024 + d_o * 128: il * 1024 + (d_o + 1) * 128],
                                    d_h[:, i_ * TCF:(i_ + 1) * TCF],
                                    start=st, stop=False))(par, itl, dto, it, first_mm[dto]),
                                waits=pw,
                                incs=[(sPE, 1)] if (itl == 3 and dto == 7) else [])
                        first_mm[dto] = False
                wd_free[par] = [(sPE, sPE.n)]
            for dto in range(8):
                prog.op("tensor",
                        (lambda d_o: lambda t: t.matmul(
                            pb[d_o][:, 0:TCF], m128_row_bf[:, :], d_hsum[:, :],
                            start=False, stop=True))(dto),
                        waits=[(sAC, hsum_ac)] if dto == 0 else [],
                        incs=[(sPE, 1)] if dto == 7 else [])
            mmv = sPE.n
            for dto in range(8):
                dt = dh * 8 + dto
                prog.op("vector",
                        (lambda d_o, d_: lambda v: v.scalar_tensor_tensor(
                            d_r2T[:, d_ * TCF:(d_ + 1) * TCF], pb[d_o][:, 0:TCF],
                            d_scd[:, d_: d_ + 1], d_xaT[:, d_ * TCF:(d_ + 1) * TCF],
                            ALU.mult, ALU.add))(dto, dt),
                        waits=[(sPE, mmv)] if dto == 0 else [], incs=[(sDV, 1)])
                bank_free[dto] = [(sDV, sDV.n)]
        r2_dv = sDV.n
        if debug and stage == "full" and tc == 0:
            prog.op("gpsimd", lambda g: g.dma_start(out=r2_dbg[:, :], in_=d_r2T[:, :]),
                    waits=[(sDV, r2_dv)], incs=[(sOU, 16)])

        for ttl in range(TCF // 128):
            for dt in range(DT):
                bk = dt % 2
                pw = pe_wait_bank(bk) + ([(sDV, r2_dv)] if (ttl == 0 and dt < 2) else [])
                prog.op("tensor",
                        (lambda d_, tl, b: lambda t: t.transpose(
                            pb[b][:, 0:128], d_r2T[:, d_ * TCF + tl * 128: d_ * TCF + (tl + 1) * 128],
                            ident_f32[:, :]))(dt, ttl, bk),
                        waits=pw, incs=[(sPE, 1)])
                mmv = sPE.n
                aw = [(sPE, mmv)]
                if dt == 0:
                    aw.append((sAC, prev_sq_ac))
                prog.op("vector",
                        (lambda d_, b: lambda v: v.tensor_copy(
                            d_r2t[:, d_ * 128:(d_ + 1) * 128], pb[b][:, 0:128]))(dt, bk),
                        waits=aw, incs=[(sDV, 1)])
                bank_free[bk] = [(sDV, sDV.n)]
            tile_dv = sDV.n
            prog.op("vector", lambda v: v.reduce_sum(stats[:, 24:25], d_r2t[:, :], axis=AX.X),
                    incs=[(sDV, 1)])
            prog.op("vector", lambda v: v.tensor_scalar_mul(stats[:, 24:25], stats[:, 24:25], 1.0 / Dm),
                    waits=[(sDV, sDV.n)], incs=[(sDV, 1)])
            prog.op("scalar", lambda s: s.activation(d_sq[:, :], d_r2t[:, :], AF.Square,
                                                     accum_out=stats[:, 36:37]),
                    waits=[(sDV, tile_dv), (sOU, sOU.n)], incs=[(sAC, 1)])
            sq_ac = sAC.n
            prev_sq_ac = sq_ac
            prog.op("vector", lambda v: v.tensor_tensor(stats[:, 28:29], stats[:, 24:25], stats[:, 24:25], ALU.mult),
                    waits=[(sDV, sDV.n)], incs=[(sDV, 1)])
            prog.op("vector", lambda v: v.tensor_scalar_add(stats[:, 28:29], stats[:, 28:29], -EPS_LN),
                    waits=[(sDV, sDV.n)], incs=[(sDV, 1)])
            prog.op("vector", lambda v: v.scalar_tensor_tensor(
                stats[:, 32:33], stats[:, 36:37], 1.0 / Dm, stats[:, 28:29], ALU.mult, ALU.subtract),
                    waits=[(sAC, sq_ac), (sDV, sDV.n)], incs=[(sDV, 1)])
            var_dv = sDV.n
            prog.op("scalar", lambda s: s.activation(stats[:, 40:41], stats[:, 32:33], AF.Ln),
                    waits=[(sDV, var_dv)], incs=[(sAC, 1)])
            prog.op("scalar", lambda s: s.activation(stats[:, 44:45], stats[:, 40:41], AF.Exp, scale=misc_sb[:, 0:1]),
                    waits=[(sAC, sAC.n)], incs=[(sAC, 1)])
            rstd_ac = sAC.n
            prog.op("vector", lambda v: v.tensor_scalar(
                d_sq[:, :], d_r2t[:, :], stats[:, 24:25], stats[:, 44:45], ALU.subtract, ALU.mult),
                    waits=[(sAC, rstd_ac)], incs=[(sDV, 1)])
            out_dv = sDV.n
            trow = tb + ttl * 128
            prog.op("sync",
                    (lambda r: lambda s: s.dma_start(
                        out=out_d[r:r + 128, :], in_=d_sq[:, :]))(trow),
                    waits=[(sDV, out_dv)], incs=[(sOU, 16)])
        xa_free = [(sPE, sPE.n), (sDV, sDV.n), (sOU, sOU.n)]
        wg_free[0] = [(sPE, sPE.n), (sDV, sDV.n)]
        wg_free[1] = [(sPE, sPE.n), (sDV, sDV.n)]

    prog.op("sync", lambda s: s.nop(), waits=[(sOU, sOU.n), (sSP, sSP.n)])

    _finalize(nc, prog, guards)
    return nc


def _finalize(nc, prog, guards):
    blk = nc.Block()
    block = blk.__enter__()
    try:
        block.sync(lambda e: prog.replay("sync", e))
        block.tensor(lambda e: prog.replay("tensor", e))
        block.vector(lambda e: prog.replay("vector", e))
        block.scalar(lambda e: prog.replay("scalar", e))
        block.gpsimd(lambda e: prog.replay("gpsimd", e))
    finally:
        blk.__exit__(None, None, None)
        for g in reversed(guards):
            g.__exit__(None, None, None)


_NC_CACHE = {}


def _get_nc():
    key = (STAGE, DEBUG, SERIAL)
    if key not in _NC_CACHE:
        _NC_CACHE[key] = build_nc(*key)
    return _NC_CACHE[key]


def _u8_view(a_int32):
    a = np.ascontiguousarray(a_int32)
    return np.ascontiguousarray(a.view(np.uint8).reshape(a.shape + (4,))[..., 0])


def make_in_maps(inputs):
    x = np.asarray(inputs["L0_attn_hidden_states"], np.float32)
    wqT = np.ascontiguousarray(np.asarray(inputs["L0_attn_Wq"], np.float32).T)
    wkT = np.ascontiguousarray(np.asarray(inputs["L0_attn_Wk"], np.float32).T)
    wvT = np.ascontiguousarray(np.asarray(inputs["L0_attn_Wv"], np.float32).T)
    woT = np.ascontiguousarray(np.asarray(inputs["L0_attn_Wo"], np.float32).T)
    wfq = np.ascontiguousarray(np.asarray(inputs["L0_attn_Wfq"], np.float32).reshape(H * HD, FD))
    wfk = np.ascontiguousarray(np.asarray(inputs["L0_attn_Wfk"], np.float32).reshape(H * HD, FD))
    wg8 = _u8_view(np.asarray(inputs["L0_ffn_gate_w"]).T)
    wu8 = _u8_view(np.asarray(inputs["L0_ffn_up_w"]).T)
    wd8 = _u8_view(np.asarray(inputs["L0_ffn_down_w"]).T)
    scg = np.ascontiguousarray(np.asarray(inputs["L0_ffn_gate_scale"], np.float32).reshape(-1, 128).T)
    scu = np.ascontiguousarray(np.asarray(inputs["L0_ffn_up_scale"], np.float32).reshape(-1, 128).T)
    scd = np.ascontiguousarray(np.asarray(inputs["L0_ffn_down_scale"], np.float32).reshape(-1, 128).T)
    ident = np.eye(128, dtype=np.float32)
    maskA = np.triu(np.ones((128, 128), np.float32))
    onesrow = np.ones((1, 128), np.float32)
    m128row = np.full((1, 128), -128.0, np.float32)
    onescol = np.ones((128, 1), np.float32)
    misc = np.zeros((128, 2), np.float32)
    misc[:, 0] = -0.5
    eps65 = np.zeros((1, 65), np.float32)
    eps65[0, 64] = EPS_ATTN

    in_maps = []
    for c in range(NCORES):
        b, q = c // 4, c % 4
        t0 = q * T
        xs = x[b, t0:t0 + T, :]
        ccm = np.zeros((128, 8), np.float32)
        for j in range(NCORES):
            if j // 4 == b and (j % 4) < q:
                ccm[:, j] = 1.0
        in_maps.append({
            "xT": np.ascontiguousarray(xs.T),
            "xn": np.ascontiguousarray(xs),
            "wqT": wqT, "wkT": wkT, "wvT": wvT, "woT": woT,
            "wfq": wfq, "wfk": wfk,
            "wg8": wg8, "wu8": wu8, "wd8": wd8,
            "scg": scg, "scu": scu, "scd": scd,
            "identf": ident, "maskA": maskA,
            "onesrow": onesrow, "m128row": m128row, "onescol": onescol,
            "ccmask": ccm, "misc": misc, "eps65": eps65,
        })
    return in_maps


def kernel(**inputs):
    nc = _get_nc()
    in_maps = make_in_maps(inputs)
    res = run_bass_kernel_spmd(nc, in_maps, core_ids=list(range(NCORES)))
    out = np.empty((Bb, L, Dm), np.float32)
    for c in range(NCORES):
        b, q = c // 4, c % 4
        out[b, q * T:(q + 1) * T, :] = res.results[c]["out"]
    return out


# version 08 (final)
# speedup vs baseline: 1.0732x; 1.0054x over previous
import numpy as np

import concourse.bass as bass
import concourse.mybir as mybir
from concourse.bass_utils import run_bass_kernel_spmd

F32 = mybir.dt.float32
BF16 = mybir.dt.bfloat16
U8 = mybir.dt.uint8
AF = mybir.ActivationFunctionType
ALU = mybir.AluOpType
AX = mybir.AxisListType

NCORES = 8
Bb, L, Dm = 2, 8192, 2048
H, HD, FD = 32, 64, 64
Ii = 8192
T = 2048
HF = 1024
CH = 128
NCH = T // CH
DT = Dm // 128
OT = Dm // 128
TCF = 512
NTC = T // TCF
EPS_ATTN = 1e-12
EPS_LN = 1e-5

STAGE = "full"
DEBUG = False
SERIAL = False


class Sem:
    def __init__(self, h):
        self.h = h
        self.n = 0


class Prog:
    ENGINES = ("sync", "tensor", "vector", "scalar", "gpsimd")

    def __init__(self):
        self.ops = {e: [] for e in self.ENGINES}

    def op(self, eng, builder, waits=(), incs=()):
        waits = [(s, v) for (s, v) in waits if v > 0]
        incs = [(s, n) for (s, n) in incs]
        for s, n in incs:
            s.n += n
        self.ops[eng].append((builder, waits, incs))

    def replay(self, name, eng):
        for builder, waits, incs in self.ops[name]:
            for s, v in waits:
                eng.wait_ge(s.h, v)
            ins = builder(eng)
            if incs:
                assert ins is not None
                for s, n in incs:
                    ins = ins.then_inc(s.h, n)


class Arena:

    def __init__(self, tile, base=0):
        self.tile = tile
        self.off = base

    def alloc(self, parts, elems, dtype, align=512):
        esz = {F32: 4, BF16: 2, U8: 1}[dtype]
        self.off = (self.off + align - 1) // align * align
        nbytes = elems * esz
        ap = self.tile[0:parts, self.off:self.off + nbytes]
        if dtype != U8:
            ap = ap.bitcast(dtype)
        self.off += nbytes
        return ap


ARENA_B = 204544


def build_nc(stage=STAGE, debug=DEBUG, serial=SERIAL):
    nc = bass.Bass("TRN2", target_bir_lowering=False)

    def din(name, shape, dt):
        return nc.dram_tensor(name, shape, dt, kind="ExternalInput").ap()

    xT_d = din("xT", [Dm, T], F32)
    xn_d = din("xn", [T, Dm], F32)
    wqT_d = din("wqT", [Dm, Dm], F32)
    wkT_d = din("wkT", [Dm, Dm], F32)
    wvT_d = din("wvT", [Dm, Dm], F32)
    woT_d = din("woT", [Dm, Dm], F32)
    wfq_d = din("wfq", [H * HD, FD], F32)
    wfk_d = din("wfk", [H * HD, FD], F32)
    wg8_d = din("wg8", [Dm, Ii], U8)
    wu8_d = din("wu8", [Dm, Ii], U8)
    wd8_d = din("wd8", [Ii, Dm], U8)
    scg_d = din("scg", [128, Ii // 128], F32)
    scu_d = din("scu", [128, Ii // 128], F32)
    scd_d = din("scd", [128, Dm // 128], F32)
    ident_d = din("identf", [128, 128], F32)
    maskA_d = din("maskA", [128, 128], F32)
    onesrow_d = din("onesrow", [1, 128], F32)
    m128row_d = din("m128row", [1, 128], F32)
    onescol_d = din("onescol", [128, 1], F32)
    ccmask_d = din("ccmask", [128, 8], F32)
    misc_d = din("misc", [128, 2], F32)
    eps65_d = din("eps65", [1, 65], F32)

    out_d = nc.dram_tensor("out", [T, Dm], F32, kind="ExternalOutput").ap()

    ekT_d = nc.dram_tensor("ekT_s", [128, OT * T], BF16).ap()
    fqT_d = nc.dram_tensor("fqT_s", [128, OT * T], BF16).ap()
    ek_d = nc.dram_tensor("ek_s", [T, H * FD], BF16).ap()
    vp_d = nc.dram_tensor("vp_s", [T, H * 65], BF16).ap()
    xaT_d = nc.dram_tensor("xaT_s", [128, DT * T], BF16).ap()
    yT_dbg = nc.dram_tensor("yT_dbg", [128, NCH * OT * CH], F32, kind="ExternalOutput").ap() if (debug and stage == "c") else None
    h_dbg = nc.dram_tensor("h_dbg", [128, 8 * TCF], F32, kind="ExternalOutput").ap() if (debug and stage == "full") else None
    r2_dbg = nc.dram_tensor("r2_dbg", [128, DT * TCF], F32, kind="ExternalOutput").ap() if (debug and stage == "full") else None
    xs_dbg = nc.dram_tensor("xs_dbg", [1, TCF], F32, kind="ExternalOutput").ap() if (debug and stage == "full") else None
    sg_dbg = nc.dram_tensor("sg_dbg", [128, TCF], F32, kind="ExternalOutput").ap() if (debug and stage == "full") else None
    wub_dbg = nc.dram_tensor("wub_dbg", [128, 16 * TCF], F32, kind="ExternalOutput").ap() if (debug and stage == "full") else None
    ps_dbg = nc.dram_tensor("ps_dbg", [128, TCF], F32, kind="ExternalOutput").ap() if (debug and stage == "full") else None
    psg_sb = None
    uu_dbg = nc.dram_tensor("uu_dbg", [128, TCF], F32, kind="ExternalOutput").ap() if (debug and stage == "full") else None
    xat_dbg = nc.dram_tensor("xat_dbg", [128, DT * TCF], F32, kind="ExternalOutput").ap() if (debug and stage == "full") else None
    r1_dbg = nc.dram_tensor("r1_dbg", [128, Dm], F32, kind="ExternalOutput").ap() if debug else None
    s_dbg = nc.dram_tensor("s_dbg", [64, NCH * H * 65], F32, kind="ExternalOutput").ap() if debug else None
    am8_dbg = nc.dram_tensor("am8_dbg", [128, 4 * CH], F32, kind="ExternalOutput").ap() if debug else None
    sbf_dbg = nc.dram_tensor("sbf_dbg", [128, H * 65], F32, kind="ExternalOutput").ap() if debug else None
    fq8_dbg = nc.dram_tensor("fq8_dbg", [128, OT * CH], F32, kind="ExternalOutput").ap() if debug else None
    vp8_dbg = nc.dram_tensor("vp8_dbg", [128, H * 65], F32, kind="ExternalOutput").ap() if debug else None
    st_dbg = nc.dram_tensor("st_dbg", [128, 48], F32, kind="ExternalOutput").ap() if debug else None
    am_dbg = nc.dram_tensor("am_dbg", [128, 4 * CH], F32, kind="ExternalOutput").ap() if debug else None
    dn_dbg = nc.dram_tensor("dn_dbg", [1, 4 * CH], F32, kind="ExternalOutput").ap() if debug else None
    dbg_f32 = {}
    if debug:
        for nm, src_ap in (("ekT_f", ekT_d), ("fqT_f", fqT_d), ("ek_f", ek_d),
                           ("vp_f", vp_d), ("xaT_f", xaT_d)):
            dbg_f32[nm] = (nc.dram_tensor(nm, list(src_ap.shape), F32,
                                          kind="ExternalOutput").ap(), src_ap)
    tc_dbg = nc.dram_tensor("tc_dbg", [64, H * 65], F32, kind="ExternalOutput").ap() if debug else None
    s0_dbg = nc.dram_tensor("s0_dbg", [64, H * 65], F32, kind="ExternalOutput").ap() if debug else None

    ag_in = nc.dram_tensor("ag_in", [64, H * 65], F32).ap()
    ag_out = nc.dram_tensor("ag_out", [NCORES * 64, H * 65], F32, addr_space="Shared").ap()

    prog = Prog()
    guards = []

    def enter(g):
        guards.append(g)
        return g.__enter__()

    arena_t = enter(nc.sbuf_tensor("arena", [128, ARENA_B], U8))
    pb = [enter(nc.psum_tensor(f"pb{k}", [128, 512], F32)) for k in range(8)]

    def S(name):
        return Sem(enter(nc.semaphore(name)))

    sPE, sDV, sAC = S("sPE"), S("sDV"), S("sAC")
    sXT = S("sXT")
    sWT = [S("sWT0"), S("sWT1")]
    sWV = [S("sWV0"), S("sWV1")]
    sWF = S("sWF")
    sXN = S("sXN")
    sSP = S("sSP")
    sCH = [S("sCH0"), S("sCH1")]
    sWO = S("sWO")
    sCC = S("sCC")
    sAG = S("sAG")
    sWG = [S("sWG0"), S("sWG1")]
    sWD = [S("sWD0"), S("sWD1")]
    sXA = S("sXA")
    sOU = S("sOU")

    def barrier():
        prog.op("tensor", lambda t: t.nop(), incs=[(sPE, 1)])
        prog.op("vector", lambda v: v.nop(), incs=[(sDV, 1)])
        prog.op("scalar", lambda s: s.nop(), incs=[(sAC, 1)])
        w = [(sPE, sPE.n), (sDV, sDV.n), (sAC, sAC.n), (sOU, sOU.n), (sSP, sSP.n)]
        for e in ("tensor", "vector", "scalar", "sync", "gpsimd"):
            prog.op(e, lambda x: x.nop(), waits=list(w))

    A = Arena(arena_t[:, :], 0)
    ident_bf = A.alloc(128, 128, BF16)
    ident_f32 = A.alloc(128, 128, F32)
    mask_bf = A.alloc(128, 128, BF16)
    ones_row_bf = A.alloc(1, 128, BF16)
    ones_row_f32 = A.alloc(1, 128, F32)
    ones_hi_t = A.alloc(128, 64, F32)
    ones_row_f32_hi = ones_hi_t[64:65, :]
    m128_row_bf = A.alloc(1, 128, BF16)
    ones_col_bf = A.alloc(128, 1, BF16)
    ccmask_sb = A.alloc(128, 8, F32)
    misc_sb = A.alloc(128, 2, F32)
    eps65_bf = A.alloc(1, 65, BF16)
    stats = A.alloc(128, 48, F32)
    S_sb = A.alloc(64, H * 65, F32)
    S_bf = A.alloc(128, H * 65, BF16)
    Tc_sb = A.alloc(64, H * 65, F32)
    d_scg = A.alloc(128, 64, F32)
    d_scu = A.alloc(128, 64, F32)
    d_scd = A.alloc(128, 16, F32)
    pers_end = A.off

    A.off = pers_end
    a_xT = A.alloc(128, DT * HF, BF16)
    a_kqT = A.alloc(128, OT * HF, BF16)
    a_ekT_st = A.alloc(128, 2 * HF, BF16)
    a_fqT_st = A.alloc(128, 2 * HF, BF16)
    a_ek = A.alloc(128, 8 * H * FD, BF16)
    a_vp = A.alloc(128, 8 * H * 65, BF16)
    a_wT = A.alloc(128, 2 * DT * 128, BF16)
    a_wV0 = A.alloc(128, DT * 512, BF16)
    a_wfq = A.alloc(128, H * FD, BF16)
    a_wfk = A.alloc(128, H * FD, BF16)
    a_sk = A.alloc(128, 8 * H, F32)
    a_rsk = A.alloc(128, 8 * H, F32)
    assert A.off <= ARENA_B, A.off
    a_wV1 = a_kqT[:, 0:DT * 512]

    A.off = pers_end
    c_woT = A.alloc(128, OT * Dm, BF16)
    c_ekT = [A.alloc(128, OT * CH, BF16) for _ in range(2)]
    c_fqT = [A.alloc(128, OT * CH, BF16) for _ in range(2)]
    c_ek = [A.alloc(128, H * FD, BF16) for _ in range(2)]
    c_vp = [A.alloc(128, H * 65, BF16) for _ in range(2)]
    c_xn = [A.alloc(128, Dm, F32) for _ in range(2)]
    c_am = A.alloc(128, H * CH, BF16)
    c_dnrc = A.alloc(128, 4096, F32)
    c_rcb = c_dnrc[0:64, 0:2048]
    c_dn = c_dnrc[64:65, 0:2048]
    c_rc = c_dnrc[64:65, 2048:4096]
    c_yn = A.alloc(128, 8 * CH, BF16)
    c_yT = A.alloc(128, OT * CH, BF16)
    c_r1 = A.alloc(128, Dm, F32)
    c_sq = A.alloc(128, Dm, BF16)
    c_xa = A.alloc(128, Dm, BF16)
    c_xaT = A.alloc(128, DT * CH, BF16)
    assert A.off <= ARENA_B, A.off
    b_rk = [c_r1[:, 0:1040], c_xn[0][:, 0:1040]]
    b_acc = c_xn[1][:, 0:1040]

    A.off = pers_end
    off_xa = A.off
    d_xaT = A.alloc(128, DT * TCF, BF16)
    end_xa = A.off
    A.off = off_xa
    d_r2t = A.alloc(128, Dm, F32)
    d_sq = A.alloc(128, Dm, F32)
    assert A.off == end_xa
    d_wg = [A.alloc(128, 16 * TCF, BF16) for _ in range(2)]
    off_wu = A.off
    d_wu = [A.alloc(128, 16 * TCF, BF16) for _ in range(2)]
    end_wu = A.off
    A.off = off_wu
    d_r2T = A.alloc(128, DT * TCF, F32)
    assert A.off == end_wu
    d_h = A.alloc(128, 64 * TCF, BF16)
    d_sg = [A.alloc(128, TCF, BF16) for _ in range(2)]
    d_uu = [A.alloc(128, TCF, BF16) for _ in range(2)]
    d_xsum = A.alloc(1, TCF, BF16)
    d_hsum = A.alloc(1, TCF, BF16)
    d_dbgst = A.alloc(128, TCF, F32) if debug else None
    assert A.off <= ARENA_B, A.off
    d_wd = [d_wg[0][:, 0:4096], d_wg[1][:, 0:4096]]

    def ld(dst, src):
        return lambda g: g.dma_start(out=dst, in_=src)

    for dst, src in (
            (ident_bf[:, :], ident_d[:, :]),
            (ident_f32[:, :], ident_d[:, :]),
            (mask_bf[:, :], maskA_d[:, :]),
            (ones_row_bf[:, :], onesrow_d[:, :]),
            (ones_row_f32[:, :], onesrow_d[:, :]),
            (ones_row_f32_hi[:, :], onesrow_d[:, 0:64]),
            (m128_row_bf[:, :], m128row_d[:, :]),
            (ones_col_bf[:, :], onescol_d[:, :]),
            (ccmask_sb[:, :], ccmask_d[:, :]),
            (misc_sb[:, :], misc_d[:, :]),
            (eps65_bf[:, :], eps65_d[:, :]),
            (a_wfq[0:64, :].rearrange("p (h f) -> p h f", h=H),
             wfq_d.rearrange("(h p) f -> p h f", p=64)),
            (a_wfq[64:128, :].rearrange("p (h f) -> p h f", h=H),
             wfq_d.rearrange("(h p) f -> p h f", p=64)),
            (a_wfk[0:64, :].rearrange("p (h f) -> p h f", h=H),
             wfk_d.rearrange("(h p) f -> p h f", p=64)),
            (a_wfk[64:128, :].rearrange("p (h f) -> p h f", h=H),
             wfk_d.rearrange("(h p) f -> p h f", p=64)),
            (d_scg[:, :], scg_d[:, :]),
            (d_scu[:, :], scu_d[:, :]),
            (d_scd[:, :], scd_d[:, :]),
    ):
        prog.op("gpsimd", ld(dst, src), incs=[(sWF, 16)])
    WF_ALL = sWF.n

    prog.op("vector", lambda v: v.memset(Tc_sb[:, :], 0.0), incs=[(sDV, 1)])

    bank_free = [[] for _ in range(8)]

    def pe_wait_bank(k):
        w = bank_free[k]
        bank_free[k] = []
        return w

    wT_free = [[], []]
    wV_free = [[], []]
    ekst_free = [[], []]
    fqst_free = [[], []]
    prev_half = {"sp": 0, "dv": 0}

    def half_A(hf):
        t0 = hf * HF
        xw = [(sPE, sPE.n)] if hf else []
        for j in range(4):
            src = xT_d[:, t0:t0 + HF].rearrange("(dt p) t -> p dt t", p=128)[:, j * 4:(j + 1) * 4, :]
            dst = a_xT.rearrange("p (dt t) -> p dt t", dt=DT)[:, j * 4:(j + 1) * 4, :]
            prog.op("gpsimd", ld(dst, src), waits=xw if j == 0 else [], incs=[(sXT, 16)])
        xt_ready = sXT.n

        for which, w_dram in (("k", wkT_d), ("q", wqT_d)):
            for ot in range(OT):
                par = ot % 2
                src = w_dram[:, ot * 128:(ot + 1) * 128].rearrange("(dt p) o -> p dt o", p=128)
                dst = a_wT[:, par * DT * 128:(par + 1) * DT * 128].rearrange("p (dt o) -> p dt o", dt=DT)
                prog.op("gpsimd", ld(dst, src), waits=wT_free[par], incs=[(sWT[par], 16)])
                wT_free[par] = []
                w_ready = sWT[par].n
                for nt in range(HF // 512):
                    bk = (nt + ot) % 2
                    pw = pe_wait_bank(bk) + ([(sWT[par], w_ready)] if nt == 0 else [])
                    if nt == 0 and ot == 0:
                        pw.append((sXT, xt_ready))
                    for dt in range(DT):
                        lhs = a_wT[:, par * DT * 128 + dt * 128: par * DT * 128 + (dt + 1) * 128]
                        rhs = a_xT[:, dt * HF + nt * 512: dt * HF + (nt + 1) * 512]
                        prog.op("tensor",
                                (lambda l, r, b, s0, s1: lambda t: t.matmul(
                                    pb[b][:, :], l, r, start=s0, stop=s1))(
                                    lhs, rhs, bk, dt == 0, dt == DT - 1),
                                waits=pw if dt == 0 else [],
                                incs=[(sPE, 1)] if dt == DT - 1 else [])
                    mmv = sPE.n
                    dst_kq = a_kqT[:, ot * HF + nt * 512: ot * HF + (nt + 1) * 512]
                    prog.op("vector",
                            (lambda d, b: lambda v: v.tensor_copy(d, pb[b][:, :]))(dst_kq, bk),
                            waits=[(sPE, mmv)], incs=[(sDV, 1)])
                    bank_free[bk] = [(sDV, sDV.n)]
                wT_free[par] = [(sPE, sPE.n)]

            for h in range(H):
                ot, pbase = h // 2, (h % 2) * 64
                wf = a_wfk if which == "k" else a_wfq
                stt = a_ekT_st if which == "k" else a_fqT_st
                stf = ekst_free if which == "k" else fqst_free
                for nt in range(HF // 512):
                    bk = 2 + (h + nt) % 2
                    pw = pe_wait_bank(bk)
                    if h == 0 and nt == 0:
                        pw += [(sWF, WF_ALL), (sDV, sDV.n)]
                    prog.op("tensor",
                            (lambda hh, oo, pp, nn, b, w: lambda t: t.matmul(
                                pb[b][0:64, 0:512],
                                w[pp:pp + 64, hh * FD:(hh + 1) * FD],
                                a_kqT[pp:pp + 64, oo * HF + nn * 512: oo * HF + (nn + 1) * 512],
                                start=True, stop=True))(h, ot, pbase, nt, bk, wf),
                            waits=pw, incs=[(sPE, 1)])
                    mmv = sPE.n
                    aw = [(sPE, mmv)]
                    if pbase == 0 and nt == 0:
                        aw += stf[ot % 2]
                        stf[ot % 2] = []
                    dstst = stt[pbase:pbase + 64, (ot % 2) * HF + nt * 512:(ot % 2) * HF + (nt + 1) * 512]
                    prog.op("scalar",
                            (lambda d, b: lambda s: s.activation(d, pb[b][0:64, 0:512], AF.Exp))(dstst, bk),
                            waits=aw, incs=[(sAC, 1)])
                    bank_free[bk] = [(sAC, sAC.n)]
                if which == "k":
                    for tt in range(HF // CH):
                        bk = 4 + (h + tt) % 2
                        pw = pe_wait_bank(bk)
                        prog.op("tensor",
                                (lambda hh, oo, pp, t_, b: lambda t: t.matmul(
                                    pb[b][:, 0:64],
                                    a_kqT[pp:pp + 64, oo * HF + t_ * CH: oo * HF + (t_ + 1) * CH],
                                    a_wfk[pp:pp + 64, hh * FD:(hh + 1) * FD],
                                    start=True, stop=True))(h, ot, pbase, tt, bk),
                                waits=pw, incs=[(sPE, 1)])
                        mmv = sPE.n
                        aw = [(sPE, mmv)]
                        if h == 0 and tt == 0 and hf == 1:
                            aw.append((sSP, prev_half["sp"]))
                        prog.op("scalar",
                                (lambda hh, t_, b: lambda s: s.activation(
                                    a_ek[:, t_ * H * FD + hh * FD: t_ * H * FD + (hh + 1) * FD],
                                    pb[b][:, 0:64], AF.Exp,
                                    accum_out=a_sk[:, t_ * H + hh: t_ * H + hh + 1]))(h, tt, bk),
                                waits=aw, incs=[(sAC, 1)])
                        bank_free[bk] = [(sAC, sAC.n)]
                if h % 2 == 1:
                    dd = ekT_d if which == "k" else fqT_d
                    acv = sAC.n
                    prog.op("scalar",
                            (lambda o, s_, d_: lambda s: s.dma_start(
                                out=d_[:, o * T + t0: o * T + t0 + HF],
                                in_=s_[:, (o % 2) * HF:(o % 2 + 1) * HF]))(ot, stt, dd),
                            waits=[(sAC, acv)], incs=[(sSP, 16)])
                    stf[ot % 2] = [(sSP, sSP.n)]

            if which == "k":
                kfeat_pe = sPE.n
                prog.op("vector",
                        lambda v: v.reciprocal(a_rsk[:, 0:8 * H], a_sk[:, 0:8 * H]),
                        waits=[(sAC, sAC.n)], incs=[(sDV, 1)])
                rsk_ready = sDV.n

                for oc in range(4):
                    par = oc % 2
                    wv_buf = a_wV0 if par == 0 else a_wV1
                    src = wvT_d[:, oc * 512:(oc + 1) * 512].rearrange("(dt p) o -> p dt o", p=128)
                    dst = wv_buf.rearrange("p (dt o) -> p dt o", dt=DT)
                    w = list(wV_free[par])
                    if par == 1:
                        w.append((sPE, kfeat_pe))
                    prog.op("gpsimd", ld(dst, src), waits=w, incs=[(sWV[par], 16)])
                    wV_free[par] = []
                    wv_ready = sWV[par].n
                    for tt in range(HF // CH):
                        bk = (oc + tt) % 2
                        pw = pe_wait_bank(bk) + ([(sWV[par], wv_ready)] if tt == 0 else [])
                        for dt in range(DT):
                            lhs = a_xT[:, dt * HF + tt * CH: dt * HF + (tt + 1) * CH]
                            rhs = wv_buf[:, dt * 512:(dt + 1) * 512]
                            prog.op("tensor",
                                    (lambda l, r, b, s0, s1: lambda t: t.matmul(
                                        pb[b][:, :],
                                        l, r, start=s0, stop=s1))(lhs, rhs, bk, dt == 0, dt == DT - 1),
                                    waits=pw if dt == 0 else [],
                                    incs=[(sPE, 1)] if dt == DT - 1 else [])
                        mmv = sPE.n
                        for hh in range(8):
                            h = oc * 8 + hh
                            aw = []
                            if hh == 0:
                                aw = [(sPE, mmv), (sDV, rsk_ready)]
                                if tt == 0 and oc == 0 and hf == 1:
                                    aw.append((sSP, prev_half["sp"]))
                            prog.op("scalar",
                                    (lambda h_, t_, b, hh_: lambda s: s.activation(
                                        a_vp[:, t_ * H * 65 + h_ * 65: t_ * H * 65 + h_ * 65 + 64],
                                        pb[b][:, hh_ * 64:(hh_ + 1) * 64], AF.Copy,
                                        scale=a_rsk[:, t_ * H + h_: t_ * H + h_ + 1]))(h, tt, bk, hh),
                                    waits=aw, incs=[(sAC, 1)] if hh == 7 else [])
                        bank_free[bk] = [(sAC, sAC.n)]
                    wV_free[par] = [(sPE, sPE.n)]
                for tt in range(HF // CH):
                    prog.op("vector",
                            (lambda t_: lambda v: v.tensor_copy(
                                a_vp[:, t_ * H * 65 + 64: t_ * H * 65 + 64 + (H - 1) * 65 + 1: 65],
                                a_rsk[:, t_ * H:(t_ + 1) * H]))(tt),
                            waits=[(sAC, sAC.n)] if tt == 0 else [], incs=[(sDV, 1)])
                vp_done_dv = sDV.n

                for tt in range(HF // CH):
                    for hg in range(8):
                        bk = 6 + (tt * 8 + hg) % 2
                        pw = pe_wait_bank(bk)
                        if tt == 0 and hg == 0:
                            pw += [(sDV, vp_done_dv)]
                        for j in range(4):
                            h = hg * 4 + j
                            prog.op("tensor",
                                    (lambda h_, t_, b, j_: lambda t: t.matmul(
                                        pb[b][0:64, j_ * 65:(j_ + 1) * 65],
                                        a_ek[:, t_ * H * FD + h_ * FD: t_ * H * FD + (h_ + 1) * FD],
                                        a_vp[:, t_ * H * 65 + h_ * 65: t_ * H * 65 + (h_ + 1) * 65],
                                        start=True, stop=True))(h, tt, bk, j),
                                    waits=pw if j == 0 else [],
                                    incs=[(sPE, 1)] if j == 3 else [])
                        mmv = sPE.n
                        aw = [(sPE, mmv)]
                        if tt == 0 and hg == 0 and hf == 0:
                            aw.append((sDV, 1))
                        prog.op("vector",
                                (lambda b, g_: lambda v: v.tensor_tensor(
                                    Tc_sb[:, g_ * 260:(g_ + 1) * 260],
                                    pb[b][0:64, 0:260],
                                    Tc_sb[:, g_ * 260:(g_ + 1) * 260], ALU.add))(bk, hg),
                                waits=aw, incs=[(sDV, 1)])
                        bank_free[bk] = [(sDV, sDV.n)]

                dvv = sDV.n
                for tt in range(HF // CH):
                    prog.op("scalar",
                            (lambda t_: lambda s: s.dma_start(
                                out=ek_d[t0 + t_ * CH: t0 + (t_ + 1) * CH, :],
                                in_=a_ek[:, t_ * H * FD:(t_ + 1) * H * FD]))(tt),
                            waits=[(sDV, dvv)] if tt == 0 else [], incs=[(sSP, 16)])
                    prog.op("scalar",
                            (lambda t_: lambda s: s.dma_start(
                                out=vp_d[t0 + t_ * CH: t0 + (t_ + 1) * CH, :],
                                in_=a_vp[:, t_ * H * 65:(t_ + 1) * H * 65]))(tt),
                            incs=[(sSP, 16)])

    half_A(0)
    prev_half = {"sp": sSP.n, "dv": sDV.n}
    half_A(1)
    a_done_dv = sDV.n
    a_done_sp = sSP.n
    a_done_ac = sAC.n
    a_done_pe = sPE.n

    if stage == "a":
        if debug:
            prog.op("sync", lambda s: s.dma_start(out=tc_dbg[:, :], in_=Tc_sb[:, :]),
                    waits=[(sDV, a_done_dv)], incs=[(sOU, 16)])
            for nm in ("ekT_f", "fqT_f", "ek_f", "vp_f"):
                o, i = dbg_f32[nm]
                prog.op("gpsimd", (lambda o_, i_: lambda g: g.dma_start(out=o_[:, :], in_=i_[:, :]))(o, i),
                        waits=[(sSP, a_done_sp)], incs=[(sOU, 16)])
        prog.op("sync", lambda s: s.nop(),
                waits=[(sOU, sOU.n), (sSP, a_done_sp), (sDV, a_done_dv), (sAC, a_done_ac)])
        _finalize(nc, prog, guards)
        return nc

    prog.op("sync", lambda s: s.dma_start(out=ag_in[:, :], in_=Tc_sb[:, :]),
            waits=[(sDV, a_done_dv)], incs=[(sXN, 16)])
    prog.op("gpsimd",
            lambda g: g.collective_compute(
                "AllGather", ALU.bypass, replica_groups=[list(range(NCORES))],
                ins=[ag_in[:, :].opt()], outs=[ag_out[:, :].opt()]),
            waits=[(sXN, sXN.n)], incs=[(sCC, 1)])
    prog.op("vector", lambda v: v.memset(b_acc[:, :], 0.0),
            waits=[(sWF, WF_ALL), (sSP, a_done_sp)], incs=[(sDV, 1)])
    rd_free = [[], []]
    for rk in range(NCORES):
        par = rk % 2
        src = ag_out[rk * 64:(rk + 1) * 64, :].rearrange("r (two c) -> (r two) c", two=2)
        w = [(sCC, 1), (sSP, a_done_sp)] if rk == 0 else []
        w += rd_free[par]
        rd_free[par] = []
        prog.op("sync",
                (lambda s_, d_: lambda s: s.dma_start(out=d_, in_=s_))(src, b_rk[par][:, :]),
                waits=w, incs=[(sAG, 16)])
        agv = sAG.n
        prog.op("vector",
                (lambda p_, r_: lambda v: v.scalar_tensor_tensor(
                    b_acc[:, :], b_rk[p_][:, :],
                    ccmask_sb[:, r_:r_ + 1],
                    b_acc[:, :], ALU.mult, ALU.add))(par, rk),
                waits=[(sAG, agv), (sDV, sDV.n)], incs=[(sDV, 1)])
        rd_free[par] = [(sDV, sDV.n)]
    prog.op("sync", lambda s: s.dma_start(
        out=ag_in[:, :].rearrange("r (two c) -> (r two) c", two=2), in_=b_acc[:, :]),
            waits=[(sDV, sDV.n)], incs=[(sXN, 16)])
    prog.op("sync", lambda s: s.dma_start(out=S_sb[:, :], in_=ag_in[:, :]),
            waits=[(sXN, sXN.n)], incs=[(sXN, 16)])
    sinit_xn = sXN.n
    prog.op("scalar", lambda s: s.activation(S_bf[0:64, :], S_sb[:, :], AF.Copy),
            waits=[(sXN, sinit_xn)], incs=[(sAC, 1)])
    prog.op("scalar", lambda s: s.activation(S_bf[64:128, :], S_sb[:, :], AF.Copy),
            incs=[(sAC, 1)])
    sbf_ready = sAC.n

    if debug:
        prog.op("sync", lambda s: s.dma_start(out=s0_dbg[:, :], in_=S_sb[:, :]),
                waits=[(sXN, sinit_xn)], incs=[(sOU, 16)])
        prog.op("sync", lambda s: s.dma_start(out=tc_dbg[:, :], in_=Tc_sb[:, :]),
                incs=[(sOU, 16)])

    for ot in range(OT):
        prog.op("gpsimd",
                (lambda o: lambda g: g.dma_start(
                    out=c_woT[:, o * Dm:(o + 1) * Dm],
                    in_=woT_d[o * 128:(o + 1) * 128, :]))(ot),
                waits=[(sPE, a_done_pe), (sAC, a_done_ac)] if ot == 0 else [],
                incs=[(sWO, 16)])
    wo_ready = sWO.n

    ch_free = [[], []]

    def load_chunk(ch):
        par = ch % 2
        c0 = ch * CH
        w = list(ch_free[par]) + ([(sSP, a_done_sp)] if ch < 2 else [])
        ch_free[par] = []
        if ch < 2:
            w.append((sDV, sDV.n))
        prog.op("sync",
                (lambda p, c: lambda s: s.dma_start(
                    out=c_ekT[p].rearrange("p (o t) -> p o t", o=OT),
                    in_=ekT_d.rearrange("p (o t) -> p o t", o=OT)[:, :, c:c + CH]))(par, c0),
                waits=w, incs=[(sCH[par], 16)])
        prog.op("sync",
                (lambda p, c: lambda s: s.dma_start(
                    out=c_fqT[p].rearrange("p (o t) -> p o t", o=OT),
                    in_=fqT_d.rearrange("p (o t) -> p o t", o=OT)[:, :, c:c + CH]))(par, c0),
                incs=[(sCH[par], 16)])
        prog.op("sync",
                (lambda p, c: lambda s: s.dma_start(
                    out=c_ek[p][:, :], in_=ek_d[c:c + CH, :]))(par, c0),
                incs=[(sCH[par], 16)])
        prog.op("sync",
                (lambda p, c: lambda s: s.dma_start(
                    out=c_vp[p][:, :], in_=vp_d[c:c + CH, :]))(par, c0),
                incs=[(sCH[par], 16)])
        prog.op("sync",
                (lambda p, c: lambda s: s.dma_start(
                    out=c_xn[p][:, :], in_=xn_d[c:c + CH, :]))(par, c0),
                incs=[(sCH[par], 16)])

    load_chunk(0)
    load_chunk(1)

    sbf_wait = sbf_ready
    last = {"rc_dv": [0, 0], "mult_dv": [0, 0]}
    prev_wo_pe = 0
    prev_xat_sp = 0
    prev_tr_pe = 0
    prev_sq_ac = 0

    for ch in range(NCH):
        par = ch % 2
        chv = sCH[par].n
        if debug and stage == "c" and ch == 8:
            prog.op("gpsimd", (lambda p: lambda g: g.dma_start(out=sbf_dbg[:, :], in_=S_bf[:, :]))(par),
                    waits=[(sAC, sbf_wait)], incs=[(sOU, 16)])
            prog.op("gpsimd", (lambda p: lambda g: g.dma_start(out=fq8_dbg[:, :], in_=c_fqT[p][:, :]))(par),
                    waits=[(sCH[par], chv)], incs=[(sOU, 16)])
            prog.op("gpsimd", (lambda p: lambda g: g.dma_start(out=vp8_dbg[:, :], in_=c_vp[p][:, :]))(par),
                    incs=[(sOU, 16)])
        for q in range(4):
            pq = q % 2
            pw = []
            for b in range(8):
                pw += pe_wait_bank(b)
            if q == 0:
                pw += [(sCH[par], chv)]
                if ch == 0:
                    pw += [(sWF, WF_ALL)]
            for j in range(8):
                h = q * 8 + j
                ot, pbase = h // 2, (h % 2) * 64
                prog.op("tensor",
                        (lambda p, o, pbs, b: lambda t: t.matmul(
                            pb[b][:, 0:CH],
                            c_ekT[p][pbs:pbs + 64, o * CH:(o + 1) * CH],
                            c_fqT[p][pbs:pbs + 64, o * CH:(o + 1) * CH],
                            start=True, stop=True))(par, ot, pbase, j),
                        waits=pw if j == 0 else [], incs=[(sPE, 1)] if j == 7 else [])
            at_pe = sPE.n
            for j in range(8):
                h = q * 8 + j
                prog.op("vector",
                        (lambda b, hh: lambda v: v.tensor_tensor(
                            c_am[:, hh * CH:(hh + 1) * CH],
                            pb[b][:, 0:CH], mask_bf[:, :], ALU.mult))(j, h),
                        waits=[(sPE, at_pe)] if j == 0 else [], incs=[(sDV, 1)] if j == 7 else [])
            am_dv = sDV.n
            pw = [(sDV, am_dv)]
            if q == 0:
                pw += [(sAC, sbf_wait)]
                if ch == 0:
                    pw += [(sXN, sinit_xn)]
            for j in range(8):
                h = q * 8 + j
                ot, pbase = h // 2, (h % 2) * 64
                prog.op("tensor",
                        (lambda p, hh, b: lambda t: t.matmul(
                            pb[b][0:65, 0:CH],
                            c_vp[p][:, hh * 65:(hh + 1) * 65],
                            c_am[:, hh * CH:(hh + 1) * CH], start=True, stop=False))(par, h, j),
                        waits=pw if j == 0 else [])
                prog.op("tensor",
                        (lambda p, hh, o, pbs, b: lambda t: t.matmul(
                            pb[b][0:65, 0:CH],
                            S_bf[pbs:pbs + 64, hh * 65:(hh + 1) * 65],
                            c_fqT[p][pbs:pbs + 64, o * CH:(o + 1) * CH],
                            start=False, stop=False))(par, h, ot, pbase, j))
                prog.op("tensor",
                        (lambda b: lambda t: t.matmul(
                            pb[b][0:65, 0:CH], eps65_bf[:, :], ones_row_bf[:, 0:CH],
                            start=False, stop=True))(j),
                        incs=[(sPE, 1)] if j == 7 else [])
            yn_pe = sPE.n
            aw = [(sPE, yn_pe), (sDV, last["rc_dv"][pq])]
            for j in range(8):
                prog.op("scalar",
                        (lambda b, jj, p_: lambda s: s.activation(
                            c_dn[:, p_ * 8 * CH + jj * CH: p_ * 8 * CH + (jj + 1) * CH],
                            pb[b][64:65, 0:CH], AF.Copy))(j, j, pq),
                        waits=aw if j == 0 else [], incs=[(sAC, 1)] if j == 7 else [])
            dn_ac = sAC.n
            for j in range(8):
                prog.op("vector",
                        (lambda b, jj: lambda v: v.tensor_copy(
                            c_yn[0:64, jj * CH:(jj + 1) * CH], pb[b][0:64, 0:CH]))(j, j),
                        waits=[(sPE, yn_pe), (sDV, sDV.n)] if j == 0 else [],
                        incs=[(sDV, 1)] if j == 7 else [])
            ynsb_dv = sDV.n
            prog.op("vector",
                    (lambda p_: lambda v: v.reciprocal(
                        c_rc[:, p_ * 8 * CH:(p_ + 1) * 8 * CH],
                        c_dn[:, p_ * 8 * CH:(p_ + 1) * 8 * CH]))(pq),
                    waits=[(sAC, dn_ac), (sDV, sDV.n)], incs=[(sDV, 1)])
            rc_dv = sDV.n
            last["rc_dv"][pq] = rc_dv
            pw = [(sDV, ynsb_dv), (sDV, rc_dv)]
            for j in range(8):
                prog.op("tensor",
                        (lambda b, jj, p_: lambda t: t.matmul(
                            pb[b][0:64, 0:CH], ones_row_f32_hi[:, :],
                            c_rc[:, p_ * 8 * CH + jj * CH: p_ * 8 * CH + (jj + 1) * CH],
                            start=True, stop=True))(j, j, pq),
                        waits=pw if j == 0 else [], incs=[(sPE, 1)] if j == 7 else [])
            bc_pe = sPE.n
            aw = [(sPE, bc_pe), (sDV, last["mult_dv"][pq])]
            for j in range(8):
                prog.op("scalar",
                        (lambda b, jj, p_: lambda s: s.activation(
                            c_rcb[:, p_ * 8 * CH + jj * CH: p_ * 8 * CH + (jj + 1) * CH],
                            pb[b][0:64, 0:CH], AF.Copy))(j, j, pq),
                        waits=aw if j == 0 else [], incs=[(sAC, 1)] if j == 7 else [])
            rcb_ac = sAC.n
            for b in range(8):
                bank_free[b] = [(sAC, rcb_ac)]
            aw = [(sAC, rcb_ac)]
            if q == 0:
                aw.append((sPE, prev_wo_pe))
            for j in range(8):
                h = q * 8 + j
                ot, pbase = h // 2, (h % 2) * 64
                prog.op("vector",
                        (lambda jj, p_, o, pbs: lambda v: v.tensor_tensor(
                            c_yT[pbs:pbs + 64, o * CH:(o + 1) * CH],
                            c_yn[0:64, jj * CH:(jj + 1) * CH],
                            c_rcb[:, p_ * 8 * CH + jj * CH: p_ * 8 * CH + (jj + 1) * CH],
                            ALU.mult))(j, pq, ot, pbase),
                        waits=aw if j == 0 else [], incs=[(sDV, 1)] if j == 7 else [])
            last["mult_dv"][pq] = sDV.n
        heads_dv = sDV.n
        if debug and stage == "c":
            prog.op("gpsimd",
                    (lambda c: lambda g: g.dma_start(
                        out=yT_dbg[:, c * OT * CH:(c + 1) * OT * CH], in_=c_yT[:, :]))(ch),
                    waits=[(sDV, heads_dv)], incs=[(sOU, 16)])

        for hg in range(8):
            bk = 6 + hg % 2
            pw = pe_wait_bank(bk)
            for j in range(4):
                h = hg * 4 + j
                prog.op("tensor",
                        (lambda p, h_, b, j_: lambda t: t.matmul(
                            pb[b][0:64, j_ * 65:(j_ + 1) * 65],
                            c_ek[p][:, h_ * FD:(h_ + 1) * FD],
                            c_vp[p][:, h_ * 65:(h_ + 1) * 65],
                            start=True, stop=True))(par, h, bk, j),
                        waits=pw if j == 0 else [],
                        incs=[(sPE, 1)] if j == 3 else [])
            mmv = sPE.n
            prog.op("vector",
                    (lambda b, g_: lambda v: v.tensor_tensor(
                        S_sb[:, g_ * 260:(g_ + 1) * 260],
                        pb[b][0:64, 0:260],
                        S_sb[:, g_ * 260:(g_ + 1) * 260], ALU.add))(bk, hg),
                    waits=[(sPE, mmv)], incs=[(sDV, 1)])
            bank_free[bk] = [(sDV, sDV.n)]
        states_pe = sPE.n
        if debug and stage == "c":
            prog.op("gpsimd",
                    (lambda c: lambda g: g.dma_start(
                        out=s_dbg[:, c * H * 65:(c + 1) * H * 65], in_=S_sb[:, :]))(ch),
                    waits=[(sDV, sDV.n)], incs=[(sOU, 16)])
        prog.op("scalar", lambda s: s.activation(S_bf[0:64, :], S_sb[:, :], AF.Copy),
                waits=[(sDV, sDV.n), (sPE, states_pe)], incs=[(sAC, 1)])
        prog.op("scalar", lambda s: s.activation(S_bf[64:128, :], S_sb[:, :], AF.Copy),
                incs=[(sAC, 1)])
        sbf_wait = sAC.n

        for ncx in range(4):
            bk = 4 + ncx % 2
            pw = pe_wait_bank(bk) + ([(sDV, heads_dv), (sWO, wo_ready)] if ncx == 0 else [])
            for ot in range(OT):
                prog.op("tensor",
                        (lambda o, n_, b: lambda t: t.matmul(
                            pb[b][:, :],
                            c_yT[:, o * CH:(o + 1) * CH],
                            c_woT[:, o * Dm + n_ * 512: o * Dm + (n_ + 1) * 512],
                            start=(o == 0), stop=(o == OT - 1)))(ot, ncx, bk),
                        waits=pw if ot == 0 else [],
                        incs=[(sPE, 1)] if ot == OT - 1 else [])
            mmv = sPE.n
            aw = [(sPE, mmv), (sCH[par], chv)]
            if ncx == 0:
                aw.append((sAC, prev_sq_ac))
            prog.op("vector",
                    (lambda p, n_, b: lambda v: v.tensor_tensor(
                        c_r1[:, n_ * 512:(n_ + 1) * 512], pb[b][:, :],
                        c_xn[p][:, n_ * 512:(n_ + 1) * 512], ALU.add))(par, ncx, bk),
                    waits=aw, incs=[(sDV, 1)])
            bank_free[bk] = [(sDV, sDV.n)]
        wo_pe = sPE.n
        prev_wo_pe = wo_pe
        r1_dv = sDV.n
        if debug and stage == "c" and ch == 0:
            prog.op("gpsimd", lambda g: g.dma_start(out=r1_dbg[:, :], in_=c_r1[:, :]),
                    waits=[(sDV, r1_dv)], incs=[(sOU, 16)])

        prog.op("vector", lambda v: v.reduce_sum(stats[:, 0:1], c_r1[:, :], axis=AX.X),
                incs=[(sDV, 1)])
        prog.op("vector", lambda v: v.tensor_scalar_mul(stats[:, 0:1], stats[:, 0:1], 1.0 / Dm),
                waits=[(sDV, sDV.n)], incs=[(sDV, 1)])
        prog.op("scalar", lambda s: s.activation(c_sq[:, :], c_r1[:, :], AF.Square,
                                                 accum_out=stats[:, 12:13]),
                waits=[(sDV, r1_dv)], incs=[(sAC, 1)])
        sq_ac = sAC.n
        prev_sq_ac = sq_ac
        prog.op("vector", lambda v: v.tensor_tensor(stats[:, 4:5], stats[:, 0:1], stats[:, 0:1], ALU.mult),
                waits=[(sDV, sDV.n)], incs=[(sDV, 1)])
        prog.op("vector", lambda v: v.tensor_scalar_add(stats[:, 4:5], stats[:, 4:5], -EPS_LN),
                waits=[(sDV, sDV.n)], incs=[(sDV, 1)])
        prog.op("vector", lambda v: v.scalar_tensor_tensor(
            stats[:, 8:9], stats[:, 12:13], 1.0 / Dm, stats[:, 4:5], ALU.mult, ALU.subtract),
                waits=[(sAC, sq_ac), (sDV, sDV.n)], incs=[(sDV, 1)])
        var_dv = sDV.n
        prog.op("scalar", lambda s: s.activation(stats[:, 16:17], stats[:, 8:9], AF.Ln),
                waits=[(sDV, var_dv)], incs=[(sAC, 1)])
        prog.op("scalar", lambda s: s.activation(stats[:, 20:21], stats[:, 16:17], AF.Exp, scale=misc_sb[:, 0:1]),
                waits=[(sAC, sAC.n)], incs=[(sAC, 1)])
        rstd_ac = sAC.n
        if debug and stage == "c" and ch == 0:
            prog.op("gpsimd", lambda g: g.dma_start(out=st_dbg[:, :], in_=stats[:, :]),
                    waits=[(sAC, rstd_ac)], incs=[(sOU, 16)])
        aw = [(sAC, rstd_ac), (sPE, prev_tr_pe)]
        prog.op("vector", lambda v: v.tensor_scalar(
            c_xa[:, :], c_r1[:, :], stats[:, 0:1], stats[:, 20:21], ALU.subtract, ALU.mult),
                waits=aw, incs=[(sDV, 1)])
        xa_dv = sDV.n

        for dt in range(DT):
            bk = 6 + dt % 2
            pw = pe_wait_bank(bk) + ([(sDV, xa_dv)] if dt < 2 else [])
            prog.op("tensor",
                    (lambda d, b: lambda t: t.transpose(
                        pb[b][:, 0:64].bitcast(BF16), c_xa[:, d * 128:(d + 1) * 128],
                        ident_bf[:, :]))(dt, bk),
                    waits=pw, incs=[(sPE, 1)])
            mmv = sPE.n
            aw = [(sPE, mmv)]
            if dt == 0:
                aw.append((sSP, prev_xat_sp))
            prog.op("vector",
                    (lambda d, b: lambda v: v.tensor_copy(
                        c_xaT[:, d * CH:(d + 1) * CH], pb[b][:, 0:64].bitcast(BF16)))(dt, bk),
                    waits=aw, incs=[(sDV, 1)])
            bank_free[bk] = [(sDV, sDV.n)]
        prev_tr_pe = sPE.n
        xat_dv = sDV.n
        prog.op("scalar",
                (lambda c: lambda s: s.dma_start(
                    out=xaT_d.rearrange("p (d t) -> p d t", d=DT)[:, :, c * CH:(c + 1) * CH],
                    in_=c_xaT.rearrange("p (d t) -> p d t", d=DT)))(ch),
                waits=[(sDV, xat_dv)], incs=[(sSP, 16)])
        prev_xat_sp = sSP.n

        if serial:
            barrier()
        ch_free[par] = [(sPE, states_pe), (sDV, r1_dv)]
        if ch + 2 < NCH:
            load_chunk(ch + 2)

    c_done_sp = sSP.n
    c_done_pe = sPE.n
    c_done_dv = sDV.n
    c_done_ac = sAC.n

    if stage == "c":
        if debug:
            for nm in ("xaT_f", "fqT_f", "ekT_f"):
                o, i = dbg_f32[nm]
                prog.op("gpsimd", (lambda o_, i_: lambda g: g.dma_start(out=o_[:, :], in_=i_[:, :]))(o, i),
                        waits=[(sSP, c_done_sp)], incs=[(sOU, 16)])
        prog.op("sync", lambda s: s.nop(),
                waits=[(sSP, c_done_sp), (sDV, sDV.n), (sOU, sOU.n)])
        _finalize(nc, prog, guards)
        return nc

    wg_free = [[], []]
    wd_free = [[], []]
    xa_free = []
    gateup_pe_done = 0
    last_mult_dv = [0, 0]

    for tc in range(NTC):
        tb = tc * TCF
        w = list(xa_free) + ([(sSP, c_done_sp), (sPE, c_done_pe), (sDV, c_done_dv),
                              (sAC, c_done_ac)] if tc == 0 else [])
        xa_free = []
        prog.op("sync",
                (lambda t_: lambda s: s.dma_start(
                    out=d_xaT.rearrange("p (d t) -> p d t", d=DT),
                    in_=xaT_d.rearrange("p (d t) -> p d t", d=DT)[:, :, t_ * TCF:(t_ + 1) * TCF]))(tc),
                waits=w, incs=[(sXA, 16)])
        xa_ready = sXA.n

        pw = pe_wait_bank(4) + [(sXA, xa_ready)]
        for dt in range(DT):
            prog.op("tensor",
                    (lambda d: lambda t: t.matmul(
                        pb[4][0:1, 0:TCF], ones_col_bf[:, :],
                        d_xaT[:, d * TCF:(d + 1) * TCF],
                        start=(d == 0), stop=(d == DT - 1)))(dt),
                    waits=pw if dt == 0 else [],
                    incs=[(sPE, 1)] if dt == DT - 1 else [])
        mmv = sPE.n
        prog.op("scalar", lambda s: s.activation(d_xsum[:, :], pb[4][0:1, 0:TCF], AF.Copy),
                waits=[(sPE, mmv)], incs=[(sAC, 1)])
        xsum_ac = sAC.n
        bank_free[4] = [(sAC, xsum_ac)]
        if debug and stage == "full" and tc == 0:
            prog.op("gpsimd", lambda g: g.dma_start(out=xs_dbg[:, :], in_=d_xsum[:, :]),
                    waits=[(sAC, xsum_ac)], incs=[(sOU, 16)])

        for ic in range(16):
            par = ic % 2
            for wdram, buf in ((wg8_d, d_wg[par]), (wu8_d, d_wu[par])):
                w2 = list(wg_free[par]) if wdram is wg8_d else []
                if tc == 0 and ic < 2 and wdram is wg8_d:
                    w2 += [(sPE, c_done_pe), (sDV, c_done_dv), (sAC, c_done_ac),
                           (sSP, c_done_sp)]
                prog.op("gpsimd",
                        (lambda wd_, b_, i_: lambda g: g.dma_start(
                            out=b_.rearrange("p (ht i) -> p ht i", ht=16),
                            in_=wd_[:, i_ * TCF:(i_ + 1) * TCF].rearrange(
                                "(ht p) i -> p ht i", p=128)))(wdram, buf, ic),
                        waits=w2,
                        incs=[(sWG[par], 16)])
            wg_free[par] = []
            wg_ready = sWG[par].n
            if debug and stage == "full" and tc == 0 and ic == 0:
                prog.op("gpsimd", lambda g: g.dma_start(out=wub_dbg[:, :], in_=d_wu[0][:, :]),
                        waits=[(sWG[0], wg_ready)], incs=[(sOU, 16)])
            for isub in range(4):
                it = ic * 4 + isub
                sp_ = isub % 2
                for bk, buf, scl, act, dst in (
                        (isub % 2, d_wg[par], d_scg, AF.Silu, d_sg[sp_]),
                        (2 + isub % 2, d_wu[par], d_scu, AF.Copy, d_uu[sp_])):
                    pw = pe_wait_bank(bk)
                    if isub == 0 and bk < 2:
                        pw += [(sWG[par], wg_ready), (sAC, xsum_ac)]
                    elif isub == 0:
                        pw += [(sWG[par], wg_ready)]
                    for ht in range(16):
                        prog.op("tensor",
                                (lambda b, bf_, h_, s_: lambda t: t.matmul(
                                    pb[b][:, 0:TCF],
                                    bf_[:, h_ * TCF + s_ * 128: h_ * TCF + s_ * 128 + 128],
                                    d_xaT[:, h_ * TCF:(h_ + 1) * TCF],
                                    start=(h_ == 0), stop=False))(bk, buf, ht, isub),
                                waits=pw if ht == 0 else [])
                    prog.op("tensor",
                            (lambda b: lambda t: t.matmul(
                                pb[b][:, 0:TCF], m128_row_bf[:, :], d_xsum[:, :],
                                start=False, stop=True))(bk),
                            incs=[(sPE, 1)])
                    mmv = sPE.n
                    rawdump = (debug and stage == "full" and tc == 0 and ic == 0
                               and isub == 0 and bk >= 2)
                    if rawdump:
                        prog.op("scalar",
                                (lambda b, a_, ds_: lambda s: s.activation(
                                    ds_[:, :], pb[b][:, 0:TCF], a_))(bk, act, dst),
                                waits=[(sPE, mmv), (sDV, last_mult_dv[sp_])], incs=[(sAC, 1)])
                    else:
                        prog.op("scalar",
                                (lambda b, sc_, a_, ds_, i_: lambda s: s.activation(
                                    ds_[:, :], pb[b][:, 0:TCF], a_,
                                    scale=sc_[:, i_: i_ + 1]))(bk, scl, act, dst, it),
                                waits=[(sPE, mmv), (sDV, last_mult_dv[sp_])], incs=[(sAC, 1)])
                    bank_free[bk] = [(sAC, sAC.n)]
                if debug and stage == "full" and tc == 0 and it == 0:
                    prog.op("gpsimd", (lambda s_: lambda g: g.dma_start(out=sg_dbg[:, :], in_=d_sg[s_][:, :]))(sp_),
                            waits=[(sAC, sAC.n)], incs=[(sOU, 16)])
                    prog.op("gpsimd", (lambda s_: lambda g: g.dma_start(out=uu_dbg[:, :], in_=d_uu[s_][:, :]))(sp_),
                            incs=[(sOU, 16)])
                    prog.op("gpsimd", lambda g: g.dma_start(out=xat_dbg[:, :], in_=d_xaT[:, :]),
                            incs=[(sOU, 16)])
                prog.op("vector",
                        (lambda i_, s_: lambda v: v.tensor_tensor(
                            d_h[:, i_ * TCF:(i_ + 1) * TCF], d_sg[s_][:, :], d_uu[s_][:, :],
                            ALU.mult))(it, sp_),
                        waits=[(sAC, sAC.n), (sOU, sOU.n) if (debug and stage == "full" and tc == 0 and it == 0) else (sAC, 0)], incs=[(sDV, 1)])
                last_mult_dv[sp_] = sDV.n
            wg_free[par] = [(sPE, sPE.n), (sAC, sAC.n)]
        h_dv = sDV.n
        gateup_pe_done = sPE.n
        if debug and stage == "full" and tc == 0:
            prog.op("gpsimd", lambda g: g.dma_start(out=h_dbg[:, :], in_=d_h[:, 0:8 * TCF]),
                    waits=[(sDV, h_dv)], incs=[(sOU, 16)])

        pw = pe_wait_bank(4) + [(sDV, h_dv)]
        for it in range(64):
            prog.op("tensor",
                    (lambda i_: lambda t: t.matmul(
                        pb[4][0:1, 0:TCF], ones_col_bf[:, :],
                        d_h[:, i_ * TCF:(i_ + 1) * TCF],
                        start=(i_ == 0), stop=(i_ == 63)))(it),
                    waits=pw if it == 0 else [],
                    incs=[(sPE, 1)] if it == 63 else [])
        mmv = sPE.n
        prog.op("scalar", lambda s: s.activation(d_hsum[:, :], pb[4][0:1, 0:TCF], AF.Copy),
                waits=[(sPE, mmv)], incs=[(sAC, 1)])
        hsum_ac = sAC.n
        bank_free[4] = [(sAC, hsum_ac)]

        for dh in range(2):
            pw_all = []
            for dto in range(8):
                pw_all += pe_wait_bank(dto)
            first_mm = [True] * 8
            for ic in range(16):
                par = ic % 2
                w = list(wd_free[par])
                if dh == 0 and ic < 2:
                    w += [(sPE, gateup_pe_done)]
                prog.op("gpsimd",
                        (lambda i_, d_, p_: lambda g: g.dma_start(
                            out=d_wd[p_].rearrange("p (itl d) -> p itl d", itl=4),
                            in_=wd8_d[i_ * TCF:(i_ + 1) * TCF,
                                      d_ * 1024:(d_ + 1) * 1024].rearrange(
                                "(itl p) d -> p itl d", p=128)))(ic, dh, par),
                        waits=w, incs=[(sWD[par], 16)])
                wd_free[par] = []
                wd_ready = sWD[par].n
                for itl in range(4):
                    it = ic * 4 + itl
                    for dto in range(8):
                        pw = []
                        if itl == 0 and dto == 0:
                            pw = [(sWD[par], wd_ready), (sDV, h_dv)] + (pw_all if ic == 0 else [])
                        prog.op("tensor",
                                (lambda p_, il, d_o, i_, st: lambda t: t.matmul(
                                    pb[d_o][:, 0:TCF],
                                    d_wd[p_][:, il * 1024 + d_o * 128: il * 1024 + (d_o + 1) * 128],
                                    d_h[:, i_ * TCF:(i_ + 1) * TCF],
                                    start=st, stop=False))(par, itl, dto, it, first_mm[dto]),
                                waits=pw,
                                incs=[(sPE, 1)] if (itl == 3 and dto == 7) else [])
                        first_mm[dto] = False
                wd_free[par] = [(sPE, sPE.n)]
            for dto in range(8):
                prog.op("tensor",
                        (lambda d_o: lambda t: t.matmul(
                            pb[d_o][:, 0:TCF], m128_row_bf[:, :], d_hsum[:, :],
                            start=False, stop=True))(dto),
                        waits=[(sAC, hsum_ac)] if dto == 0 else [],
                        incs=[(sPE, 1)] if dto == 7 else [])
            mmv = sPE.n
            for dto in range(8):
                dt = dh * 8 + dto
                prog.op("vector",
                        (lambda d_o, d_: lambda v: v.scalar_tensor_tensor(
                            d_r2T[:, d_ * TCF:(d_ + 1) * TCF], pb[d_o][:, 0:TCF],
                            d_scd[:, d_: d_ + 1], d_xaT[:, d_ * TCF:(d_ + 1) * TCF],
                            ALU.mult, ALU.add))(dto, dt),
                        waits=[(sPE, mmv)] if dto == 0 else [], incs=[(sDV, 1)])
                bank_free[dto] = [(sDV, sDV.n)]
        r2_dv = sDV.n
        if debug and stage == "full" and tc == 0:
            prog.op("gpsimd", lambda g: g.dma_start(out=r2_dbg[:, :], in_=d_r2T[:, :]),
                    waits=[(sDV, r2_dv)], incs=[(sOU, 16)])

        for ttl in range(TCF // 128):
            for rnd in range(2):
                pw = []
                for b in range(8):
                    pw += pe_wait_bank(b)
                if ttl == 0 and rnd == 0:
                    pw += [(sDV, r2_dv)]
                for j in range(8):
                    dt = rnd * 8 + j
                    prog.op("tensor",
                            (lambda d_, tl, b: lambda t: t.transpose(
                                pb[b][:, 0:128], d_r2T[:, d_ * TCF + tl * 128: d_ * TCF + (tl + 1) * 128],
                                ident_f32[:, :]))(dt, ttl, j),
                            waits=pw if j == 0 else [], incs=[(sPE, 1)] if j == 7 else [])
                mmv = sPE.n
                for j in range(8):
                    dt = rnd * 8 + j
                    aw = [(sPE, mmv)] if j == 0 else []
                    if j == 0 and rnd == 0:
                        aw.append((sAC, prev_sq_ac))
                    prog.op("vector",
                            (lambda d_, b: lambda v: v.tensor_copy(
                                d_r2t[:, d_ * 128:(d_ + 1) * 128], pb[b][:, 0:128]))(dt, j),
                            waits=aw, incs=[(sDV, 1)] if j == 7 else [])
                for b in range(8):
                    bank_free[b] = [(sDV, sDV.n)]
            tile_dv = sDV.n
            prog.op("vector", lambda v: v.reduce_sum(stats[:, 24:25], d_r2t[:, :], axis=AX.X),
                    incs=[(sDV, 1)])
            prog.op("vector", lambda v: v.tensor_scalar_mul(stats[:, 24:25], stats[:, 24:25], 1.0 / Dm),
                    waits=[(sDV, sDV.n)], incs=[(sDV, 1)])
            prog.op("scalar", lambda s: s.activation(d_sq[:, :], d_r2t[:, :], AF.Square,
                                                     accum_out=stats[:, 36:37]),
                    waits=[(sDV, tile_dv), (sOU, sOU.n)], incs=[(sAC, 1)])
            sq_ac = sAC.n
            prev_sq_ac = sq_ac
            prog.op("vector", lambda v: v.tensor_tensor(stats[:, 28:29], stats[:, 24:25], stats[:, 24:25], ALU.mult),
                    waits=[(sDV, sDV.n)], incs=[(sDV, 1)])
            prog.op("vector", lambda v: v.tensor_scalar_add(stats[:, 28:29], stats[:, 28:29], -EPS_LN),
                    waits=[(sDV, sDV.n)], incs=[(sDV, 1)])
            prog.op("vector", lambda v: v.scalar_tensor_tensor(
                stats[:, 32:33], stats[:, 36:37], 1.0 / Dm, stats[:, 28:29], ALU.mult, ALU.subtract),
                    waits=[(sAC, sq_ac), (sDV, sDV.n)], incs=[(sDV, 1)])
            var_dv = sDV.n
            prog.op("scalar", lambda s: s.activation(stats[:, 40:41], stats[:, 32:33], AF.Ln),
                    waits=[(sDV, var_dv)], incs=[(sAC, 1)])
            prog.op("scalar", lambda s: s.activation(stats[:, 44:45], stats[:, 40:41], AF.Exp, scale=misc_sb[:, 0:1]),
                    waits=[(sAC, sAC.n)], incs=[(sAC, 1)])
            rstd_ac = sAC.n
            prog.op("vector", lambda v: v.tensor_scalar(
                d_sq[:, :], d_r2t[:, :], stats[:, 24:25], stats[:, 44:45], ALU.subtract, ALU.mult),
                    waits=[(sAC, rstd_ac)], incs=[(sDV, 1)])
            out_dv = sDV.n
            trow = tb + ttl * 128
            prog.op("sync",
                    (lambda r: lambda s: s.dma_start(
                        out=out_d[r:r + 128, :], in_=d_sq[:, :]))(trow),
                    waits=[(sDV, out_dv)], incs=[(sOU, 16)])
        xa_free = [(sPE, sPE.n), (sDV, sDV.n), (sOU, sOU.n)]
        wg_free[0] = [(sPE, sPE.n), (sDV, sDV.n)]
        wg_free[1] = [(sPE, sPE.n), (sDV, sDV.n)]

    prog.op("sync", lambda s: s.nop(), waits=[(sOU, sOU.n), (sSP, sSP.n)])

    _finalize(nc, prog, guards)
    return nc


def _finalize(nc, prog, guards):
    blk = nc.Block()
    block = blk.__enter__()
    try:
        block.sync(lambda e: prog.replay("sync", e))
        block.tensor(lambda e: prog.replay("tensor", e))
        block.vector(lambda e: prog.replay("vector", e))
        block.scalar(lambda e: prog.replay("scalar", e))
        block.gpsimd(lambda e: prog.replay("gpsimd", e))
    finally:
        blk.__exit__(None, None, None)
        for g in reversed(guards):
            g.__exit__(None, None, None)


_NC_CACHE = {}


def _get_nc():
    key = (STAGE, DEBUG, SERIAL)
    if key not in _NC_CACHE:
        _NC_CACHE[key] = build_nc(*key)
    return _NC_CACHE[key]


def _u8_view(a_int32):
    a = np.ascontiguousarray(a_int32)
    return np.ascontiguousarray(a.view(np.uint8).reshape(a.shape + (4,))[..., 0])


def make_in_maps(inputs):
    x = np.asarray(inputs["L0_attn_hidden_states"], np.float32)
    wqT = np.ascontiguousarray(np.asarray(inputs["L0_attn_Wq"], np.float32).T)
    wkT = np.ascontiguousarray(np.asarray(inputs["L0_attn_Wk"], np.float32).T)
    wvT = np.ascontiguousarray(np.asarray(inputs["L0_attn_Wv"], np.float32).T)
    woT = np.ascontiguousarray(np.asarray(inputs["L0_attn_Wo"], np.float32).T)
    wfq = np.ascontiguousarray(np.asarray(inputs["L0_attn_Wfq"], np.float32).reshape(H * HD, FD))
    wfk = np.ascontiguousarray(np.asarray(inputs["L0_attn_Wfk"], np.float32).reshape(H * HD, FD))
    wg8 = _u8_view(np.asarray(inputs["L0_ffn_gate_w"]).T)
    wu8 = _u8_view(np.asarray(inputs["L0_ffn_up_w"]).T)
    wd8 = _u8_view(np.asarray(inputs["L0_ffn_down_w"]).T)
    scg = np.ascontiguousarray(np.asarray(inputs["L0_ffn_gate_scale"], np.float32).reshape(-1, 128).T)
    scu = np.ascontiguousarray(np.asarray(inputs["L0_ffn_up_scale"], np.float32).reshape(-1, 128).T)
    scd = np.ascontiguousarray(np.asarray(inputs["L0_ffn_down_scale"], np.float32).reshape(-1, 128).T)
    ident = np.eye(128, dtype=np.float32)
    maskA = np.triu(np.ones((128, 128), np.float32))
    onesrow = np.ones((1, 128), np.float32)
    m128row = np.full((1, 128), -128.0, np.float32)
    onescol = np.ones((128, 1), np.float32)
    misc = np.zeros((128, 2), np.float32)
    misc[:, 0] = -0.5
    eps65 = np.zeros((1, 65), np.float32)
    eps65[0, 64] = EPS_ATTN

    in_maps = []
    for c in range(NCORES):
        b, q = c // 4, c % 4
        t0 = q * T
        xs = x[b, t0:t0 + T, :]
        ccm = np.zeros((128, 8), np.float32)
        for j in range(NCORES):
            if j // 4 == b and (j % 4) < q:
                ccm[:, j] = 1.0
        in_maps.append({
            "xT": np.ascontiguousarray(xs.T),
            "xn": np.ascontiguousarray(xs),
            "wqT": wqT, "wkT": wkT, "wvT": wvT, "woT": woT,
            "wfq": wfq, "wfk": wfk,
            "wg8": wg8, "wu8": wu8, "wd8": wd8,
            "scg": scg, "scu": scu, "scd": scd,
            "identf": ident, "maskA": maskA,
            "onesrow": onesrow, "m128row": m128row, "onescol": onescol,
            "ccmask": ccm, "misc": misc, "eps65": eps65,
        })
    return in_maps


def kernel(**inputs):
    nc = _get_nc()
    in_maps = make_in_maps(inputs)
    res = run_bass_kernel_spmd(nc, in_maps, core_ids=list(range(NCORES)))
    out = np.empty((Bb, L, Dm), np.float32)
    for c in range(NCORES):
        b, q = c // 4, c % 4
        out[b, q * T:(q + 1) * T, :] = res.results[c]["out"]
    return out
